# version 18
# baseline (speedup 1.0000x reference)
from contextlib import ExitStack

import numpy as np
import ml_dtypes
import concourse.bass as bass
import concourse.mybir as mybir
from concourse.bass_utils import run_bass_kernel_spmd

F32 = mybir.dt.float32
BF16 = mybir.dt.bfloat16
FP8 = mybir.dt.float8e4
U8 = mybir.dt.uint8
AF = mybir.ActivationFunctionType
ALU = mybir.AluOpType
AX = mybir.AxisListType

P = 128
NF = 16
NH = 16
NJ = 16
NB = 4
RB = 512
EPS = 1e-5
L2EPS = 1e-12
INV2048 = 1.0 / 2048.0
INV448 = 1.0 / 448.0

_TAPS = []


class Led:
    def __init__(self):
        self.n = {}

    def inc(self, s, k=1):
        self.n[s] = self.n.get(s, 0) + k
        return self.n[s]

    def val(self, s):
        return self.n.get(s, 0)


def build_program(taps=()):
    nc = bass.Bass("TRN2", target_bir_lowering=False, detect_race_conditions=False)

    xblk = nc.dram_tensor("xblk", [NB, P, NF, RB], F32, kind="ExternalInput")
    Wq_t = nc.dram_tensor("Wq_t", [4, NF, P, 512], F32, kind="ExternalInput")
    Wkv_t = nc.dram_tensor("Wkv_t", [NF, P, 256], F32, kind="ExternalInput")
    Wo_t = nc.dram_tensor("Wo_t", [4, NF, P, 512], F32, kind="ExternalInput")
    WtT = nc.dram_tensor("WtT", [NF, P, 2048], BF16, kind="ExternalInput")
    gN2 = nc.dram_tensor("gN2", [P, NF], F32, kind="ExternalInput")
    gO2 = nc.dram_tensor("gO2", [P, NF], F32, kind="ExternalInput")
    iden = nc.dram_tensor("iden", [P, P], F32, kind="ExternalInput")

    outT = nc.dram_tensor("outT", [NF, P, 1024], F32, kind="ExternalOutput")

    w8d = nc.dram_tensor("w8d", [4, NF // 2, P, 1024], FP8)
    ccin = nc.dram_tensor("ccin", [1, 1], F32)
    ccout = nc.dram_tensor("ccout", [1, 1], F32)

    tap_dram = {}
    for t in taps:
        shp, dt = {
            "kT": ([P, 2048], F32), "v": ([P, NJ, P], F32),
            "qn": ([P, NH, 2 * RB], F32), "att": ([P, NF, 2 * RB], F32),
            "C": ([P, NF, 1024], F32), "xq": ([P, NF, 2 * RB], BF16),
            "stat1": ([P, 8, RB], F32), "w8chk": ([4, NF, P, 512], BF16),
            "scal": ([P, 8], F32), "tiny": ([1, 8], F32),
            "Cmid": ([P, NF, 1024], F32),
        }[t]
        tap_dram[t] = nc.dram_tensor("tap_" + t, shp, dt, kind="ExternalOutput")

    led = Led()

    with ExitStack() as stx:
        e = stx.enter_context

        LDKEYS = ("xb0", "xb1", "xg0", "xg1", "w0", "w1",
                  "kv0", "kv1", "kv2", "kv3",
                  "wt0", "wt1", "w80", "w81", "w82", "w83", "cst", "cc2", "cc3")
        LD = {k: e(nc.semaphore("ld_" + k)) for k in LDKEYS}
        so = e(nc.semaphore("so"))
        pe = e(nc.semaphore("pe"))
        ve = e(nc.semaphore("ve"))
        ac = e(nc.semaphore("ac"))
        cc = e(nc.semaphore("cc"))
        SEMS = {"so": so, "pe": pe, "ve": ve, "ac": ac}
        for k in LDKEYS:
            SEMS["ld_" + k] = LD[k]

        sI = e(nc.sbuf_tensor("sI", [P, P], F32))
        sOne = e(nc.sbuf_tensor("sOne", [P, P], F32))
        sGN = e(nc.sbuf_tensor("sGN", [P, NF], F32))
        sGO = e(nc.sbuf_tensor("sGO", [P, NF], F32))
        SCAL = e(nc.sbuf_tensor("SCAL", [P, 8], F32))
        tiny = e(nc.sbuf_tensor("tiny", [1, 8], F32))
        amc = e(nc.sbuf_tensor("amc", [P, 2], F32))
        amcols = e(nc.sbuf_tensor("amcols", [P, 100], F32))
        A = e(nc.sbuf_tensor("A", [P, NH, RB], F32))
        B = e(nc.sbuf_tensor("B", [P, NF, RB], F32))
        C = e(nc.sbuf_tensor("C", [P, NF, 1024], F32))
        S = e(nc.sbuf_tensor("S", [P, 5, RB], F32))
        SQ = e(nc.sbuf_tensor("SQ", [P, 2, RB], F32))
        Smu_o = e(nc.sbuf_tensor("Smu_o", [P, 2, RB], F32))
        Sr12_o = e(nc.sbuf_tensor("Sr12_o", [P, 2, RB], F32))
        Smu3 = e(nc.sbuf_tensor("Smu3", [P, 2, RB], F32))
        Srho3 = e(nc.sbuf_tensor("Srho3", [P, 2, RB], F32))
        xbig = e(nc.sbuf_tensor("xbig", [P, 4, RB], F32))
        wst = e(nc.sbuf_tensor("wst", [P, 2, 512], F32))
        wkvst = e(nc.sbuf_tensor("wkvst", [P, 4, 256], F32))
        wtst = e(nc.sbuf_tensor("wtst", [P, 2, 512], BF16))
        w8st = e(nc.sbuf_tensor("w8st", [P, 3, 1024], FP8))
        qb1 = e(nc.sbuf_tensor("qb1", [P, 512], BF16))
        w8c = e(nc.sbuf_tensor("w8c", [P, 2, 512], FP8))

        att_stack = ExitStack()
        kT = att_stack.enter_context(nc.sbuf_tensor("kT", [P, 2048], F32))
        vv = att_stack.enter_context(nc.sbuf_tensor("vv", [P, NJ, P], F32))
        ptil = att_stack.enter_context(nc.sbuf_tensor("ptil", [P, 2, RB], F32))

        ps = [e(nc.psum_tensor(f"ps{i}", [P, 512], F32)) for i in range(8)]

        block = e(nc.Block())

        def on(engname, fn):
            getattr(block, engname)(fn)

        def dma_in(q, stream, out_ap, in_ap):
            q.dma_start(out=out_ap, in_=in_ap).then_inc(LD[stream], 16)
            return led.inc("ld_" + stream, 16)

        def dma_out(q, out_ap, in_ap):
            q.dma_start(out=out_ap, in_=in_ap).then_inc(so, 16)
            return led.inc("so", 16)

        def emit_barrier():
            bv = {s: led.val(s) for s in SEMS}

            def mk(eng):
                for nm, val in bv.items():
                    if val:
                        eng.wait_ge(SEMS[nm], val)
            for en in ("sync", "tensor", "vector", "scalar"):
                on(en, mk)

        def emit_rsqrt(vio, rout, epsb, t1):
            def dv0(v):
                v.tensor_single_scalar(vio, vio, float(epsb), ALU.add).then_inc(ve, 1)
            on("vector", dv0)
            v0 = led.inc("ve")

            def sc0(s):
                b0 = led.val("ac")
                s.wait_ge(ve, v0)
                s.activation(t1, vio, AF.Ln).then_inc(ac, 1)
                s.wait_ge(ac, b0 + 1)
                s.activation(rout, t1, AF.Exp, scale=-0.5).then_inc(ac, 1)
            on("scalar", sc0)
            led.inc("ac")
            a0 = led.inc("ac")

            def dv1(v):
                v.wait_ge(ac, a0)
                v.tensor_tensor(t1, rout, rout, ALU.mult)
                v.scalar_tensor_tensor(t1, t1, -0.5, vio, ALU.mult, ALU.mult)
                v.tensor_single_scalar(t1, t1, 1.5, ALU.add)
                v.tensor_tensor(rout, rout, t1, ALU.mult).then_inc(ve, 1)
            on("vector", dv1)
            return led.inc("ve")

        def emit_recip(vin, rout, t0, t1):
            def sc0(s):
                b0 = led.val("ac")
                s.wait_ge(ve, led.val("ve"))
                s.wait_ge(pe, led.val("pe"))
                s.activation(t0, vin, AF.Ln).then_inc(ac, 1)
                s.wait_ge(ac, b0 + 1)
                s.activation(rout, t0, AF.Exp, scale=-1.0).then_inc(ac, 1)
            on("scalar", sc0)
            led.inc("ac")
            a0 = led.inc("ac")

            def dv1(v):
                v.wait_ge(ac, a0)
                v.tensor_tensor(t1, vin, rout, ALU.mult)
                v.tensor_scalar(t1, t1, -1.0, 2.0, ALU.mult, ALU.add)
                v.tensor_tensor(rout, rout, t1, ALU.mult).then_inc(ve, 1)
            on("vector", dv1)
            return led.inc("ve")

        def emit_quant(v, src_ap, dst, rs2_col):
            return v.tensor_scalar(dst, src_ap, rs2_col, None, ALU.mult)

        def p0_sync(q):
            dma_in(q, "cst", sI[:, :], iden[:, :])
            dma_in(q, "cst", sGN[:, :], gN2[:, :])
            dma_in(q, "cst", sGO[:, :], gO2[:, :])
        on("sync", p0_sync)
        ld_cst = led.val("ld_cst")

        def p0_act(s):
            from concourse.hw_specs import get_activation_tables
            idx = None
            for i, fns in enumerate(get_activation_tables(nc.m.arch).values()):
                names = {f.name for f in fns}
                if {"Exp", "Ln", "Square"} <= names:
                    idx = i
                    break
            assert idx is not None
            s.add_instruction(mybir.InstLoadActFuncSet(
                name=nc.get_next_instruction_name(),
                act_func_set_id=idx, ins=[], outs=[]))
        on("scalar", p0_act)

        def p0_vec(v):
            v.memset(sOne[:, :], 1.0)
            v.memset(tiny[:, :], 0.0)
            v.memset(SCAL[:, :], 0.0)
            v.memset(amc[:, :], 0.0).then_inc(ve, 1)
        on("vector", p0_vec)
        led.inc("ve")

        pw_units = []
        wt_gate = {0: 0, 1: 0}
        w8c_gate = {0: 0, 1: 0}
        pw_state = {"scal": None}

        def mk_amax_unit(t, qq):
            def unit():
                slot = (t * 4 + qq) % 2
                g = wt_gate[slot]

                def ld_(q):
                    if g:
                        q.wait_ge(ve, g)
                    dma_in(q, "wt%d" % slot, wtst[:, slot, :],
                           WtT[t, :, qq * 512:(qq + 1) * 512])
                on("sync", ld_)
                t_ld = led.val("ld_wt%d" % slot)

                def red_(v):
                    v.wait_ge(LD["wt%d" % slot], t_ld)
                    v.reduce_max(amcols[:, t * 4 + qq:t * 4 + qq + 1],
                                 wtst[:, slot, :], AX.X,
                                 apply_absolute_value=True).then_inc(ve, 1)
                on("vector", red_)
                wt_gate[slot] = led.inc("ve")
            return unit

        def pw_combine_unit():
            ve_hop = led.val("ve")

            def pw_comb(v):
                v.wait_ge(ve, ve_hop)
                v.reduce_max(amc[:, 0:1], amcols[:, 0:64], AX.X).then_inc(ve, 1)
            on("vector", pw_comb)
            ve_am = led.inc("ve")

            def pw_tr(tn):
                tn.wait_ge(ve, ve_am)
                tn.wait_ge(LD["cst"], ld_cst)
                tn.transpose(ps[5][0:1, 0:P], amc[:, 0:1],
                             sI[:, :]).then_inc(pe, 1)
            on("tensor", pw_tr)
            pe_tr = led.inc("pe")

            def pw_sw(v):
                b0 = led.val("ve")
                v.wait_ge(pe, pe_tr)
                v.reduce_max(tiny[0:1, 0:1], ps[5][0:1, 0:P],
                             AX.X).then_inc(ve, 1)
                v.wait_ge(ve, b0 + 1)
                v.tensor_single_scalar(tiny[0:1, 0:1], tiny[0:1, 0:1], INV448,
                                       ALU.mult).then_inc(ve, 1)
                v.wait_ge(ve, b0 + 2)
                v.tensor_single_scalar(tiny[0:1, 0:1], tiny[0:1, 0:1], 1e-12,
                                       ALU.max).then_inc(ve, 1)
                v.wait_ge(ve, b0 + 3)
                v.reciprocal(amcols[0:1, 96:97], tiny[0:1, 0:1]).then_inc(ve, 1)
                v.wait_ge(ve, b0 + 4)
                v.tensor_single_scalar(amcols[0:1, 97:98], amcols[0:1, 96:97],
                                       0.5, ALU.mult).then_inc(ve, 1)
            on("vector", pw_sw)
            for _ in range(4):
                led.inc("ve")
            ve_sw = led.inc("ve")

            def pw_bc(tn):
                tn.wait_ge(ve, ve_sw)
                tn.matmul(ps[5][0:P, 0:2], sOne[0:1, 0:P], amcols[0:1, 96:98],
                          start=True, stop=True).then_inc(pe, 1)
            on("tensor", pw_bc)
            pe_bc = led.inc("pe")

            def pw_bc_ev(v):
                v.wait_ge(pe, pe_bc)
                v.tensor_copy(SCAL[:, 0:2], ps[5][0:P, 0:2]).then_inc(ve, 1)
            on("vector", pw_bc_ev)
            pw_state["scal"] = led.inc("ve")

        def mk_quant_unit(t, qq):
            def unit():
                slot = (t * 4 + qq) % 2
                g = wt_gate[slot]
                sog = w8c_gate[slot]

                def ld_(q):
                    if g:
                        q.wait_ge(ve, g)
                    dma_in(q, "wt%d" % slot, wtst[:, slot, :],
                           WtT[t, :, qq * 512:(qq + 1) * 512])
                on("sync", ld_)
                t_ld = led.val("ld_wt%d" % slot)

                def q_(v):
                    v.wait_ge(LD["wt%d" % slot], t_ld)
                    v.wait_ge(ve, pw_state["scal"])
                    if sog:
                        v.wait_ge(so, sog)
                    emit_quant(v, wtst[:, slot, :], w8c[:, slot, :],
                               SCAL[:, 1:2]).then_inc(ve, 1)
                on("vector", q_)
                wt_gate[slot] = ve_q = led.inc("ve")

                def st_(q):
                    q.wait_ge(ve, ve_q)
                    dma_out(q, w8d[qq, t // 2, :,
                                   (t % 2) * 512:(t % 2) * 512 + 512],
                            w8c[:, slot, :])
                on("sync", st_)
                w8c_gate[slot] = led.val("so")
            return unit

        for t in range(NF):
            for qq in range(4):
                pw_units.append(mk_amax_unit(t, qq))
        pw_units.append(pw_combine_unit)
        for t in range(NF):
            for qq in range(4):
                pw_units.append(mk_quant_unit(t, qq))

        def pump(n):
            for _ in range(n):
                if pw_units:
                    pw_units.pop(0)()

        xgate = [None, None]

        def xwait(q, sl):
            if xgate[sl] is not None:
                gp, ga, gv = xgate[sl]
                q.wait_ge(pe, gp)
                q.wait_ge(ac, ga)
                q.wait_ge(ve, gv)

        def xmark(sl):
            xgate[sl] = (led.val("pe"), led.val("ac"), led.val("ve"))

        sc_mark = {}
        r12_mark = {}
        apply_end_pe = {}
        kv_gate = {0: 0, 1: 0, 2: 0, 3: 0}

        def stat_bufs(blk):
            own = blk >= 2
            mu_dst = Smu_o[:, blk - 2, :] if own else Smu3[:, 0, :]
            r12_dst = Sr12_o[:, blk - 2, :] if own else Srho3[:, 0, :]
            return mu_dst, r12_dst

        stats_end_pe = {}

        def emit_stats(blk):
            pa, pb = (0, 1) if blk % 2 == 0 else (6, 7)
            for g2 in range(8):
                sl = g2 % 2

                def p1_ld(q, blk=blk, g2=g2, sl=sl):
                    xwait(q, sl)
                    dma_in(q, "xb%d" % sl, xbig[:, sl * 2:sl * 2 + 2, :],
                           xblk[blk, :, g2 * 2:(g2 + 1) * 2, :])
                on("sync", p1_ld)
                t_ld = led.val("ld_xb%d" % sl)
                pe_base = led.val("pe")

                def p1_sq(s, g2=g2, sl=sl, t_ld=t_ld, pe_base=pe_base):
                    s.wait_ge(LD["xb%d" % sl], t_ld)
                    s.wait_ge(pe, pe_base)
                    for i in range(2):
                        s.activation(ptil[:, i, :], xbig[:, sl * 2 + i, :],
                                     AF.Square).then_inc(ac, 1)
                on("scalar", p1_sq)
                ac_marks = [led.inc("ac") for _ in range(2)]

                ve_ps01 = sc_mark.get(blk - 2, 0)

                def p1_mm(tn, g2=g2, sl=sl, t_ld=t_ld, ac_marks=ac_marks,
                          ve_ps01=ve_ps01, pa=pa, pb=pb):
                    tn.wait_ge(LD["xb%d" % sl], t_ld)
                    if g2 == 0 and ve_ps01:
                        tn.wait_ge(ve, ve_ps01)
                    for i in range(2):
                        fc = g2 * 2 + i
                        tn.matmul(ps[pa][:, :], sOne[:, :],
                                  xbig[:, sl * 2 + i, :],
                                  start=(fc == 0), stop=(fc == NF - 1))
                        tn.wait_ge(ac, ac_marks[i])
                        tn.matmul(ps[pb][:, :], sOne[:, :], ptil[:, i, :],
                                  start=(fc == 0),
                                  stop=(fc == NF - 1)).then_inc(pe, 1)
                on("tensor", p1_mm)
                for _ in range(2):
                    led.inc("pe")
                xmark(sl)
                pump(2)
            stats_end_pe[blk] = led.val("pe")

        def emit_chain(blk):
            mu_dst, r12_dst = stat_bufs(blk)
            pa, pb = (0, 1) if blk % 2 == 0 else (6, 7)
            pe_stats = stats_end_pe[blk]

            def p1_sc(v, mu_dst=mu_dst, pe_stats=pe_stats, pa=pa, pb=pb):
                v.wait_ge(pe, pe_stats)
                v.tensor_scalar(mu_dst, ps[pa][:, :], INV2048, None, ALU.mult)
                v.tensor_scalar(S[:, 1, :], ps[pb][:, :], INV2048, None,
                                ALU.mult)
                v.tensor_tensor(S[:, 2, :], mu_dst, mu_dst, ALU.mult)
                v.tensor_tensor(S[:, 2, :], S[:, 1, :], S[:, 2, :],
                                ALU.subtract)
                v.tensor_single_scalar(S[:, 4, :], S[:, 2, :], 1.0,
                                       ALU.mult).then_inc(ve, 1)
            on("vector", p1_sc)
            sc_mark[blk] = led.inc("ve")
            emit_rsqrt(S[:, 4, :], S[:, 3, :], EPS, S[:, 1, :])

            def p1_sc2(v):
                v.tensor_tensor(S[:, 1, :], S[:, 3, :], S[:, 3, :], ALU.mult)
                v.tensor_tensor(S[:, 1, :], S[:, 1, :], S[:, 2, :],
                                ALU.mult).then_inc(ve, 1)
            on("vector", p1_sc2)
            led.inc("ve")
            emit_rsqrt(S[:, 1, :], S[:, 4, :], EPS, S[:, 2, :])

            def p1_sc3(v, r12_dst=r12_dst):
                v.tensor_tensor(r12_dst, S[:, 3, :], S[:, 4, :],
                                ALU.mult).then_inc(ve, 1)
            on("vector", p1_sc3)
            r12_mark[blk] = led.inc("ve")

        def emit_apply(blk):
            mu_dst, r12_dst = stat_bufs(blk)
            ve_r12 = r12_mark[blk]
            ap_gate = apply_end_pe.get(blk - 1, 0)
            prev_kv_mark = 0
            for g2 in range(8):
                sl = g2 % 2
                gates = [kv_gate[(g2 * 2 + i) % 4] for i in range(2)]

                def p1_ld2(q, blk=blk, g2=g2, sl=sl, gates=gates):
                    xwait(q, sl)
                    for i in range(2):
                        fc = g2 * 2 + i
                        if gates[i]:
                            q.wait_ge(pe, gates[i])
                        dma_in(q, "kv%d" % (fc % 4), wkvst[:, fc % 4, :],
                               Wkv_t[fc, :, :])
                    dma_in(q, "xb%d" % sl, xbig[:, sl * 2:sl * 2 + 2, :],
                           xblk[blk, :, g2 * 2:(g2 + 1) * 2, :])
                on("sync", p1_ld2)
                t_ld = led.val("ld_xb%d" % sl)
                t_kv = {i: led.val("ld_kv%d" % i) for i in range(4)}

                if blk == 2:
                    xn2dst = [B[:, g2 * 2, :], B[:, g2 * 2 + 1, :]]
                elif blk == 3:
                    xn2dst = [C[:, g2 * 2, RB:2 * RB],
                              C[:, g2 * 2 + 1, RB:2 * RB]]
                else:
                    xn2dst = [SQ[:, 0, :], SQ[:, 1, :]]

                def p1_ap(v, g2=g2, sl=sl, t_ld=t_ld, ve_r12=ve_r12,
                          ap_gate=ap_gate, mu_dst=mu_dst, r12_dst=r12_dst,
                          xn2dst=xn2dst, prev_kv_mark=prev_kv_mark):
                    v.wait_ge(LD["xb%d" % sl], t_ld)
                    if g2 == 0:
                        v.wait_ge(ve, ve_r12)
                        if ap_gate:
                            v.wait_ge(pe, ap_gate)
                    elif prev_kv_mark:
                        v.wait_ge(pe, prev_kv_mark)
                    for i in range(2):
                        fc = g2 * 2 + i
                        v.tensor_tensor(S[:, 0, :], xbig[:, sl * 2 + i, :],
                                        mu_dst, ALU.subtract)
                        v.scalar_tensor_tensor(
                            xn2dst[i], S[:, 0, :], sGN[:, fc:fc + 1],
                            r12_dst, ALU.mult, ALU.mult).then_inc(ve, 1)
                on("vector", p1_ap)
                ve_marks = [led.inc("ve") for _ in range(2)]

                def p1_kv(tn, g2=g2, t_kv=t_kv, ve_marks=ve_marks,
                          xn2dst=xn2dst):
                    for i in range(2):
                        fc = g2 * 2 + i
                        tn.wait_ge(LD["kv%d" % (fc % 4)], t_kv[fc % 4])
                        tn.wait_ge(ve, ve_marks[i])
                        tn.matmul(ps[2][:, :], wkvst[:, fc % 4, 0:128],
                                  xn2dst[i],
                                  start=(fc == 0), stop=(fc == NF - 1))
                        tn.matmul(ps[3][:, :], wkvst[:, fc % 4, 128:256],
                                  xn2dst[i],
                                  start=(fc == 0),
                                  stop=(fc == NF - 1)).then_inc(pe, 1)
                on("tensor", p1_kv)
                for i in range(2):
                    kv_gate[(g2 * 2 + i) % 4] = led.inc("pe")
                prev_kv_mark = led.val("pe")
                xmark(sl)
                pump(2)

            pe_kv = led.val("pe")

            def p1_ev(v, blk=blk, pe_kv=pe_kv):
                v.wait_ge(pe, pe_kv)
                v.tensor_copy(kT[:, blk * RB:(blk + 1) * RB], ps[2][:, :])
                v.tensor_copy(SQ[:, 0, :], ps[3][:, :]).then_inc(ve, 1)
            on("vector", p1_ev)
            ve_ev = led.inc("ve")

            ve_trbase = led.val("ve")
            for jj in range(4):
                jc = blk * 4 + jj

                def p1_tr(tn, jj=jj, ve_ev=ve_ev, ve_trbase=ve_trbase):
                    tn.wait_ge(ve, ve_ev)
                    if jj >= 1:
                        tn.wait_ge(ve, ve_trbase + jj)
                    tn.transpose(ps[4][:, 0:128],
                                 SQ[:, 0, jj * 128:(jj + 1) * 128],
                                 sI[:, :]).then_inc(pe, 1)
                on("tensor", p1_tr)
                pe_t = led.inc("pe")

                def p1_trev(v, jc=jc, pe_t=pe_t):
                    v.wait_ge(pe, pe_t)
                    v.tensor_copy(vv[:, jc, :],
                                  ps[4][:, 0:128]).then_inc(ve, 1)
                on("vector", p1_trev)
                led.inc("ve")
            apply_end_pe[blk] = led.val("pe")

        emit_stats(0)
        emit_stats(1)
        emit_chain(0)
        emit_stats(2)
        emit_apply(0)
        emit_chain(1)
        emit_stats(3)
        emit_apply(1)
        emit_chain(2)
        emit_apply(2)
        emit_chain(3)
        emit_apply(3)

        pump(len(pw_units))

        ve_kl0 = led.val("ve")
        pe_kl0 = led.val("pe")
        klm = []
        for b4 in range(NB):
            def kl_sq(s, b4=b4, ve_kl0=ve_kl0, pe_kl0=pe_kl0,
                      klm=tuple(klm)):
                if b4 == 0:
                    s.wait_ge(ve, ve_kl0)
                    s.wait_ge(pe, pe_kl0)
                if b4 >= 2:
                    s.wait_ge(pe, klm[b4 - 2])
                s.activation(ptil[:, b4 % 2, :],
                             kT[:, b4 * RB:(b4 + 1) * RB],
                             AF.Square).then_inc(ac, 1)
            on("scalar", kl_sq)
            ac_m = led.inc("ac")

            def kl_mm(tn, b4=b4, ac_m=ac_m, ve_kl0=ve_kl0):
                tn.wait_ge(ac, ac_m)
                tn.wait_ge(ve, ve_kl0)
                tn.matmul(ps[4 + b4][:, :], sOne[:, :], ptil[:, b4 % 2, :],
                          start=True, stop=True).then_inc(pe, 1)
            on("tensor", kl_mm)
            klm.append(led.inc("pe"))

        kl_lad = [(xbig[:, 0:2, :], xbig[:, 2:4, :], 0),
                  (Smu_o[:, :, :], Sr12_o[:, :, :], 2)]
        v0m = []
        for li, (vio, rout, bb) in enumerate(kl_lad):
            def kl_ev(v, vio=vio, li=li, pe_l=klm[li * 2 + 1]):
                v.wait_ge(pe, pe_l)
                v.tensor_copy(vio[:, 0, :], ps[4 + 2 * li][:, :])
                v.tensor_copy(vio[:, 1, :], ps[5 + 2 * li][:, :])
                v.tensor_single_scalar(vio, vio, float(L2EPS),
                                       ALU.add).then_inc(ve, 1)
            on("vector", kl_ev)
            v0m.append(led.inc("ve"))

        a0m = []
        for li, (vio, rout, bb) in enumerate(kl_lad):
            def kl_ln(s, vio=vio, rout=rout, v0=v0m[li]):
                b0 = led.val("ac")
                s.wait_ge(ve, v0)
                s.activation(rout, vio, AF.Ln).then_inc(ac, 1)
                s.wait_ge(ac, b0 + 1)
                s.activation(rout, rout, AF.Exp,
                             scale=-0.5).then_inc(ac, 1)
            on("scalar", kl_ln)
            led.inc("ac")
            a0m.append(led.inc("ac"))

        for li, (vio, rout, bb) in enumerate(kl_lad):
            def kl_nw(v, vio=vio, rout=rout, bb=bb, a0=a0m[li]):
                v.wait_ge(ac, a0)
                v.tensor_tensor(vio, vio, rout, ALU.mult)
                v.tensor_tensor(vio, vio, rout, ALU.mult)
                v.tensor_scalar(vio, vio, -0.5, 1.5, ALU.mult, ALU.add)
                v.tensor_tensor(rout, rout, vio, ALU.mult)
                v.tensor_single_scalar(rout, rout, 4.0, ALU.mult)
                for hh in range(2):
                    b4 = bb + hh
                    ins = v.tensor_tensor(kT[:, b4 * RB:(b4 + 1) * RB],
                                          kT[:, b4 * RB:(b4 + 1) * RB],
                                          rout[:, hh, :], ALU.mult)
                ins.then_inc(ve, 1)
            on("vector", kl_nw)
            led.inc("ve")

        emit_barrier()
        if "kT" in tap_dram:
            on("sync", lambda q: dma_out(q, tap_dram["kT"][:, :], kT[:, :]))
        if "v" in tap_dram:
            on("sync", lambda q: dma_out(q, tap_dram["v"][:, :, :], vv[:, :, :]))
        if "stat1" in tap_dram:
            def ts1(q):
                dma_out(q, tap_dram["stat1"][:, 0:2, :], Smu_o[:, :, :])
                dma_out(q, tap_dram["stat1"][:, 2:4, :], Sr12_o[:, :, :])
            on("sync", ts1)

        def emit_tail(rc, oblk, xfree=None):
            pe_st = led.val("pe")

            def p5_sc(v, pe_st=pe_st):
                v.wait_ge(pe, pe_st)
                v.tensor_scalar(S[:, 1, :], ps[4][:, :], INV2048, None,
                                ALU.mult)
                v.tensor_scalar(S[:, 2, :], ps[5][:, :], INV2048, None,
                                ALU.mult)
                v.tensor_tensor(S[:, 0, :], S[:, 1, :], S[:, 1, :], ALU.mult)
                v.tensor_tensor(S[:, 2, :], S[:, 2, :], S[:, 0, :],
                                ALU.subtract).then_inc(ve, 1)
            on("vector", p5_sc)
            sc_m = led.inc("ve")
            emit_rsqrt(S[:, 2, :], S[:, 3, :], EPS, S[:, 4, :])

            resid_marks = []
            for g2 in range(8):
                sl = g2 % 2

                def p5_xl(g, oblk=oblk, g2=g2, sl=sl, xfree=xfree):
                    xwait(g, sl)
                    if xfree and g2 < 2:
                        g.wait_ge(ve, xfree)
                    dma_in(g, "xg%d" % sl, xbig[:, sl * 2:sl * 2 + 2, :],
                           xblk[oblk, :, g2 * 2:(g2 + 1) * 2, :])
                on("gpsimd", p5_xl)
                t_ld = led.val("ld_xg%d" % sl)

                def p5_res(v, g2=g2, sl=sl, rc=rc, t_ld=t_ld):
                    v.wait_ge(LD["xg%d" % sl], t_ld)
                    for i in range(2):
                        fc = g2 * 2 + i
                        ctile = C[:, fc, rc * RB:(rc + 1) * RB]
                        v.tensor_tensor(S[:, 0, :], ctile, S[:, 1, :],
                                        ALU.subtract)
                        v.scalar_tensor_tensor(S[:, 0, :], S[:, 0, :],
                                               sGO[:, fc:fc + 1], S[:, 3, :],
                                               ALU.mult, ALU.mult)
                        ins = v.tensor_tensor(ctile, xbig[:, sl * 2 + i, :],
                                              S[:, 0, :], ALU.add)
                    ins.then_inc(ve, 1)
                on("vector", p5_res)
                resid_marks.append(led.inc("ve"))
                xmark(sl)

            pe6_base = led.val("pe")
            for fc in range(NF):
                def p6_sq(s, fc=fc, rc=rc, pe6_base=pe6_base,
                          rm=resid_marks[fc // 2]):
                    s.wait_ge(ve, rm)
                    s.wait_ge(pe, pe6_base + max(0, fc - 1))
                    s.activation(ptil[:, fc % 2, :],
                                 C[:, fc, rc * RB:(rc + 1) * RB],
                                 AF.Square).then_inc(ac, 1)
                on("scalar", p6_sq)
                ac_m = led.inc("ac")

                def p6_smm(tn, fc=fc, rc=rc, ac_m=ac_m, sc_m=sc_m,
                           rm=resid_marks[fc // 2]):
                    if fc == 0:
                        tn.wait_ge(ve, sc_m)
                    tn.wait_ge(ve, rm)
                    tn.matmul(ps[4][:, :], sOne[:, :],
                              C[:, fc, rc * RB:(rc + 1) * RB],
                              start=(fc == 0), stop=(fc == NF - 1))
                    tn.wait_ge(ac, ac_m)
                    tn.matmul(ps[5][:, :], sOne[:, :], ptil[:, fc % 2, :],
                              start=(fc == 0),
                              stop=(fc == NF - 1)).then_inc(pe, 1)
                on("tensor", p6_smm)
                led.inc("pe")

            pe_st6 = led.val("pe")

            def p6_sc(v, rc=rc, pe_st6=pe_st6):
                v.wait_ge(pe, pe_st6)
                v.tensor_scalar(Smu3[:, rc, :], ps[4][:, :], INV2048, None,
                                ALU.mult)
                v.tensor_scalar(S[:, 2, :], ps[5][:, :], INV2048, None,
                                ALU.mult)
                v.tensor_tensor(S[:, 0, :], Smu3[:, rc, :], Smu3[:, rc, :],
                                ALU.mult)
                v.tensor_tensor(S[:, 2, :], S[:, 2, :], S[:, 0, :],
                                ALU.subtract).then_inc(ve, 1)
            on("vector", p6_sc)
            led.inc("ve")
            emit_rsqrt(S[:, 2, :], Srho3[:, rc, :], EPS, S[:, 4, :])

            for fc in range(NF):
                def p6_am(v, fc=fc, rc=rc, last=(fc == NF - 1)):
                    v.tensor_tensor(S[:, 0, :],
                                    C[:, fc, rc * RB:(rc + 1) * RB],
                                    Smu3[:, rc, :], ALU.subtract)
                    v.tensor_tensor(S[:, 0, :], S[:, 0, :], Srho3[:, rc, :],
                                    ALU.mult)
                    col = 64 + rc * NF + fc
                    ins = v.reduce_max(amcols[:, col:col + 1], S[:, 0, :],
                                       AX.X, apply_absolute_value=True)
                    if last:
                        ins.then_inc(ve, 1)
                on("vector", p6_am)
            led.inc("ve")

        pending_tail = []
        xbig_free = {}
        for rc in range(2):
            oblk = 2 + rc


            pe_qend_prev = None
            ve_evq_prev = None
            for hg in range(4):
                pe_base = led.val("pe")
                ve_done_prev = ve_evq_prev if hg else led.val("ve")
                pe_wst_prev = pe_qend_prev if hg else pe_base

                for fc in range(NF):
                    def p3_ld(q, hg=hg, fc=fc, pe_base=pe_base,
                              pe_wst_prev=pe_wst_prev):
                        if fc >= 2:
                            q.wait_ge(pe, pe_base + fc - 1)
                        else:
                            q.wait_ge(pe, pe_wst_prev)
                        dma_in(q, "w%d" % (fc % 2), wst[:, fc % 2, :],
                               Wq_t[hg, fc, :, :])
                    on("sync", p3_ld)
                    t_ld = led.val("ld_w%d" % (fc % 2))

                    def p3_mm(tn, fc=fc, rc=rc, t_ld=t_ld,
                              ve_done_prev=ve_done_prev):
                        tn.wait_ge(LD["w%d" % (fc % 2)], t_ld)
                        if fc == 0:
                            tn.wait_ge(ve, ve_done_prev)
                        xn2src = (B[:, fc, :] if rc == 0
                                  else C[:, fc, RB:2 * RB])
                        for hh in range(4):
                            ins = tn.matmul(
                                ps[hh][:, :],
                                wst[:, fc % 2, hh * 128:(hh + 1) * 128],
                                xn2src, start=(fc == 0),
                                stop=(fc == NF - 1))
                        ins.then_inc(pe, 1)
                    on("tensor", p3_mm)
                    led.inc("pe")
                pe_qend_prev = led.val("pe")

                pe_q = led.val("pe")

                def p3_evq(v, hg=hg, pe_q=pe_q):
                    v.wait_ge(pe, pe_q)
                    for hh in range(4):
                        ins = v.tensor_copy(A[:, hg * 4 + hh, :], ps[hh][:, :])
                    ins.then_inc(ve, 1)
                on("vector", p3_evq)
                ve_evq_prev = led.inc("ve")

                if rc == 0:
                    lad = [(xbig[:, 0:2, :], xbig[:, 2:4, :], 0),
                           (Smu_o[:, :, :], Sr12_o[:, :, :], 2)]
                    banks = (4, 5, 6, 7)
                else:
                    lad = [(B[:, 0:2, :], B[:, 2:4, :], 0),
                           (Smu_o[:, :, :], Sr12_o[:, :, :], 2)]
                    banks = (6, 7, 6, 7)
                ve_evq = led.val("ve")
                pe_now3 = led.val("pe")
                ve_lfree = led.val("ve")
                mm_marks = []
                v0m = []
                for li, (vio, rout, hb) in enumerate(lad):
                    for j in range(2):
                        hh = li * 2 + j
                        h = hg * 4 + hh

                        def p3_sq(s, h=h, hh=hh, ve_evq=ve_evq,
                                  pe_now3=pe_now3, mm_marks=tuple(mm_marks)):
                            if hh == 0:
                                s.wait_ge(ve, ve_evq)
                                s.wait_ge(pe, pe_now3)
                            if hh >= 2:
                                s.wait_ge(pe, mm_marks[hh - 2])
                            s.activation(SQ[:, hh % 2, :], A[:, h, :],
                                         AF.Square).then_inc(ac, 1)
                        on("scalar", p3_sq)
                        ac_m = led.inc("ac")

                        def p3_lmm(tn, ac_m=ac_m, hh=hh, bank=banks[hh],
                                   ve_lfree=ve_lfree,
                                   ev1=(v0m[0] if v0m else None)):
                            tn.wait_ge(ac, ac_m)
                            if hh == 0:
                                tn.wait_ge(ve, ve_lfree)
                            if hh == 2 and ev1 is not None and bank == banks[0]:
                                tn.wait_ge(ve, ev1)
                            tn.matmul(ps[bank][:, :], sOne[:, :],
                                      SQ[:, hh % 2, :],
                                      start=True, stop=True).then_inc(pe, 1)
                        on("tensor", p3_lmm)
                        mm_marks.append(led.inc("pe"))

                    def p3_ev(v, vio=vio, b0=banks[li * 2], b1=banks[li * 2 + 1],
                              pe_l=mm_marks[-1]):
                        v.wait_ge(pe, pe_l)
                        v.tensor_copy(vio[:, 0, :], ps[b0][:, :])
                        v.tensor_copy(vio[:, 1, :], ps[b1][:, :])
                        v.tensor_single_scalar(vio, vio,
                                               float(L2EPS / 256.0),
                                               ALU.add).then_inc(ve, 1)
                    on("vector", p3_ev)
                    v0m.append(led.inc("ve"))

                a0m = []
                for li, (vio, rout, hb) in enumerate(lad):
                    def p3_ln(s, vio=vio, rout=rout, v0=v0m[li]):
                        b0 = led.val("ac")
                        s.wait_ge(ve, v0)
                        s.activation(rout, vio, AF.Ln).then_inc(ac, 1)
                        s.wait_ge(ac, b0 + 1)
                        s.activation(rout, rout, AF.Exp,
                                     scale=-0.5).then_inc(ac, 1)
                    on("scalar", p3_ln)
                    led.inc("ac")
                    a0m.append(led.inc("ac"))

                for li, (vio, rout, hb) in enumerate(lad):
                    def p3_nw(v, vio=vio, rout=rout, hb=hb, hg=hg,
                              a0=a0m[li]):
                        v.wait_ge(ac, a0)
                        v.tensor_tensor(vio, vio, rout, ALU.mult)
                        v.tensor_tensor(vio, vio, rout, ALU.mult)
                        v.tensor_scalar(vio, vio, -0.5, 1.5,
                                        ALU.mult, ALU.add)
                        v.tensor_tensor(rout, rout, vio, ALU.mult)
                        v.tensor_single_scalar(rout, rout, 4.0, ALU.mult)
                        for hh in range(2):
                            h = hg * 4 + hb + hh
                            ins = v.tensor_tensor(A[:, h, :], A[:, h, :],
                                                  rout[:, hh, :], ALU.mult)
                        ins.then_inc(ve, 1)
                    on("vector", p3_nw)
                    led.inc("ve")

            if rc == 0:
                xbig_free[0] = led.val("ve")
            elif pending_tail:
                pending_tail.pop(0)()

            emit_barrier()
            if "qn" in tap_dram:
                def tqn(q, rc=rc):
                    dma_out(q, tap_dram["qn"][:, :, rc * RB:(rc + 1) * RB],
                            A[:, :, :])
                on("sync", tqn)

            for h in range(NH):
                pe_base = led.val("pe")
                ac_base = led.val("ac")
                ve_att_prev = led.val("ve")

                plan = ["sim0", "sim1"]
                for jc in range(NJ):
                    plan.append("sa%d" % jc)
                    if jc + 2 < NJ:
                        plan.append("sim%d" % (jc + 2))
                sim_mark, av_mark = {}, {}
                m = pe_base
                for stp in plan:
                    m += 1
                    if stp.startswith("sim"):
                        sim_mark[int(stp[3:])] = m
                    else:
                        av_mark[int(stp[2:])] = m

                def p4_pe(tn, h=h, pe_base=pe_base, ac_base=ac_base,
                          ve_att_prev=ve_att_prev, plan=plan):
                    for stp in plan:
                        if stp.startswith("sim"):
                            jc = int(stp[3:])
                            if jc < 2:
                                tn.wait_ge(ac, ac_base)
                            else:
                                tn.wait_ge(ac, ac_base + jc - 1)
                            tn.matmul(ps[jc % 2][:, :],
                                      kT[:, jc * 128:(jc + 1) * 128],
                                      A[:, h, :], start=True,
                                      stop=True).then_inc(pe, 1)
                        else:
                            jc = int(stp[2:])
                            if jc == 0:
                                tn.wait_ge(ve, ve_att_prev)
                            tn.wait_ge(ac, ac_base + jc + 1)
                            tn.matmul(ps[3][:, :], sOne[:, :],
                                      ptil[:, jc % 2, :],
                                      start=(jc == 0), stop=(jc == NJ - 1))
                            tn.matmul(ps[2][:, :], vv[:, jc, :],
                                      ptil[:, jc % 2, :],
                                      start=(jc == 0),
                                      stop=(jc == NJ - 1)).then_inc(pe, 1)
                on("tensor", p4_pe)
                for _ in range(2 * NJ):
                    led.inc("pe")
                assert led.val("pe") == m

                def p4_exp(s, sim_mark=sim_mark, av_mark=av_mark):
                    for jc in range(NJ):
                        s.wait_ge(pe, sim_mark[jc])
                        if jc >= 2:
                            s.wait_ge(pe, av_mark[jc - 2])
                        s.activation(ptil[:, jc % 2, :], ps[jc % 2][:, :],
                                     AF.Exp).then_inc(ac, 1)
                on("scalar", p4_exp)
                exp_marks = [led.inc("ac") for _ in range(NJ)]
                assert exp_marks[-1] == ac_base + NJ

                pe_last = av_mark[NJ - 1]

                def p4_cp(v, pe_last=pe_last):
                    v.wait_ge(pe, pe_last)
                    v.tensor_copy(S[:, 2, :], ps[3][:, :]).then_inc(ve, 1)
                on("vector", p4_cp)
                led.inc("ve")
                emit_recip(S[:, 2, :], S[:, 1, :], S[:, 4, :], S[:, 0, :])

                def p4_out(v, h=h):
                    v.tensor_tensor(B[:, h, :], ps[2][:, :], S[:, 1, :],
                                    ALU.mult).then_inc(ve, 1)
                on("vector", p4_out)
                led.inc("ve")

            emit_barrier()
            if "att" in tap_dram:
                def tat(q, rc=rc):
                    dma_out(q, tap_dram["att"][:, :, rc * RB:(rc + 1) * RB],
                            B[:, :, :])
                on("sync", tat)

            ve_p5start = led.val("ve")
            sqg = {}
            for fg in range(4):
                pe_base = led.val("pe")
                ve_prev = led.val("ve")
                for hdc in range(NF):
                    def p5_ld(q, fg=fg, hdc=hdc, pe_base=pe_base,
                              ve_prev=ve_prev):
                        if hdc >= 2:
                            q.wait_ge(pe, pe_base + hdc - 1)
                        if hdc == 0:
                            q.wait_ge(ve, ve_prev)
                        dma_in(q, "w%d" % (hdc % 2), wst[:, hdc % 2, :],
                               Wo_t[fg, hdc, :, :])
                    on("sync", p5_ld)
                    t_ld = led.val("ld_w%d" % (hdc % 2))

                    def p5_mm(tn, hdc=hdc, t_ld=t_ld):
                        tn.wait_ge(LD["w%d" % (hdc % 2)], t_ld)
                        for ff in range(4):
                            ins = tn.matmul(
                                ps[ff][:, :],
                                wst[:, hdc % 2, ff * 128:(ff + 1) * 128],
                                B[:, hdc, :], start=(hdc == 0),
                                stop=(hdc == NF - 1))
                        ins.then_inc(pe, 1)
                    on("tensor", p5_mm)
                    led.inc("pe")

                pe_fg = led.val("pe")

                def p5_ev(v, fg=fg, rc=rc, pe_fg=pe_fg):
                    v.wait_ge(pe, pe_fg)
                    for ff in range(4):
                        ins = v.tensor_copy(
                            C[:, fg * 4 + ff, rc * RB:(rc + 1) * RB],
                            ps[ff][:, :])
                    ins.then_inc(ve, 1)
                on("vector", p5_ev)
                ev_m = led.inc("ve")

                for ff in range(4):
                    fc = fg * 4 + ff

                    def p5_sq(s, fc=fc, rc=rc, ev_m=ev_m, g=sqg.get(fc - 2)):
                        s.wait_ge(ve, ev_m)
                        if g:
                            s.wait_ge(pe, g)
                        s.activation(SQ[:, fc % 2, :],
                                     C[:, fc, rc * RB:(rc + 1) * RB],
                                     AF.Square).then_inc(ac, 1)
                    on("scalar", p5_sq)
                    ac_m = led.inc("ac")

                    def p5_smm(tn, fc=fc, rc=rc, ac_m=ac_m, ev_m=ev_m,
                               ve_p5start=ve_p5start):
                        if fc == 0:
                            tn.wait_ge(ve, ve_p5start)
                        tn.wait_ge(ve, ev_m)
                        tn.matmul(ps[4][:, :], sOne[:, :],
                                  C[:, fc, rc * RB:(rc + 1) * RB],
                                  start=(fc == 0), stop=(fc == NF - 1))
                        tn.wait_ge(ac, ac_m)
                        tn.matmul(ps[5][:, :], sOne[:, :], SQ[:, fc % 2, :],
                                  start=(fc == 0),
                                  stop=(fc == NF - 1)).then_inc(pe, 1)
                    on("tensor", p5_smm)
                    sqg[fc] = led.inc("pe")

            if rc == 0:
                pending_tail.append(
                    lambda oblk=oblk: emit_tail(0, oblk, xbig_free[0]))
            else:
                emit_tail(1, oblk)

            emit_barrier()

        if "Cmid" in tap_dram:
            on("sync",
               lambda q: dma_out(q, tap_dram["Cmid"][:, :, :], C[:, :, :]))

        att_stack.close()
        xq = e(nc.sbuf_tensor("xq", [P, 2, NF, RB], FP8))

        ve_hop2 = led.val("ve")

        def p6_comb(v, ve_hop2=ve_hop2):
            v.wait_ge(ve, ve_hop2)
            v.reduce_max(amc[:, 1:2], amcols[:, 64:96], AX.X).then_inc(ve, 1)
        on("vector", p6_comb)
        led.inc("ve")
        ve_am2 = led.val("ve")

        def p6_tr(tn, ve_am2=ve_am2):
            tn.wait_ge(ve, ve_am2)
            tn.transpose(ps[7][0:1, 0:P], amc[:, 1:2], sI[:, :]).then_inc(pe, 1)
        on("tensor", p6_tr)
        pe_tr2 = led.inc("pe")

        def p6_loc(v, pe_tr2=pe_tr2):
            v.wait_ge(pe, pe_tr2)
            v.reduce_max(tiny[0:1, 1:2], ps[7][0:1, 0:P], AX.X).then_inc(ve, 1)
        on("vector", p6_loc)
        ve_loc = led.inc("ve")

        def p6_ccout(q, ve_loc=ve_loc):
            q.wait_ge(ve, ve_loc)
            dma_in(q, "cc2", ccin[:, :], tiny[0:1, 1:2])
        on("sync", p6_ccout)
        t_ldcc = led.val("ld_cc2")

        def p6_cc(g, t_ldcc=t_ldcc):
            g.wait_ge(LD["cc2"], t_ldcc)
            g.collective_compute(
                "AllReduce", ALU.max, replica_groups=[list(range(8))],
                ins=[ccin.ap().opt()], outs=[ccout.ap().opt()]).then_inc(cc, 1)
        on("gpsimd", p6_cc)
        led.inc("cc")

        def p6_ccin(g):
            g.wait_ge(cc, 1)
            dma_in(g, "cc3", tiny[0:1, 2:3], ccout[:, :])
        on("gpsimd", p6_ccin)
        t_ldg = led.val("ld_cc3")

        def p6_scale(v, t_ldg=t_ldg):
            b0 = led.val("ve")
            v.wait_ge(LD["cc3"], t_ldg)
            v.tensor_single_scalar(tiny[0:1, 3:4], tiny[0:1, 2:3], INV448,
                                   ALU.mult).then_inc(ve, 1)
            v.wait_ge(ve, b0 + 1)
            v.tensor_single_scalar(tiny[0:1, 3:4], tiny[0:1, 3:4], 1e-12,
                                   ALU.max).then_inc(ve, 1)
            v.wait_ge(ve, b0 + 2)
            v.reciprocal(S[0:1, 0, 0:1], tiny[0:1, 3:4]).then_inc(ve, 1)
            v.wait_ge(ve, b0 + 3)
            v.tensor_single_scalar(S[0:1, 0, 1:2], S[0:1, 0, 0:1], 0.5,
                                   ALU.mult).then_inc(ve, 1)
            v.wait_ge(ve, b0 + 4)
            v.tensor_tensor(S[0:1, 0, 2:3], tiny[0:1, 3:4], tiny[0:1, 0:1],
                            ALU.mult).then_inc(ve, 1)
            v.wait_ge(ve, b0 + 5)
            v.tensor_single_scalar(S[0:1, 0, 2:3], S[0:1, 0, 2:3], 4.0,
                                   ALU.mult).then_inc(ve, 1)
        on("vector", p6_scale)
        for _ in range(5):
            led.inc("ve")
        ve_row = led.inc("ve")

        def p6_bc(tn, ve_row=ve_row):
            tn.wait_ge(ve, ve_row)
            tn.matmul(ps[7][0:P, 0:3], sOne[0:1, 0:P], S[0:1, 0, 0:3],
                      start=True, stop=True).then_inc(pe, 1)
        on("tensor", p6_bc)
        pe_bc2 = led.inc("pe")

        def p6_bc_ev(v, pe_bc2=pe_bc2):
            v.wait_ge(pe, pe_bc2)
            v.tensor_copy(SCAL[:, 2:5], ps[7][0:P, 0:3]).then_inc(ve, 1)
        on("vector", p6_bc_ev)
        ve_scal_x = led.inc("ve")

        if "tiny" in tap_dram:
            on("sync", lambda q: dma_out(q, tap_dram["tiny"][:, :], tiny[:, :]))
        if "scal" in tap_dram:
            on("sync", lambda q: dma_out(q, tap_dram["scal"][:, :], SCAL[:, :]))

        ve_xq_rc = [None, None]
        for rc in range(2):
            for fc in range(NF):
                def p7_q(v, fc=fc, rc=rc, last=(fc == NF - 1)):
                    if rc == 0 and fc == 0:
                        v.wait_ge(ve, ve_scal_x)
                    v.tensor_tensor(S[:, 0, :], C[:, fc, rc * RB:(rc + 1) * RB],
                                    Smu3[:, rc, :], ALU.subtract)
                    v.tensor_tensor(S[:, 0, :], S[:, 0, :], Srho3[:, rc, :],
                                    ALU.mult)
                    ins = emit_quant(v, S[:, 0, :], xq[:, rc, fc, :],
                                     SCAL[:, 3:4])
                    if last:
                        ins.then_inc(ve, 1)
                on("vector", p7_q)
            ve_xq_rc[rc] = led.inc("ve")

            if "xq" in tap_dram:
                def txq(q, rc=rc, ve_xq=ve_xq_rc[rc]):
                    q.wait_ge(ve, ve_xq)
                    dma_out(q, tap_dram["xq"][:, :, rc * RB:(rc + 1) * RB],
                            xq[:, rc, :, :])
                on("sync", txq)

        w8_gate = {0: 0, 1: 0, 2: 0}
        for rc in range(2):

            for og in range(4):
                pe_base = led.val("pe")
                ve_xq = ve_xq_rc[rc]
                for pr in range(NF // 2):
                    slot = ((rc * 4 + og) * 8 + pr) % 3
                    g = w8_gate[slot]

                    def p7_ld(q, og=og, pr=pr, slot=slot, g=g):
                        if g:
                            q.wait_ge(pe, g)
                        dma_in(q, "w8%d" % slot, w8st[:, slot, :],
                               w8d[og, pr, :, :])
                    on("sync", p7_ld)
                    t_ld = led.val("ld_w8%d" % slot)

                    ve_now7 = led.val("ve")

                    def p7_mm(tn, pr=pr, rc=rc, slot=slot, t_ld=t_ld,
                              ve_now7=ve_now7, ve_xq=ve_xq):
                        tn.wait_ge(LD["w8%d" % slot], t_ld)
                        if pr == 0:
                            tn.wait_ge(ve, ve_xq)
                            tn.wait_ge(ve, ve_now7)
                        for i in range(2):
                            fc = pr * 2 + i
                            for oi in range(4):
                                ins = tn.matmul(
                                    ps[oi][:, :],
                                    w8st[:, slot,
                                         i * 512 + oi * 128:
                                         i * 512 + (oi + 1) * 128],
                                    xq[:, rc, fc, :], start=(fc == 0),
                                    stop=(fc == NF - 1))
                            ins.then_inc(pe, 1)
                    on("tensor", p7_mm)
                    led.inc("pe")
                    w8_gate[slot] = led.inc("pe")

                pe_og = led.val("pe")

                def p7_fin(v, og=og, rc=rc, pe_og=pe_og):
                    v.wait_ge(pe, pe_og)
                    for oi in range(4):
                        o = og * 4 + oi
                        ctile = C[:, o, rc * RB:(rc + 1) * RB]
                        v.tensor_scalar(qb1[:, :], ps[oi][:, :],
                                        SCAL[:, 4:5], None, ALU.mult)
                        ins = v.tensor_tensor(ctile, ctile, qb1[:, :], ALU.add)
                    ins.then_inc(ve, 1)
                on("vector", p7_fin)
                ve_fin = led.inc("ve")

                def p7_st(q, og=og, rc=rc, ve_fin=ve_fin):
                    q.wait_ge(ve, ve_fin)
                    for oi in range(4):
                        o = og * 4 + oi
                        dma_out(q, outT[o, :, rc * RB:(rc + 1) * RB],
                                C[:, o, rc * RB:(rc + 1) * RB])
                on("scalar", p7_st)

        if "C" in tap_dram:
            on("sync", lambda q: dma_out(q, tap_dram["C"][:, :, :], C[:, :, :]))

        so_total = led.val("so")

        def fin(q, so_total=so_total):
            q.wait_ge(so, so_total)
        on("sync", fin)

    return nc


def _layout_inputs(L0_attn_x, L0_attn_g_norm, L0_attn_Wq, L0_attn_Wkv,
                   L0_attn_Wout, L0_attn_g_out, L0_ffn_weight):
    x = np.ascontiguousarray(np.asarray(L0_attn_x, dtype=np.float32))
    Wq = np.ascontiguousarray(np.asarray(L0_attn_Wq, dtype=np.float32))
    Wkv = np.ascontiguousarray(np.asarray(L0_attn_Wkv, dtype=np.float32))
    Wout = np.ascontiguousarray(np.asarray(L0_attn_Wout, dtype=np.float32))
    Wt = np.asarray(L0_ffn_weight)
    gN = np.ascontiguousarray(np.asarray(L0_attn_g_norm, dtype=np.float32))
    gO = np.ascontiguousarray(np.asarray(L0_attn_g_out, dtype=np.float32))

    Wq_t = np.ascontiguousarray(Wq.reshape(NF, P, 4, 512).transpose(2, 0, 1, 3))
    Wkv_t = np.ascontiguousarray(Wkv.reshape(NF, P, 256))
    Wo_t = np.ascontiguousarray(Wout.reshape(NF, P, 4, 512).transpose(2, 0, 1, 3))
    WtT = np.ascontiguousarray(Wt.T).reshape(NF, P, 2048)
    gN2 = np.ascontiguousarray(gN.reshape(NF, P).T)
    gO2 = np.ascontiguousarray(gO.reshape(NF, P).T)
    iden = np.eye(P, dtype=np.float32)

    in_maps = []
    for c in range(8):
        b, half = c // 2, c % 2
        order = [2 * (1 - half), 2 * (1 - half) + 1, 2 * half, 2 * half + 1]
        xb = np.empty((NB, P, NF, RB), dtype=np.float32)
        for i, bi in enumerate(order):
            blkd = x[b, bi * RB:(bi + 1) * RB, :]
            xb[i] = blkd.T.reshape(NF, P, RB).transpose(1, 0, 2)
        in_maps.append(dict(xblk=xb, Wq_t=Wq_t, Wkv_t=Wkv_t, Wo_t=Wo_t,
                            WtT=WtT, gN2=gN2, gO2=gO2, iden=iden))
    return in_maps


def kernel(L0_attn_x, L0_attn_g_norm, L0_attn_Wq, L0_attn_Wkv, L0_attn_Wout,
           L0_attn_g_out, L0_ffn_weight):
    in_maps = _layout_inputs(L0_attn_x, L0_attn_g_norm, L0_attn_Wq,
                             L0_attn_Wkv, L0_attn_Wout, L0_attn_g_out,
                             L0_ffn_weight)
    nc = build_program(taps=tuple(_TAPS))
    res = run_bass_kernel_spmd(nc, in_maps, core_ids=list(range(8)))
    out = np.empty((4, 2048, 2048), dtype=np.float32)
    for c in range(8):
        b, half = c // 2, c % 2
        o = np.asarray(res.results[c]["outT"])
        out[b, half * 1024:(half + 1) * 1024, :] = (
            o.transpose(2, 0, 1).reshape(1024, 2048))
    if _TAPS:
        kernel._last_results = res
    return out


# version 21
# speedup vs baseline: 1.0214x; 1.0214x over previous
from contextlib import ExitStack

import numpy as np
import ml_dtypes
import concourse.bass as bass
import concourse.mybir as mybir
from concourse.bass_utils import run_bass_kernel_spmd

F32 = mybir.dt.float32
BF16 = mybir.dt.bfloat16
FP8 = mybir.dt.float8e4
U8 = mybir.dt.uint8
AF = mybir.ActivationFunctionType
ALU = mybir.AluOpType
AX = mybir.AxisListType

P = 128
NF = 16
NH = 16
NJ = 16
NB = 4
RB = 512
EPS = 1e-5
L2EPS = 1e-12
INV2048 = 1.0 / 2048.0
INV448 = 1.0 / 448.0

_TAPS = []


class Led:
    def __init__(self):
        self.n = {}

    def inc(self, s, k=1):
        self.n[s] = self.n.get(s, 0) + k
        return self.n[s]

    def val(self, s):
        return self.n.get(s, 0)


def build_program(taps=()):
    nc = bass.Bass("TRN2", target_bir_lowering=False, detect_race_conditions=False)

    xblk = nc.dram_tensor("xblk", [NB, P, NF, RB], F32, kind="ExternalInput")
    Wq_t = nc.dram_tensor("Wq_t", [4, NF, P, 512], F32, kind="ExternalInput")
    Wkv_t = nc.dram_tensor("Wkv_t", [NF, P, 256], F32, kind="ExternalInput")
    Wo_t = nc.dram_tensor("Wo_t", [4, NF, P, 512], F32, kind="ExternalInput")
    WtT = nc.dram_tensor("WtT", [NF, P, 2048], BF16, kind="ExternalInput")
    gN2 = nc.dram_tensor("gN2", [P, NF], F32, kind="ExternalInput")
    gO2 = nc.dram_tensor("gO2", [P, NF], F32, kind="ExternalInput")
    iden = nc.dram_tensor("iden", [P, P], F32, kind="ExternalInput")

    outT = nc.dram_tensor("outT", [NF, P, 1024], F32, kind="ExternalOutput")

    w8d = nc.dram_tensor("w8d", [4, NF // 2, P, 1024], FP8)
    ccin = nc.dram_tensor("ccin", [1, 1], F32)
    ccout = nc.dram_tensor("ccout", [1, 1], F32)

    tap_dram = {}
    for t in taps:
        shp, dt = {
            "kT": ([P, 2048], F32), "v": ([P, NJ, P], F32),
            "qn": ([P, NH, 2 * RB], F32), "att": ([P, NF, 2 * RB], F32),
            "C": ([P, NF, 1024], F32), "xq": ([P, NF, 2 * RB], BF16),
            "stat1": ([P, 8, RB], F32), "w8chk": ([4, NF, P, 512], BF16),
            "scal": ([P, 8], F32), "tiny": ([1, 8], F32),
            "Cmid": ([P, NF, 1024], F32),
        }[t]
        tap_dram[t] = nc.dram_tensor("tap_" + t, shp, dt, kind="ExternalOutput")

    led = Led()

    with ExitStack() as stx:
        e = stx.enter_context

        LDKEYS = ("xb0", "xb1", "xg0", "xg1", "w0", "w1",
                  "kv0", "kv1", "kv2", "kv3",
                  "wt0", "wt1", "w80", "w81", "w82", "w83", "cst", "cc2", "cc3")
        LD = {k: e(nc.semaphore("ld_" + k)) for k in LDKEYS}
        so = e(nc.semaphore("so"))
        pe = e(nc.semaphore("pe"))
        ve = e(nc.semaphore("ve"))
        ac = e(nc.semaphore("ac"))
        cc = e(nc.semaphore("cc"))
        SEMS = {"so": so, "pe": pe, "ve": ve, "ac": ac}
        for k in LDKEYS:
            SEMS["ld_" + k] = LD[k]

        sI = e(nc.sbuf_tensor("sI", [P, P], F32))
        sOne = e(nc.sbuf_tensor("sOne", [P, P], F32))
        sGN = e(nc.sbuf_tensor("sGN", [P, NF], F32))
        sGO = e(nc.sbuf_tensor("sGO", [P, NF], F32))
        SCAL = e(nc.sbuf_tensor("SCAL", [P, 8], F32))
        tiny = e(nc.sbuf_tensor("tiny", [1, 8], F32))
        amc = e(nc.sbuf_tensor("amc", [P, 2], F32))
        amcols = e(nc.sbuf_tensor("amcols", [P, 100], F32))
        A = e(nc.sbuf_tensor("A", [P, NH, RB], F32))
        B = e(nc.sbuf_tensor("B", [P, NF, RB], F32))
        C = e(nc.sbuf_tensor("C", [P, NF, 1024], F32))
        S = e(nc.sbuf_tensor("S", [P, 5, RB], F32))
        SQ = e(nc.sbuf_tensor("SQ", [P, 2, RB], F32))
        Smu_o = e(nc.sbuf_tensor("Smu_o", [P, 2, RB], F32))
        Sr12_o = e(nc.sbuf_tensor("Sr12_o", [P, 2, RB], F32))
        Smu3 = e(nc.sbuf_tensor("Smu3", [P, 2, RB], F32))
        Srho3 = e(nc.sbuf_tensor("Srho3", [P, 2, RB], F32))
        xbig = e(nc.sbuf_tensor("xbig", [P, 4, RB], F32))
        wst = e(nc.sbuf_tensor("wst", [P, 2, 512], F32))
        wkvst = e(nc.sbuf_tensor("wkvst", [P, 4, 256], F32))
        wtst = e(nc.sbuf_tensor("wtst", [P, 2, 512], BF16))
        w8st = e(nc.sbuf_tensor("w8st", [P, 3, 1024], FP8))
        qb1 = e(nc.sbuf_tensor("qb1", [P, 512], BF16))
        w8c = e(nc.sbuf_tensor("w8c", [P, 2, 512], FP8))

        att_stack = ExitStack()
        kT = att_stack.enter_context(nc.sbuf_tensor("kT", [P, 2048], F32))
        vv = att_stack.enter_context(nc.sbuf_tensor("vv", [P, NJ, P], F32))
        ptil = att_stack.enter_context(nc.sbuf_tensor("ptil", [P, 2, RB], F32))

        ps = [e(nc.psum_tensor(f"ps{i}", [P, 512], F32)) for i in range(8)]

        block = e(nc.Block())

        def on(engname, fn):
            getattr(block, engname)(fn)

        def dma_in(q, stream, out_ap, in_ap):
            q.dma_start(out=out_ap, in_=in_ap).then_inc(LD[stream], 16)
            return led.inc("ld_" + stream, 16)

        def dma_out(q, out_ap, in_ap):
            q.dma_start(out=out_ap, in_=in_ap).then_inc(so, 16)
            return led.inc("so", 16)

        def emit_barrier():
            bv = {s: led.val(s) for s in SEMS}

            def mk(eng):
                for nm, val in bv.items():
                    if val:
                        eng.wait_ge(SEMS[nm], val)
            for en in ("sync", "tensor", "vector", "scalar"):
                on(en, mk)

        def emit_rsqrt(vio, rout, epsb, t1):
            def dv0(v):
                v.tensor_single_scalar(vio, vio, float(epsb), ALU.add).then_inc(ve, 1)
            on("vector", dv0)
            v0 = led.inc("ve")

            def sc0(s):
                b0 = led.val("ac")
                s.wait_ge(ve, v0)
                s.activation(t1, vio, AF.Ln).then_inc(ac, 1)
                s.wait_ge(ac, b0 + 1)
                s.activation(rout, t1, AF.Exp, scale=-0.5).then_inc(ac, 1)
            on("scalar", sc0)
            led.inc("ac")
            a0 = led.inc("ac")

            def dv1(v):
                v.wait_ge(ac, a0)
                v.tensor_tensor(t1, rout, rout, ALU.mult)
                v.scalar_tensor_tensor(t1, t1, -0.5, vio, ALU.mult, ALU.mult)
                v.tensor_single_scalar(t1, t1, 1.5, ALU.add)
                v.tensor_tensor(rout, rout, t1, ALU.mult).then_inc(ve, 1)
            on("vector", dv1)
            return led.inc("ve")

        def emit_recip(vin, rout, t0, t1):
            def sc0(s):
                b0 = led.val("ac")
                s.wait_ge(ve, led.val("ve"))
                s.wait_ge(pe, led.val("pe"))
                s.activation(t0, vin, AF.Ln).then_inc(ac, 1)
                s.wait_ge(ac, b0 + 1)
                s.activation(rout, t0, AF.Exp, scale=-1.0).then_inc(ac, 1)
            on("scalar", sc0)
            led.inc("ac")
            a0 = led.inc("ac")

            def dv1(v):
                v.wait_ge(ac, a0)
                v.tensor_tensor(t1, vin, rout, ALU.mult)
                v.tensor_scalar(t1, t1, -1.0, 2.0, ALU.mult, ALU.add)
                v.tensor_tensor(rout, rout, t1, ALU.mult).then_inc(ve, 1)
            on("vector", dv1)
            return led.inc("ve")

        def emit_quant(v, src_ap, dst, rs2_col):
            return v.tensor_scalar(dst, src_ap, rs2_col, None, ALU.mult)

        def p0_sync(q):
            dma_in(q, "cst", sI[:, :], iden[:, :])
            dma_in(q, "cst", sGN[:, :], gN2[:, :])
            dma_in(q, "cst", sGO[:, :], gO2[:, :])
        on("sync", p0_sync)
        ld_cst = led.val("ld_cst")

        def p0_act(s):
            from concourse.hw_specs import get_activation_tables
            idx = None
            for i, fns in enumerate(get_activation_tables(nc.m.arch).values()):
                names = {f.name for f in fns}
                if {"Exp", "Ln", "Square"} <= names:
                    idx = i
                    break
            assert idx is not None
            s.add_instruction(mybir.InstLoadActFuncSet(
                name=nc.get_next_instruction_name(),
                act_func_set_id=idx, ins=[], outs=[]))
        on("scalar", p0_act)

        def p0_vec(v):
            v.memset(sOne[:, :], 1.0)
            v.memset(tiny[:, :], 0.0)
            v.memset(SCAL[:, :], 0.0)
            v.memset(amc[:, :], 0.0).then_inc(ve, 1)
        on("vector", p0_vec)
        led.inc("ve")

        pw_units = []
        wt_gate = {0: 0, 1: 0}
        w8c_gate = {0: 0, 1: 0}
        pw_state = {"scal": None}

        def mk_amax_unit(t, qq):
            def unit():
                slot = (t * 4 + qq) % 2
                g = wt_gate[slot]

                def ld_(q):
                    if g:
                        q.wait_ge(ve, g)
                    dma_in(q, "wt%d" % slot, wtst[:, slot, :],
                           WtT[t, :, qq * 512:(qq + 1) * 512])
                on("sync", ld_)
                t_ld = led.val("ld_wt%d" % slot)

                def red_(v):
                    v.wait_ge(LD["wt%d" % slot], t_ld)
                    v.reduce_max(amcols[:, t * 4 + qq:t * 4 + qq + 1],
                                 wtst[:, slot, :], AX.X,
                                 apply_absolute_value=True).then_inc(ve, 1)
                on("vector", red_)
                wt_gate[slot] = led.inc("ve")
            return unit

        def pw_combine_unit():
            ve_hop = led.val("ve")

            def pw_comb(v):
                v.wait_ge(ve, ve_hop)
                v.reduce_max(amc[:, 0:1], amcols[:, 0:64], AX.X).then_inc(ve, 1)
            on("vector", pw_comb)
            ve_am = led.inc("ve")

            def pw_tr(tn):
                tn.wait_ge(ve, ve_am)
                tn.wait_ge(LD["cst"], ld_cst)
                tn.transpose(ps[7][0:1, 0:P], amc[:, 0:1],
                             sI[:, :]).then_inc(pe, 1)
            on("tensor", pw_tr)
            pe_tr = led.inc("pe")

            def pw_sw(v):
                b0 = led.val("ve")
                v.wait_ge(pe, pe_tr)
                v.reduce_max(tiny[0:1, 0:1], ps[7][0:1, 0:P],
                             AX.X).then_inc(ve, 1)
                v.wait_ge(ve, b0 + 1)
                v.tensor_single_scalar(tiny[0:1, 0:1], tiny[0:1, 0:1], INV448,
                                       ALU.mult).then_inc(ve, 1)
                v.wait_ge(ve, b0 + 2)
                v.tensor_single_scalar(tiny[0:1, 0:1], tiny[0:1, 0:1], 1e-12,
                                       ALU.max).then_inc(ve, 1)
                v.wait_ge(ve, b0 + 3)
                v.reciprocal(amcols[0:1, 96:97], tiny[0:1, 0:1]).then_inc(ve, 1)
                v.wait_ge(ve, b0 + 4)
                v.tensor_single_scalar(amcols[0:1, 97:98], amcols[0:1, 96:97],
                                       0.5, ALU.mult).then_inc(ve, 1)
            on("vector", pw_sw)
            for _ in range(4):
                led.inc("ve")
            ve_sw = led.inc("ve")

            def pw_bc(tn):
                tn.wait_ge(ve, ve_sw)
                tn.matmul(ps[7][0:P, 0:2], sOne[0:1, 0:P], amcols[0:1, 96:98],
                          start=True, stop=True).then_inc(pe, 1)
            on("tensor", pw_bc)
            pe_bc = led.inc("pe")

            def pw_bc_ev(v):
                v.wait_ge(pe, pe_bc)
                v.tensor_copy(SCAL[:, 0:2], ps[7][0:P, 0:2]).then_inc(ve, 1)
            on("vector", pw_bc_ev)
            pw_state["scal"] = led.inc("ve")

        def mk_quant_unit(t, qq):
            def unit():
                slot = (t * 4 + qq) % 2
                g = wt_gate[slot]
                sog = w8c_gate[slot]

                def ld_(q):
                    if g:
                        q.wait_ge(ve, g)
                    dma_in(q, "wt%d" % slot, wtst[:, slot, :],
                           WtT[t, :, qq * 512:(qq + 1) * 512])
                on("sync", ld_)
                t_ld = led.val("ld_wt%d" % slot)

                def q_(v):
                    v.wait_ge(LD["wt%d" % slot], t_ld)
                    v.wait_ge(ve, pw_state["scal"])
                    if sog:
                        v.wait_ge(so, sog)
                    emit_quant(v, wtst[:, slot, :], w8c[:, slot, :],
                               SCAL[:, 1:2]).then_inc(ve, 1)
                on("vector", q_)
                wt_gate[slot] = ve_q = led.inc("ve")

                def st_(q):
                    q.wait_ge(ve, ve_q)
                    dma_out(q, w8d[qq, t // 2, :,
                                   (t % 2) * 512:(t % 2) * 512 + 512],
                            w8c[:, slot, :])
                on("sync", st_)
                w8c_gate[slot] = led.val("so")
            return unit

        for t in range(NF):
            for qq in range(4):
                pw_units.append(mk_amax_unit(t, qq))
        pw_units.append(pw_combine_unit)
        for t in range(NF):
            for qq in range(4):
                pw_units.append(mk_quant_unit(t, qq))

        def pump(n):
            for _ in range(n):
                if pw_units:
                    pw_units.pop(0)()

        xgate = [None, None]

        def xwait(q, sl):
            if xgate[sl] is not None:
                gp, ga, gv = xgate[sl]
                q.wait_ge(pe, gp)
                q.wait_ge(ac, ga)
                q.wait_ge(ve, gv)

        def xmark(sl):
            xgate[sl] = (led.val("pe"), led.val("ac"), led.val("ve"))

        for blk in range(NB):
            own = blk >= 2
            mu_dst = Smu_o[:, blk - 2, :] if own else Smu3[:, 0, :]
            r12_dst = Sr12_o[:, blk - 2, :] if own else Srho3[:, 0, :]

            for g2 in range(8):
                sl = g2 % 2

                def p1_ld(q, blk=blk, g2=g2, sl=sl):
                    xwait(q, sl)
                    dma_in(q, "xb%d" % sl, xbig[:, sl * 2:sl * 2 + 2, :],
                           xblk[blk, :, g2 * 2:(g2 + 1) * 2, :])
                on("sync", p1_ld)
                t_ld = led.val("ld_xb%d" % sl)
                pe_base = led.val("pe")

                def p1_sq(s, sl=sl, t_ld=t_ld, pe_base=pe_base):
                    s.wait_ge(LD["xb%d" % sl], t_ld)
                    s.wait_ge(pe, pe_base)
                    for i in range(2):
                        s.activation(SQ[:, i, :], xbig[:, sl * 2 + i, :],
                                     AF.Square).then_inc(ac, 1)
                on("scalar", p1_sq)
                ac_marks = [led.inc("ac") for _ in range(2)]

                ve_emit = led.val("ve")

                def p1_mm(tn, g2=g2, sl=sl, t_ld=t_ld, ac_marks=ac_marks,
                          ve_emit=ve_emit):
                    tn.wait_ge(LD["xb%d" % sl], t_ld)
                    if g2 == 0:
                        tn.wait_ge(ve, ve_emit)
                    for i in range(2):
                        fc = g2 * 2 + i
                        tn.matmul(ps[0][:, :], sOne[:, :], xbig[:, sl * 2 + i, :],
                                  start=(fc == 0), stop=(fc == NF - 1))
                        tn.wait_ge(ac, ac_marks[i])
                        tn.matmul(ps[1][:, :], sOne[:, :], SQ[:, i, :],
                                  start=(fc == 0),
                                  stop=(fc == NF - 1)).then_inc(pe, 1)
                on("tensor", p1_mm)
                for _ in range(2):
                    led.inc("pe")
                xmark(sl)
                pump(2)

            pe_stats = led.val("pe")

            def p1_sc(v, mu_dst=mu_dst, pe_stats=pe_stats):
                v.wait_ge(pe, pe_stats)
                v.tensor_scalar(mu_dst, ps[0][:, :], INV2048, None, ALU.mult)
                v.tensor_scalar(S[:, 1, :], ps[1][:, :], INV2048, None, ALU.mult)
                v.tensor_tensor(S[:, 2, :], mu_dst, mu_dst, ALU.mult)
                v.tensor_tensor(S[:, 2, :], S[:, 1, :], S[:, 2, :],
                                ALU.subtract)
                v.tensor_single_scalar(S[:, 4, :], S[:, 2, :], 1.0,
                                       ALU.mult).then_inc(ve, 1)
            on("vector", p1_sc)
            led.inc("ve")
            emit_rsqrt(S[:, 4, :], S[:, 3, :], EPS, S[:, 1, :])

            def p1_sc2(v):
                v.tensor_tensor(S[:, 1, :], S[:, 3, :], S[:, 3, :], ALU.mult)
                v.tensor_tensor(S[:, 1, :], S[:, 1, :], S[:, 2, :],
                                ALU.mult).then_inc(ve, 1)
            on("vector", p1_sc2)
            led.inc("ve")
            emit_rsqrt(S[:, 1, :], S[:, 4, :], EPS, S[:, 2, :])

            def p1_sc3(v, r12_dst=r12_dst):
                v.tensor_tensor(r12_dst, S[:, 3, :], S[:, 4, :],
                                ALU.mult).then_inc(ve, 1)
            on("vector", p1_sc3)
            ve_r12 = led.inc("ve")

            for g2 in range(8):
                sl = g2 % 2

                def p1_ld2(q, blk=blk, g2=g2, sl=sl):
                    xwait(q, sl)
                    dma_in(q, "xb%d" % sl, xbig[:, sl * 2:sl * 2 + 2, :],
                           xblk[blk, :, g2 * 2:(g2 + 1) * 2, :])
                    for i in range(2):
                        fc = g2 * 2 + i
                        dma_in(q, "kv%d" % (fc % 4), wkvst[:, fc % 4, :],
                               Wkv_t[fc, :, :])
                on("sync", p1_ld2)
                t_ld = led.val("ld_xb%d" % sl)
                t_kv = {i: led.val("ld_kv%d" % i) for i in range(4)}
                pe_base = led.val("pe")

                if blk == 2:
                    xn2dst = [B[:, g2 * 2, :], B[:, g2 * 2 + 1, :]]
                elif blk == 3:
                    xn2dst = [C[:, g2 * 2, RB:2 * RB], C[:, g2 * 2 + 1, RB:2 * RB]]
                else:
                    xn2dst = [SQ[:, 0, :], SQ[:, 1, :]]

                def p1_ap(v, g2=g2, sl=sl, t_ld=t_ld, ve_r12=ve_r12,
                          pe_base=pe_base, mu_dst=mu_dst, r12_dst=r12_dst,
                          xn2dst=xn2dst):
                    v.wait_ge(LD["xb%d" % sl], t_ld)
                    v.wait_ge(ve, ve_r12)
                    v.wait_ge(pe, pe_base)
                    for i in range(2):
                        fc = g2 * 2 + i
                        v.tensor_tensor(S[:, 0, :], xbig[:, sl * 2 + i, :],
                                        mu_dst, ALU.subtract)
                        v.scalar_tensor_tensor(
                            xn2dst[i], S[:, 0, :], sGN[:, fc:fc + 1],
                            r12_dst, ALU.mult, ALU.mult).then_inc(ve, 1)
                on("vector", p1_ap)
                ve_marks = [led.inc("ve") for _ in range(2)]

                def p1_kv(tn, g2=g2, t_kv=t_kv, ve_marks=ve_marks,
                          xn2dst=xn2dst):
                    for i in range(2):
                        fc = g2 * 2 + i
                        tn.wait_ge(LD["kv%d" % (fc % 4)], t_kv[fc % 4])
                        tn.wait_ge(ve, ve_marks[i])
                        tn.matmul(ps[2][:, :], wkvst[:, fc % 4, 0:128],
                                  xn2dst[i],
                                  start=(fc == 0), stop=(fc == NF - 1))
                        tn.matmul(ps[3][:, :], wkvst[:, fc % 4, 128:256],
                                  xn2dst[i],
                                  start=(fc == 0),
                                  stop=(fc == NF - 1)).then_inc(pe, 1)
                on("tensor", p1_kv)
                for _ in range(2):
                    led.inc("pe")
                xmark(sl)
                pump(2)

            pe_kv = led.val("pe")

            def p1_ev(v, blk=blk, pe_kv=pe_kv):
                v.wait_ge(pe, pe_kv)
                v.tensor_copy(kT[:, blk * RB:(blk + 1) * RB], ps[2][:, :])
                v.tensor_copy(SQ[:, 0, :], ps[3][:, :]).then_inc(ve, 1)
            on("vector", p1_ev)
            ve_ev = led.inc("ve")

            ve_trbase = led.val("ve")
            for jj in range(4):
                jc = blk * 4 + jj

                def p1_tr(tn, jj=jj, ve_ev=ve_ev, ve_trbase=ve_trbase):
                    tn.wait_ge(ve, ve_ev)
                    if jj >= 1:
                        tn.wait_ge(ve, ve_trbase + jj)
                    tn.transpose(ps[4][:, 0:128],
                                 SQ[:, 0, jj * 128:(jj + 1) * 128],
                                 sI[:, :]).then_inc(pe, 1)
                on("tensor", p1_tr)
                pe_t = led.inc("pe")

                def p1_trev(v, jc=jc, pe_t=pe_t):
                    v.wait_ge(pe, pe_t)
                    v.tensor_copy(vv[:, jc, :], ps[4][:, 0:128]).then_inc(ve, 1)
                on("vector", p1_trev)
                led.inc("ve")

        pump(len(pw_units))

        for blk in range(NB):
            pe_now = led.val("pe")
            ve_now = led.val("ve")

            def kl_sq(s, blk=blk, pe_now=pe_now, ve_now=ve_now):
                s.wait_ge(pe, pe_now)
                s.wait_ge(ve, ve_now)
                s.activation(SQ[:, 0, :], kT[:, blk * RB:(blk + 1) * RB],
                             AF.Square).then_inc(ac, 1)
            on("scalar", kl_sq)
            ac_m = led.inc("ac")

            def kl_mm(tn, ac_m=ac_m):
                tn.wait_ge(ac, ac_m)
                tn.matmul(ps[5][:, :], sOne[:, :], SQ[:, 0, :],
                          start=True, stop=True).then_inc(pe, 1)
            on("tensor", kl_mm)
            pe_m = led.inc("pe")

            def kl_ev(v, pe_m=pe_m):
                v.wait_ge(pe, pe_m)
                v.tensor_copy(S[:, 2, :], ps[5][:, :]).then_inc(ve, 1)
            on("vector", kl_ev)
            led.inc("ve")
            emit_rsqrt(S[:, 2, :], S[:, 3, :], L2EPS, S[:, 4, :])

            def kl_ap(v, blk=blk):
                v.tensor_single_scalar(S[:, 3, :], S[:, 3, :], 4.0, ALU.mult)
                v.tensor_tensor(kT[:, blk * RB:(blk + 1) * RB],
                                kT[:, blk * RB:(blk + 1) * RB], S[:, 3, :],
                                ALU.mult).then_inc(ve, 1)
            on("vector", kl_ap)
            led.inc("ve")

        emit_barrier()
        if "kT" in tap_dram:
            on("sync", lambda q: dma_out(q, tap_dram["kT"][:, :], kT[:, :]))
        if "v" in tap_dram:
            on("sync", lambda q: dma_out(q, tap_dram["v"][:, :, :], vv[:, :, :]))
        if "stat1" in tap_dram:
            def ts1(q):
                dma_out(q, tap_dram["stat1"][:, 0:2, :], Smu_o[:, :, :])
                dma_out(q, tap_dram["stat1"][:, 2:4, :], Sr12_o[:, :, :])
            on("sync", ts1)

        def emit_tail(rc, oblk, xfree=None):
            pe_st = led.val("pe")

            def p5_sc(v, pe_st=pe_st):
                v.wait_ge(pe, pe_st)
                v.tensor_scalar(S[:, 1, :], ps[4][:, :], INV2048, None,
                                ALU.mult)
                v.tensor_scalar(S[:, 2, :], ps[5][:, :], INV2048, None,
                                ALU.mult)
                v.tensor_tensor(S[:, 0, :], S[:, 1, :], S[:, 1, :], ALU.mult)
                v.tensor_tensor(S[:, 2, :], S[:, 2, :], S[:, 0, :],
                                ALU.subtract).then_inc(ve, 1)
            on("vector", p5_sc)
            sc_m = led.inc("ve")
            emit_rsqrt(S[:, 2, :], S[:, 3, :], EPS, S[:, 4, :])

            resid_marks = []
            for g2 in range(8):
                sl = g2 % 2

                def p5_xl(g, oblk=oblk, g2=g2, sl=sl, xfree=xfree):
                    xwait(g, sl)
                    if xfree and g2 < 2:
                        g.wait_ge(ve, xfree)
                    dma_in(g, "xg%d" % sl, xbig[:, sl * 2:sl * 2 + 2, :],
                           xblk[oblk, :, g2 * 2:(g2 + 1) * 2, :])
                on("gpsimd", p5_xl)
                t_ld = led.val("ld_xg%d" % sl)

                def p5_res(v, g2=g2, sl=sl, rc=rc, t_ld=t_ld):
                    v.wait_ge(LD["xg%d" % sl], t_ld)
                    for i in range(2):
                        fc = g2 * 2 + i
                        ctile = C[:, fc, rc * RB:(rc + 1) * RB]
                        v.tensor_tensor(S[:, 0, :], ctile, S[:, 1, :],
                                        ALU.subtract)
                        v.scalar_tensor_tensor(S[:, 0, :], S[:, 0, :],
                                               sGO[:, fc:fc + 1], S[:, 3, :],
                                               ALU.mult, ALU.mult)
                        ins = v.tensor_tensor(ctile, xbig[:, sl * 2 + i, :],
                                              S[:, 0, :], ALU.add)
                    ins.then_inc(ve, 1)
                on("vector", p5_res)
                resid_marks.append(led.inc("ve"))
                xmark(sl)

            pe6_base = led.val("pe")
            for fc in range(NF):
                def p6_sq(s, fc=fc, rc=rc, pe6_base=pe6_base,
                          rm=resid_marks[fc // 2]):
                    s.wait_ge(ve, rm)
                    s.wait_ge(pe, pe6_base + max(0, fc - 1))
                    s.activation(ptil[:, fc % 2, :],
                                 C[:, fc, rc * RB:(rc + 1) * RB],
                                 AF.Square).then_inc(ac, 1)
                on("scalar", p6_sq)
                ac_m = led.inc("ac")

                def p6_smm(tn, fc=fc, rc=rc, ac_m=ac_m, sc_m=sc_m,
                           rm=resid_marks[fc // 2]):
                    if fc == 0:
                        tn.wait_ge(ve, sc_m)
                    tn.wait_ge(ve, rm)
                    tn.matmul(ps[4][:, :], sOne[:, :],
                              C[:, fc, rc * RB:(rc + 1) * RB],
                              start=(fc == 0), stop=(fc == NF - 1))
                    tn.wait_ge(ac, ac_m)
                    tn.matmul(ps[5][:, :], sOne[:, :], ptil[:, fc % 2, :],
                              start=(fc == 0),
                              stop=(fc == NF - 1)).then_inc(pe, 1)
                on("tensor", p6_smm)
                led.inc("pe")

            pe_st6 = led.val("pe")

            def p6_sc(v, rc=rc, pe_st6=pe_st6):
                v.wait_ge(pe, pe_st6)
                v.tensor_scalar(Smu3[:, rc, :], ps[4][:, :], INV2048, None,
                                ALU.mult)
                v.tensor_scalar(S[:, 2, :], ps[5][:, :], INV2048, None,
                                ALU.mult)
                v.tensor_tensor(S[:, 0, :], Smu3[:, rc, :], Smu3[:, rc, :],
                                ALU.mult)
                v.tensor_tensor(S[:, 2, :], S[:, 2, :], S[:, 0, :],
                                ALU.subtract).then_inc(ve, 1)
            on("vector", p6_sc)
            led.inc("ve")
            emit_rsqrt(S[:, 2, :], Srho3[:, rc, :], EPS, S[:, 4, :])

            for fc in range(NF):
                def p6_am(v, fc=fc, rc=rc, last=(fc == NF - 1)):
                    v.tensor_tensor(S[:, 0, :],
                                    C[:, fc, rc * RB:(rc + 1) * RB],
                                    Smu3[:, rc, :], ALU.subtract)
                    v.tensor_tensor(S[:, 0, :], S[:, 0, :], Srho3[:, rc, :],
                                    ALU.mult)
                    col = 64 + rc * NF + fc
                    ins = v.reduce_max(amcols[:, col:col + 1], S[:, 0, :],
                                       AX.X, apply_absolute_value=True)
                    if last:
                        ins.then_inc(ve, 1)
                on("vector", p6_am)
            led.inc("ve")

        pending_tail = []
        xbig_free = {}
        for rc in range(2):
            oblk = 2 + rc


            pe_qend_prev = None
            ve_evq_prev = None
            for hg in range(4):
                pe_base = led.val("pe")
                ve_done_prev = ve_evq_prev if hg else led.val("ve")
                pe_wst_prev = pe_qend_prev if hg else pe_base

                for fc in range(NF):
                    def p3_ld(q, hg=hg, fc=fc, pe_base=pe_base,
                              pe_wst_prev=pe_wst_prev):
                        if fc >= 2:
                            q.wait_ge(pe, pe_base + fc - 1)
                        else:
                            q.wait_ge(pe, pe_wst_prev)
                        dma_in(q, "w%d" % (fc % 2), wst[:, fc % 2, :],
                               Wq_t[hg, fc, :, :])
                    on("sync", p3_ld)
                    t_ld = led.val("ld_w%d" % (fc % 2))

                    def p3_mm(tn, fc=fc, rc=rc, t_ld=t_ld,
                              ve_done_prev=ve_done_prev):
                        tn.wait_ge(LD["w%d" % (fc % 2)], t_ld)
                        if fc == 0:
                            tn.wait_ge(ve, ve_done_prev)
                        xn2src = (B[:, fc, :] if rc == 0
                                  else C[:, fc, RB:2 * RB])
                        for hh in range(4):
                            ins = tn.matmul(
                                ps[hh][:, :],
                                wst[:, fc % 2, hh * 128:(hh + 1) * 128],
                                xn2src, start=(fc == 0),
                                stop=(fc == NF - 1))
                        ins.then_inc(pe, 1)
                    on("tensor", p3_mm)
                    led.inc("pe")
                pe_qend_prev = led.val("pe")

                pe_q = led.val("pe")

                def p3_evq(v, hg=hg, pe_q=pe_q):
                    v.wait_ge(pe, pe_q)
                    for hh in range(4):
                        ins = v.tensor_copy(A[:, hg * 4 + hh, :], ps[hh][:, :])
                    ins.then_inc(ve, 1)
                on("vector", p3_evq)
                ve_evq_prev = led.inc("ve")

                if rc == 0:
                    lad = [(xbig[:, 0:2, :], xbig[:, 2:4, :], 0),
                           (Smu_o[:, :, :], Sr12_o[:, :, :], 2)]
                    banks = (4, 5, 6, 7)
                else:
                    lad = [(B[:, 0:2, :], B[:, 2:4, :], 0),
                           (Smu_o[:, :, :], Sr12_o[:, :, :], 2)]
                    banks = (6, 7, 6, 7)
                ve_evq = led.val("ve")
                pe_now3 = led.val("pe")
                ve_lfree = led.val("ve")
                mm_marks = []
                v0m = []
                for li, (vio, rout, hb) in enumerate(lad):
                    for j in range(2):
                        hh = li * 2 + j
                        h = hg * 4 + hh

                        def p3_sq(s, h=h, hh=hh, ve_evq=ve_evq,
                                  pe_now3=pe_now3, mm_marks=tuple(mm_marks)):
                            if hh == 0:
                                s.wait_ge(ve, ve_evq)
                                s.wait_ge(pe, pe_now3)
                            if hh >= 2:
                                s.wait_ge(pe, mm_marks[hh - 2])
                            s.activation(SQ[:, hh % 2, :], A[:, h, :],
                                         AF.Square).then_inc(ac, 1)
                        on("scalar", p3_sq)
                        ac_m = led.inc("ac")

                        def p3_lmm(tn, ac_m=ac_m, hh=hh, bank=banks[hh],
                                   ve_lfree=ve_lfree,
                                   ev1=(v0m[0] if v0m else None)):
                            tn.wait_ge(ac, ac_m)
                            if hh == 0:
                                tn.wait_ge(ve, ve_lfree)
                            if hh == 2 and ev1 is not None and bank == banks[0]:
                                tn.wait_ge(ve, ev1)
                            tn.matmul(ps[bank][:, :], sOne[:, :],
                                      SQ[:, hh % 2, :],
                                      start=True, stop=True).then_inc(pe, 1)
                        on("tensor", p3_lmm)
                        mm_marks.append(led.inc("pe"))

                    def p3_ev(v, vio=vio, b0=banks[li * 2], b1=banks[li * 2 + 1],
                              pe_l=mm_marks[-1]):
                        v.wait_ge(pe, pe_l)
                        v.tensor_copy(vio[:, 0, :], ps[b0][:, :])
                        v.tensor_copy(vio[:, 1, :], ps[b1][:, :])
                        v.tensor_single_scalar(vio, vio,
                                               float(L2EPS / 256.0),
                                               ALU.add).then_inc(ve, 1)
                    on("vector", p3_ev)
                    v0m.append(led.inc("ve"))

                a0m = []
                for li, (vio, rout, hb) in enumerate(lad):
                    def p3_ln(s, vio=vio, rout=rout, v0=v0m[li]):
                        b0 = led.val("ac")
                        s.wait_ge(ve, v0)
                        s.activation(rout, vio, AF.Ln).then_inc(ac, 1)
                        s.wait_ge(ac, b0 + 1)
                        s.activation(rout, rout, AF.Exp,
                                     scale=-0.5).then_inc(ac, 1)
                    on("scalar", p3_ln)
                    led.inc("ac")
                    a0m.append(led.inc("ac"))

                for li, (vio, rout, hb) in enumerate(lad):
                    def p3_nw(v, vio=vio, rout=rout, hb=hb, hg=hg,
                              a0=a0m[li]):
                        v.wait_ge(ac, a0)
                        v.tensor_tensor(vio, vio, rout, ALU.mult)
                        v.tensor_tensor(vio, vio, rout, ALU.mult)
                        v.tensor_scalar(vio, vio, -0.5, 1.5,
                                        ALU.mult, ALU.add)
                        v.tensor_tensor(rout, rout, vio, ALU.mult)
                        v.tensor_single_scalar(rout, rout, 4.0, ALU.mult)
                        for hh in range(2):
                            h = hg * 4 + hb + hh
                            ins = v.tensor_tensor(A[:, h, :], A[:, h, :],
                                                  rout[:, hh, :], ALU.mult)
                        ins.then_inc(ve, 1)
                    on("vector", p3_nw)
                    led.inc("ve")

            if rc == 0:
                xbig_free[0] = led.val("ve")
            elif pending_tail:
                pending_tail.pop(0)()

            emit_barrier()
            if "qn" in tap_dram:
                def tqn(q, rc=rc):
                    dma_out(q, tap_dram["qn"][:, :, rc * RB:(rc + 1) * RB],
                            A[:, :, :])
                on("sync", tqn)

            for h in range(NH):
                pe_base = led.val("pe")
                ac_base = led.val("ac")
                ve_att_prev = led.val("ve")

                plan = ["sim0", "sim1"]
                for jc in range(NJ):
                    plan.append("sa%d" % jc)
                    if jc + 2 < NJ:
                        plan.append("sim%d" % (jc + 2))
                sim_mark, av_mark = {}, {}
                m = pe_base
                for stp in plan:
                    m += 1
                    if stp.startswith("sim"):
                        sim_mark[int(stp[3:])] = m
                    else:
                        av_mark[int(stp[2:])] = m

                def p4_pe(tn, h=h, pe_base=pe_base, ac_base=ac_base,
                          ve_att_prev=ve_att_prev, plan=plan):
                    for stp in plan:
                        if stp.startswith("sim"):
                            jc = int(stp[3:])
                            if jc < 2:
                                tn.wait_ge(ac, ac_base)
                            else:
                                tn.wait_ge(ac, ac_base + jc - 1)
                            tn.matmul(ps[jc % 2][:, :],
                                      kT[:, jc * 128:(jc + 1) * 128],
                                      A[:, h, :], start=True,
                                      stop=True).then_inc(pe, 1)
                        else:
                            jc = int(stp[2:])
                            if jc == 0:
                                tn.wait_ge(ve, ve_att_prev)
                            tn.wait_ge(ac, ac_base + jc + 1)
                            tn.matmul(ps[3][:, :], sOne[:, :],
                                      ptil[:, jc % 2, :],
                                      start=(jc == 0), stop=(jc == NJ - 1))
                            tn.matmul(ps[2][:, :], vv[:, jc, :],
                                      ptil[:, jc % 2, :],
                                      start=(jc == 0),
                                      stop=(jc == NJ - 1)).then_inc(pe, 1)
                on("tensor", p4_pe)
                for _ in range(2 * NJ):
                    led.inc("pe")
                assert led.val("pe") == m

                def p4_exp(s, sim_mark=sim_mark, av_mark=av_mark):
                    for jc in range(NJ):
                        s.wait_ge(pe, sim_mark[jc])
                        if jc >= 2:
                            s.wait_ge(pe, av_mark[jc - 2])
                        s.activation(ptil[:, jc % 2, :], ps[jc % 2][:, :],
                                     AF.Exp).then_inc(ac, 1)
                on("scalar", p4_exp)
                exp_marks = [led.inc("ac") for _ in range(NJ)]
                assert exp_marks[-1] == ac_base + NJ

                pe_last = av_mark[NJ - 1]

                def p4_cp(v, pe_last=pe_last):
                    v.wait_ge(pe, pe_last)
                    v.tensor_copy(S[:, 2, :], ps[3][:, :]).then_inc(ve, 1)
                on("vector", p4_cp)
                led.inc("ve")
                emit_recip(S[:, 2, :], S[:, 1, :], S[:, 4, :], S[:, 0, :])

                def p4_out(v, h=h):
                    v.tensor_tensor(B[:, h, :], ps[2][:, :], S[:, 1, :],
                                    ALU.mult).then_inc(ve, 1)
                on("vector", p4_out)
                led.inc("ve")

            emit_barrier()
            if "att" in tap_dram:
                def tat(q, rc=rc):
                    dma_out(q, tap_dram["att"][:, :, rc * RB:(rc + 1) * RB],
                            B[:, :, :])
                on("sync", tat)

            ve_p5start = led.val("ve")
            sqg = {}
            for fg in range(4):
                pe_base = led.val("pe")
                ve_prev = led.val("ve")
                for hdc in range(NF):
                    def p5_ld(q, fg=fg, hdc=hdc, pe_base=pe_base,
                              ve_prev=ve_prev):
                        if hdc >= 2:
                            q.wait_ge(pe, pe_base + hdc - 1)
                        if hdc == 0:
                            q.wait_ge(ve, ve_prev)
                        dma_in(q, "w%d" % (hdc % 2), wst[:, hdc % 2, :],
                               Wo_t[fg, hdc, :, :])
                    on("sync", p5_ld)
                    t_ld = led.val("ld_w%d" % (hdc % 2))

                    def p5_mm(tn, hdc=hdc, t_ld=t_ld):
                        tn.wait_ge(LD["w%d" % (hdc % 2)], t_ld)
                        for ff in range(4):
                            ins = tn.matmul(
                                ps[ff][:, :],
                                wst[:, hdc % 2, ff * 128:(ff + 1) * 128],
                                B[:, hdc, :], start=(hdc == 0),
                                stop=(hdc == NF - 1))
                        ins.then_inc(pe, 1)
                    on("tensor", p5_mm)
                    led.inc("pe")

                pe_fg = led.val("pe")

                def p5_ev(v, fg=fg, rc=rc, pe_fg=pe_fg):
                    v.wait_ge(pe, pe_fg)
                    for ff in range(4):
                        ins = v.tensor_copy(
                            C[:, fg * 4 + ff, rc * RB:(rc + 1) * RB],
                            ps[ff][:, :])
                    ins.then_inc(ve, 1)
                on("vector", p5_ev)
                ev_m = led.inc("ve")

                for ff in range(4):
                    fc = fg * 4 + ff

                    def p5_sq(s, fc=fc, rc=rc, ev_m=ev_m, g=sqg.get(fc - 2)):
                        s.wait_ge(ve, ev_m)
                        if g:
                            s.wait_ge(pe, g)
                        s.activation(SQ[:, fc % 2, :],
                                     C[:, fc, rc * RB:(rc + 1) * RB],
                                     AF.Square).then_inc(ac, 1)
                    on("scalar", p5_sq)
                    ac_m = led.inc("ac")

                    def p5_smm(tn, fc=fc, rc=rc, ac_m=ac_m, ev_m=ev_m,
                               ve_p5start=ve_p5start):
                        if fc == 0:
                            tn.wait_ge(ve, ve_p5start)
                        tn.wait_ge(ve, ev_m)
                        tn.matmul(ps[4][:, :], sOne[:, :],
                                  C[:, fc, rc * RB:(rc + 1) * RB],
                                  start=(fc == 0), stop=(fc == NF - 1))
                        tn.wait_ge(ac, ac_m)
                        tn.matmul(ps[5][:, :], sOne[:, :], SQ[:, fc % 2, :],
                                  start=(fc == 0),
                                  stop=(fc == NF - 1)).then_inc(pe, 1)
                    on("tensor", p5_smm)
                    sqg[fc] = led.inc("pe")

            if rc == 0:
                pending_tail.append(
                    lambda oblk=oblk: emit_tail(0, oblk, xbig_free[0]))
            else:
                emit_tail(1, oblk)

            emit_barrier()

        if "Cmid" in tap_dram:
            on("sync",
               lambda q: dma_out(q, tap_dram["Cmid"][:, :, :], C[:, :, :]))

        att_stack.close()
        xq = e(nc.sbuf_tensor("xq", [P, 2, NF, RB], FP8))

        ve_hop2 = led.val("ve")

        def p6_comb(v, ve_hop2=ve_hop2):
            v.wait_ge(ve, ve_hop2)
            v.reduce_max(amc[:, 1:2], amcols[:, 64:96], AX.X).then_inc(ve, 1)
        on("vector", p6_comb)
        led.inc("ve")
        ve_am2 = led.val("ve")

        def p6_tr(tn, ve_am2=ve_am2):
            tn.wait_ge(ve, ve_am2)
            tn.transpose(ps[7][0:1, 0:P], amc[:, 1:2], sI[:, :]).then_inc(pe, 1)
        on("tensor", p6_tr)
        pe_tr2 = led.inc("pe")

        def p6_loc(v, pe_tr2=pe_tr2):
            v.wait_ge(pe, pe_tr2)
            v.reduce_max(tiny[0:1, 1:2], ps[7][0:1, 0:P], AX.X).then_inc(ve, 1)
        on("vector", p6_loc)
        ve_loc = led.inc("ve")

        def p6_ccout(q, ve_loc=ve_loc):
            q.wait_ge(ve, ve_loc)
            dma_in(q, "cc2", ccin[:, :], tiny[0:1, 1:2])
        on("sync", p6_ccout)
        t_ldcc = led.val("ld_cc2")

        def p6_cc(g, t_ldcc=t_ldcc):
            g.wait_ge(LD["cc2"], t_ldcc)
            g.collective_compute(
                "AllReduce", ALU.max, replica_groups=[list(range(8))],
                ins=[ccin.ap().opt()], outs=[ccout.ap().opt()]).then_inc(cc, 1)
        on("gpsimd", p6_cc)
        led.inc("cc")

        def p6_ccin(g):
            g.wait_ge(cc, 1)
            dma_in(g, "cc3", tiny[0:1, 2:3], ccout[:, :])
        on("gpsimd", p6_ccin)
        t_ldg = led.val("ld_cc3")

        def p6_scale(v, t_ldg=t_ldg):
            b0 = led.val("ve")
            v.wait_ge(LD["cc3"], t_ldg)
            v.tensor_single_scalar(tiny[0:1, 3:4], tiny[0:1, 2:3], INV448,
                                   ALU.mult).then_inc(ve, 1)
            v.wait_ge(ve, b0 + 1)
            v.tensor_single_scalar(tiny[0:1, 3:4], tiny[0:1, 3:4], 1e-12,
                                   ALU.max).then_inc(ve, 1)
            v.wait_ge(ve, b0 + 2)
            v.reciprocal(S[0:1, 0, 0:1], tiny[0:1, 3:4]).then_inc(ve, 1)
            v.wait_ge(ve, b0 + 3)
            v.tensor_single_scalar(S[0:1, 0, 1:2], S[0:1, 0, 0:1], 0.5,
                                   ALU.mult).then_inc(ve, 1)
            v.wait_ge(ve, b0 + 4)
            v.tensor_tensor(S[0:1, 0, 2:3], tiny[0:1, 3:4], tiny[0:1, 0:1],
                            ALU.mult).then_inc(ve, 1)
            v.wait_ge(ve, b0 + 5)
            v.tensor_single_scalar(S[0:1, 0, 2:3], S[0:1, 0, 2:3], 4.0,
                                   ALU.mult).then_inc(ve, 1)
        on("vector", p6_scale)
        for _ in range(5):
            led.inc("ve")
        ve_row = led.inc("ve")

        def p6_bc(tn, ve_row=ve_row):
            tn.wait_ge(ve, ve_row)
            tn.matmul(ps[7][0:P, 0:3], sOne[0:1, 0:P], S[0:1, 0, 0:3],
                      start=True, stop=True).then_inc(pe, 1)
        on("tensor", p6_bc)
        pe_bc2 = led.inc("pe")

        def p6_bc_ev(v, pe_bc2=pe_bc2):
            v.wait_ge(pe, pe_bc2)
            v.tensor_copy(SCAL[:, 2:5], ps[7][0:P, 0:3]).then_inc(ve, 1)
        on("vector", p6_bc_ev)
        ve_scal_x = led.inc("ve")

        if "tiny" in tap_dram:
            on("sync", lambda q: dma_out(q, tap_dram["tiny"][:, :], tiny[:, :]))
        if "scal" in tap_dram:
            on("sync", lambda q: dma_out(q, tap_dram["scal"][:, :], SCAL[:, :]))

        ve_xq_rc = [None, None]
        for rc in range(2):
            for fc in range(NF):
                def p7_q(v, fc=fc, rc=rc, last=(fc == NF - 1)):
                    if rc == 0 and fc == 0:
                        v.wait_ge(ve, ve_scal_x)
                    v.tensor_tensor(S[:, 0, :], C[:, fc, rc * RB:(rc + 1) * RB],
                                    Smu3[:, rc, :], ALU.subtract)
                    v.tensor_tensor(S[:, 0, :], S[:, 0, :], Srho3[:, rc, :],
                                    ALU.mult)
                    ins = emit_quant(v, S[:, 0, :], xq[:, rc, fc, :],
                                     SCAL[:, 3:4])
                    if last:
                        ins.then_inc(ve, 1)
                on("vector", p7_q)
            ve_xq_rc[rc] = led.inc("ve")

            if "xq" in tap_dram:
                def txq(q, rc=rc, ve_xq=ve_xq_rc[rc]):
                    q.wait_ge(ve, ve_xq)
                    dma_out(q, tap_dram["xq"][:, :, rc * RB:(rc + 1) * RB],
                            xq[:, rc, :, :])
                on("sync", txq)

        w8_gate = {0: 0, 1: 0, 2: 0}
        fin_gate = {0: 0, 1: 0}
        for rc in range(2):

            for og in range(4):
                gidx = rc * 4 + og
                bb = 4 * (gidx % 2)
                pe_base = led.val("pe")
                ve_xq = ve_xq_rc[rc]
                for pr in range(NF // 2):
                    slot = (gidx * 8 + pr) % 3
                    g = w8_gate[slot]

                    def p7_ld(q, og=og, pr=pr, slot=slot, g=g):
                        if g:
                            q.wait_ge(pe, g)
                        dma_in(q, "w8%d" % slot, w8st[:, slot, :],
                               w8d[og, pr, :, :])
                    on("sync", p7_ld)
                    t_ld = led.val("ld_w8%d" % slot)

                    ve_bank = fin_gate[gidx % 2]

                    def p7_mm(tn, pr=pr, rc=rc, slot=slot, bb=bb, t_ld=t_ld,
                              ve_bank=ve_bank, ve_xq=ve_xq):
                        tn.wait_ge(LD["w8%d" % slot], t_ld)
                        if pr == 0:
                            tn.wait_ge(ve, ve_xq)
                            if ve_bank:
                                tn.wait_ge(ve, ve_bank)
                        for i in range(2):
                            fc = pr * 2 + i
                            for oi in range(4):
                                ins = tn.matmul(
                                    ps[bb + oi][:, :],
                                    w8st[:, slot,
                                         i * 512 + oi * 128:
                                         i * 512 + (oi + 1) * 128],
                                    xq[:, rc, fc, :], start=(fc == 0),
                                    stop=(fc == NF - 1))
                            ins.then_inc(pe, 1)
                    on("tensor", p7_mm)
                    led.inc("pe")
                    w8_gate[slot] = led.inc("pe")

                pe_og = led.val("pe")

                def p7_fin(v, og=og, rc=rc, bb=bb, pe_og=pe_og):
                    v.wait_ge(pe, pe_og)
                    for oi in range(4):
                        o = og * 4 + oi
                        ctile = C[:, o, rc * RB:(rc + 1) * RB]
                        v.tensor_scalar(qb1[:, :], ps[bb + oi][:, :],
                                        SCAL[:, 4:5], None, ALU.mult)
                        ins = v.tensor_tensor(ctile, ctile, qb1[:, :], ALU.add)
                    ins.then_inc(ve, 1)
                on("vector", p7_fin)
                ve_fin = led.inc("ve")
                fin_gate[gidx % 2] = ve_fin

                def p7_st(q, og=og, rc=rc, ve_fin=ve_fin):
                    q.wait_ge(ve, ve_fin)
                    for oi in range(4):
                        o = og * 4 + oi
                        dma_out(q, outT[o, :, rc * RB:(rc + 1) * RB],
                                C[:, o, rc * RB:(rc + 1) * RB])
                on("scalar", p7_st)

        if "C" in tap_dram:
            on("sync", lambda q: dma_out(q, tap_dram["C"][:, :, :], C[:, :, :]))

        so_total = led.val("so")

        def fin(q, so_total=so_total):
            q.wait_ge(so, so_total)
        on("sync", fin)

    return nc


def _layout_inputs(L0_attn_x, L0_attn_g_norm, L0_attn_Wq, L0_attn_Wkv,
                   L0_attn_Wout, L0_attn_g_out, L0_ffn_weight):
    x = np.ascontiguousarray(np.asarray(L0_attn_x, dtype=np.float32))
    Wq = np.ascontiguousarray(np.asarray(L0_attn_Wq, dtype=np.float32))
    Wkv = np.ascontiguousarray(np.asarray(L0_attn_Wkv, dtype=np.float32))
    Wout = np.ascontiguousarray(np.asarray(L0_attn_Wout, dtype=np.float32))
    Wt = np.asarray(L0_ffn_weight)
    gN = np.ascontiguousarray(np.asarray(L0_attn_g_norm, dtype=np.float32))
    gO = np.ascontiguousarray(np.asarray(L0_attn_g_out, dtype=np.float32))

    Wq_t = np.ascontiguousarray(Wq.reshape(NF, P, 4, 512).transpose(2, 0, 1, 3))
    Wkv_t = np.ascontiguousarray(Wkv.reshape(NF, P, 256))
    Wo_t = np.ascontiguousarray(Wout.reshape(NF, P, 4, 512).transpose(2, 0, 1, 3))
    WtT = np.ascontiguousarray(Wt.T).reshape(NF, P, 2048)
    gN2 = np.ascontiguousarray(gN.reshape(NF, P).T)
    gO2 = np.ascontiguousarray(gO.reshape(NF, P).T)
    iden = np.eye(P, dtype=np.float32)

    in_maps = []
    for c in range(8):
        b, half = c // 2, c % 2
        order = [2 * (1 - half), 2 * (1 - half) + 1, 2 * half, 2 * half + 1]
        xb = np.empty((NB, P, NF, RB), dtype=np.float32)
        for i, bi in enumerate(order):
            blkd = x[b, bi * RB:(bi + 1) * RB, :]
            xb[i] = blkd.T.reshape(NF, P, RB).transpose(1, 0, 2)
        in_maps.append(dict(xblk=xb, Wq_t=Wq_t, Wkv_t=Wkv_t, Wo_t=Wo_t,
                            WtT=WtT, gN2=gN2, gO2=gO2, iden=iden))
    return in_maps


def kernel(L0_attn_x, L0_attn_g_norm, L0_attn_Wq, L0_attn_Wkv, L0_attn_Wout,
           L0_attn_g_out, L0_ffn_weight):
    in_maps = _layout_inputs(L0_attn_x, L0_attn_g_norm, L0_attn_Wq,
                             L0_attn_Wkv, L0_attn_Wout, L0_attn_g_out,
                             L0_ffn_weight)
    nc = build_program(taps=tuple(_TAPS))
    res = run_bass_kernel_spmd(nc, in_maps, core_ids=list(range(8)))
    out = np.empty((4, 2048, 2048), dtype=np.float32)
    for c in range(8):
        b, half = c // 2, c % 2
        o = np.asarray(res.results[c]["outT"])
        out[b, half * 1024:(half + 1) * 1024, :] = (
            o.transpose(2, 0, 1).reshape(1024, 2048))
    if _TAPS:
        kernel._last_results = res
    return out


# version 23
# speedup vs baseline: 1.0613x; 1.0390x over previous
from contextlib import ExitStack

import numpy as np
import ml_dtypes
import concourse.bass as bass
import concourse.mybir as mybir
from concourse.bass_utils import run_bass_kernel_spmd

F32 = mybir.dt.float32
BF16 = mybir.dt.bfloat16
FP8 = mybir.dt.float8e4
U8 = mybir.dt.uint8
AF = mybir.ActivationFunctionType
ALU = mybir.AluOpType
AX = mybir.AxisListType

P = 128
NF = 16
NH = 16
NJ = 16
NB = 4
RB = 512
EPS = 1e-5
L2EPS = 1e-12
INV2048 = 1.0 / 2048.0
INV448 = 1.0 / 448.0

_TAPS = []


class Led:
    def __init__(self):
        self.n = {}

    def inc(self, s, k=1):
        self.n[s] = self.n.get(s, 0) + k
        return self.n[s]

    def val(self, s):
        return self.n.get(s, 0)


def build_program(taps=()):
    nc = bass.Bass("TRN2", target_bir_lowering=False, detect_race_conditions=False)

    xblk = nc.dram_tensor("xblk", [NB, P, NF, RB], F32, kind="ExternalInput")
    Wq_t = nc.dram_tensor("Wq_t", [4, NF, P, 512], F32, kind="ExternalInput")
    Wkv_t = nc.dram_tensor("Wkv_t", [NF, P, 256], F32, kind="ExternalInput")
    Wo_t = nc.dram_tensor("Wo_t", [4, NF, P, 512], F32, kind="ExternalInput")
    WtT = nc.dram_tensor("WtT", [NF, P, 2048], BF16, kind="ExternalInput")
    gN2 = nc.dram_tensor("gN2", [P, NF], F32, kind="ExternalInput")
    gO2 = nc.dram_tensor("gO2", [P, NF], F32, kind="ExternalInput")
    iden = nc.dram_tensor("iden", [P, P], F32, kind="ExternalInput")

    outT = nc.dram_tensor("outT", [NF, P, 1024], F32, kind="ExternalOutput")

    w8d = nc.dram_tensor("w8d", [4, NF // 2, P, 1024], FP8)
    ccin = nc.dram_tensor("ccin", [1, 1], F32)
    ccout = nc.dram_tensor("ccout", [1, 1], F32)
    agin = nc.dram_tensor("agin", [P, 2048], F32)
    agout = nc.dram_tensor("agout", [2, P, 2048], F32)

    tap_dram = {}
    for t in taps:
        shp, dt = {
            "kT": ([P, 2048], F32), "v": ([P, NJ, P], F32),
            "qn": ([P, NH, 2 * RB], F32), "att": ([P, NF, 2 * RB], F32),
            "C": ([P, NF, 1024], F32), "xq": ([P, NF, 2 * RB], BF16),
            "stat1": ([P, 8, RB], F32), "w8chk": ([4, NF, P, 512], BF16),
            "scal": ([P, 8], F32), "tiny": ([1, 8], F32),
            "Cmid": ([P, NF, 1024], F32),
        }[t]
        tap_dram[t] = nc.dram_tensor("tap_" + t, shp, dt, kind="ExternalOutput")

    led = Led()

    with ExitStack() as stx:
        e = stx.enter_context

        LDKEYS = ("xb0", "xb1", "xg0", "xg1", "w0", "w1",
                  "kv0", "kv1", "kv2", "kv3",
                  "wt0", "wt1", "w80", "w81", "w82", "w83", "cst", "cc2",
                  "cc3", "agp", "agf")
        LD = {k: e(nc.semaphore("ld_" + k)) for k in LDKEYS}
        so = e(nc.semaphore("so"))
        pe = e(nc.semaphore("pe"))
        ve = e(nc.semaphore("ve"))
        ac = e(nc.semaphore("ac"))
        cc = e(nc.semaphore("cc"))
        SEMS = {"so": so, "pe": pe, "ve": ve, "ac": ac}
        for k in LDKEYS:
            SEMS["ld_" + k] = LD[k]

        sI = e(nc.sbuf_tensor("sI", [P, P], F32))
        sOne = e(nc.sbuf_tensor("sOne", [P, P], F32))
        sGN = e(nc.sbuf_tensor("sGN", [P, NF], F32))
        sGO = e(nc.sbuf_tensor("sGO", [P, NF], F32))
        SCAL = e(nc.sbuf_tensor("SCAL", [P, 8], F32))
        tiny = e(nc.sbuf_tensor("tiny", [1, 8], F32))
        amc = e(nc.sbuf_tensor("amc", [P, 2], F32))
        amcols = e(nc.sbuf_tensor("amcols", [P, 100], F32))
        A = e(nc.sbuf_tensor("A", [P, NH, RB], F32))
        B = e(nc.sbuf_tensor("B", [P, NF, RB], F32))
        C = e(nc.sbuf_tensor("C", [P, NF, 1024], F32))
        S = e(nc.sbuf_tensor("S", [P, 5, RB], F32))
        SQ = e(nc.sbuf_tensor("SQ", [P, 2, RB], F32))
        Smu_o = e(nc.sbuf_tensor("Smu_o", [P, 2, RB], F32))
        Sr12_o = e(nc.sbuf_tensor("Sr12_o", [P, 2, RB], F32))
        Smu3 = e(nc.sbuf_tensor("Smu3", [P, 2, RB], F32))
        Srho3 = e(nc.sbuf_tensor("Srho3", [P, 2, RB], F32))
        xbig = e(nc.sbuf_tensor("xbig", [P, 4, RB], F32))
        wst = e(nc.sbuf_tensor("wst", [P, 2, 512], F32))
        wkvst = e(nc.sbuf_tensor("wkvst", [P, 4, 256], F32))
        wtst = e(nc.sbuf_tensor("wtst", [P, 2, 512], BF16))
        w8st = e(nc.sbuf_tensor("w8st", [P, 3, 1024], FP8))
        qb1 = e(nc.sbuf_tensor("qb1", [P, 512], BF16))
        w8c = e(nc.sbuf_tensor("w8c", [P, 2, 512], FP8))

        att_stack = ExitStack()
        kT = att_stack.enter_context(nc.sbuf_tensor("kT", [P, 2048], F32))
        vv = att_stack.enter_context(nc.sbuf_tensor("vv", [P, NJ, P], F32))
        ptil = att_stack.enter_context(nc.sbuf_tensor("ptil", [P, 2, RB], F32))

        ps = [e(nc.psum_tensor(f"ps{i}", [P, 512], F32)) for i in range(8)]

        block = e(nc.Block())

        def on(engname, fn):
            getattr(block, engname)(fn)

        def dma_in(q, stream, out_ap, in_ap):
            q.dma_start(out=out_ap, in_=in_ap).then_inc(LD[stream], 16)
            return led.inc("ld_" + stream, 16)

        def dma_out(q, out_ap, in_ap):
            q.dma_start(out=out_ap, in_=in_ap).then_inc(so, 16)
            return led.inc("so", 16)

        def emit_barrier():
            bv = {s: led.val(s) for s in SEMS}

            def mk(eng):
                for nm, val in bv.items():
                    if val:
                        eng.wait_ge(SEMS[nm], val)
            for en in ("sync", "tensor", "vector", "scalar"):
                on(en, mk)

        def emit_rsqrt(vio, rout, epsb, t1):
            def dv0(v):
                v.tensor_single_scalar(vio, vio, float(epsb), ALU.add).then_inc(ve, 1)
            on("vector", dv0)
            v0 = led.inc("ve")

            def sc0(s):
                b0 = led.val("ac")
                s.wait_ge(ve, v0)
                s.activation(t1, vio, AF.Ln).then_inc(ac, 1)
                s.wait_ge(ac, b0 + 1)
                s.activation(rout, t1, AF.Exp, scale=-0.5).then_inc(ac, 1)
            on("scalar", sc0)
            led.inc("ac")
            a0 = led.inc("ac")

            def dv1(v):
                v.wait_ge(ac, a0)
                v.tensor_tensor(t1, rout, rout, ALU.mult)
                v.scalar_tensor_tensor(t1, t1, -0.5, vio, ALU.mult, ALU.mult)
                v.tensor_single_scalar(t1, t1, 1.5, ALU.add)
                v.tensor_tensor(rout, rout, t1, ALU.mult).then_inc(ve, 1)
            on("vector", dv1)
            return led.inc("ve")

        def emit_recip(vin, rout, t0, t1):
            def sc0(s):
                b0 = led.val("ac")
                s.wait_ge(ve, led.val("ve"))
                s.wait_ge(pe, led.val("pe"))
                s.activation(t0, vin, AF.Ln).then_inc(ac, 1)
                s.wait_ge(ac, b0 + 1)
                s.activation(rout, t0, AF.Exp, scale=-1.0).then_inc(ac, 1)
            on("scalar", sc0)
            led.inc("ac")
            a0 = led.inc("ac")

            def dv1(v):
                v.wait_ge(ac, a0)
                v.tensor_tensor(t1, vin, rout, ALU.mult)
                v.tensor_scalar(t1, t1, -1.0, 2.0, ALU.mult, ALU.add)
                v.tensor_tensor(rout, rout, t1, ALU.mult).then_inc(ve, 1)
            on("vector", dv1)
            return led.inc("ve")

        def emit_quant(v, src_ap, dst, rs2_col):
            return v.tensor_scalar(dst, src_ap, rs2_col, None, ALU.mult)

        def p0_sync(q):
            dma_in(q, "cst", sI[:, :], iden[:, :])
            dma_in(q, "cst", sGN[:, :], gN2[:, :])
            dma_in(q, "cst", sGO[:, :], gO2[:, :])
        on("sync", p0_sync)
        ld_cst = led.val("ld_cst")

        def p0_act(s):
            from concourse.hw_specs import get_activation_tables
            idx = None
            for i, fns in enumerate(get_activation_tables(nc.m.arch).values()):
                names = {f.name for f in fns}
                if {"Exp", "Ln", "Square"} <= names:
                    idx = i
                    break
            assert idx is not None
            s.add_instruction(mybir.InstLoadActFuncSet(
                name=nc.get_next_instruction_name(),
                act_func_set_id=idx, ins=[], outs=[]))
        on("scalar", p0_act)

        def p0_vec(v):
            v.memset(sOne[:, :], 1.0)
            v.memset(tiny[:, :], 0.0)
            v.memset(SCAL[:, :], 0.0)
            v.memset(amc[:, :], 0.0).then_inc(ve, 1)
        on("vector", p0_vec)
        led.inc("ve")

        pw_units = []
        wt_gate = {0: 0, 1: 0}
        w8c_gate = {0: 0, 1: 0}
        pw_state = {"scal": None}

        def mk_amax_unit(t, qq):
            def unit():
                slot = (t * 4 + qq) % 2
                g = wt_gate[slot]

                def ld_(q):
                    if g:
                        q.wait_ge(ve, g)
                    dma_in(q, "wt%d" % slot, wtst[:, slot, :],
                           WtT[t, :, qq * 512:(qq + 1) * 512])
                on("sync", ld_)
                t_ld = led.val("ld_wt%d" % slot)

                def red_(v):
                    v.wait_ge(LD["wt%d" % slot], t_ld)
                    v.reduce_max(amcols[:, t * 4 + qq:t * 4 + qq + 1],
                                 wtst[:, slot, :], AX.X,
                                 apply_absolute_value=True).then_inc(ve, 1)
                on("vector", red_)
                wt_gate[slot] = led.inc("ve")
            return unit

        def pw_combine_unit():
            ve_hop = led.val("ve")

            def pw_comb(v):
                v.wait_ge(ve, ve_hop)
                v.reduce_max(amc[:, 0:1], amcols[:, 0:64], AX.X).then_inc(ve, 1)
            on("vector", pw_comb)
            ve_am = led.inc("ve")

            def pw_tr(tn):
                tn.wait_ge(ve, ve_am)
                tn.wait_ge(LD["cst"], ld_cst)
                tn.transpose(ps[7][0:1, 0:P], amc[:, 0:1],
                             sI[:, :]).then_inc(pe, 1)
            on("tensor", pw_tr)
            pe_tr = led.inc("pe")

            def pw_sw(v):
                b0 = led.val("ve")
                v.wait_ge(pe, pe_tr)
                v.reduce_max(tiny[0:1, 0:1], ps[7][0:1, 0:P],
                             AX.X).then_inc(ve, 1)
                v.wait_ge(ve, b0 + 1)
                v.tensor_single_scalar(tiny[0:1, 0:1], tiny[0:1, 0:1], INV448,
                                       ALU.mult).then_inc(ve, 1)
                v.wait_ge(ve, b0 + 2)
                v.tensor_single_scalar(tiny[0:1, 0:1], tiny[0:1, 0:1], 1e-12,
                                       ALU.max).then_inc(ve, 1)
                v.wait_ge(ve, b0 + 3)
                v.reciprocal(amcols[0:1, 96:97], tiny[0:1, 0:1]).then_inc(ve, 1)
                v.wait_ge(ve, b0 + 4)
                v.tensor_single_scalar(amcols[0:1, 97:98], amcols[0:1, 96:97],
                                       0.5, ALU.mult).then_inc(ve, 1)
            on("vector", pw_sw)
            for _ in range(4):
                led.inc("ve")
            ve_sw = led.inc("ve")

            def pw_bc(tn):
                tn.wait_ge(ve, ve_sw)
                tn.matmul(ps[7][0:P, 0:2], sOne[0:1, 0:P], amcols[0:1, 96:98],
                          start=True, stop=True).then_inc(pe, 1)
            on("tensor", pw_bc)
            pe_bc = led.inc("pe")

            def pw_bc_ev(v):
                v.wait_ge(pe, pe_bc)
                v.tensor_copy(SCAL[:, 0:2], ps[7][0:P, 0:2]).then_inc(ve, 1)
            on("vector", pw_bc_ev)
            pw_state["scal"] = led.inc("ve")

        def mk_quant_unit(t, qq):
            def unit():
                slot = (t * 4 + qq) % 2
                g = wt_gate[slot]
                sog = w8c_gate[slot]

                def ld_(q):
                    if g:
                        q.wait_ge(ve, g)
                    dma_in(q, "wt%d" % slot, wtst[:, slot, :],
                           WtT[t, :, qq * 512:(qq + 1) * 512])
                on("sync", ld_)
                t_ld = led.val("ld_wt%d" % slot)

                def q_(v):
                    v.wait_ge(LD["wt%d" % slot], t_ld)
                    v.wait_ge(ve, pw_state["scal"])
                    if sog:
                        v.wait_ge(so, sog)
                    emit_quant(v, wtst[:, slot, :], w8c[:, slot, :],
                               SCAL[:, 1:2]).then_inc(ve, 1)
                on("vector", q_)
                wt_gate[slot] = ve_q = led.inc("ve")

                def st_(q):
                    q.wait_ge(ve, ve_q)
                    dma_out(q, w8d[qq, t // 2, :,
                                   (t % 2) * 512:(t % 2) * 512 + 512],
                            w8c[:, slot, :])
                on("sync", st_)
                w8c_gate[slot] = led.val("so")
            return unit

        for t in range(NF):
            for qq in range(4):
                pw_units.append(mk_amax_unit(t, qq))
        pw_units.append(pw_combine_unit)
        for t in range(NF):
            for qq in range(4):
                pw_units.append(mk_quant_unit(t, qq))

        def pump(n):
            for _ in range(n):
                if pw_units:
                    pw_units.pop(0)()

        xgate = [None, None]

        def xwait(q, sl):
            if xgate[sl] is not None:
                gp, ga, gv = xgate[sl]
                q.wait_ge(pe, gp)
                q.wait_ge(ac, ga)
                q.wait_ge(ve, gv)

        def xmark(sl):
            xgate[sl] = (led.val("pe"), led.val("ac"), led.val("ve"))

        for blk in (2, 3):
            mu_dst = Smu_o[:, blk - 2, :]
            r12_dst = Sr12_o[:, blk - 2, :]

            for g2 in range(8):
                sl = g2 % 2

                def p1_ld(q, blk=blk, g2=g2, sl=sl):
                    xwait(q, sl)
                    dma_in(q, "xb%d" % sl, xbig[:, sl * 2:sl * 2 + 2, :],
                           xblk[blk, :, g2 * 2:(g2 + 1) * 2, :])
                on("sync", p1_ld)
                t_ld = led.val("ld_xb%d" % sl)
                pe_base = led.val("pe")

                def p1_sq(s, sl=sl, t_ld=t_ld, pe_base=pe_base):
                    s.wait_ge(LD["xb%d" % sl], t_ld)
                    s.wait_ge(pe, pe_base)
                    for i in range(2):
                        s.activation(SQ[:, i, :], xbig[:, sl * 2 + i, :],
                                     AF.Square).then_inc(ac, 1)
                on("scalar", p1_sq)
                ac_marks = [led.inc("ac") for _ in range(2)]

                ve_emit = led.val("ve")

                def p1_mm(tn, g2=g2, sl=sl, t_ld=t_ld, ac_marks=ac_marks,
                          ve_emit=ve_emit):
                    tn.wait_ge(LD["xb%d" % sl], t_ld)
                    if g2 == 0:
                        tn.wait_ge(ve, ve_emit)
                    for i in range(2):
                        fc = g2 * 2 + i
                        tn.matmul(ps[0][:, :], sOne[:, :], xbig[:, sl * 2 + i, :],
                                  start=(fc == 0), stop=(fc == NF - 1))
                        tn.wait_ge(ac, ac_marks[i])
                        tn.matmul(ps[1][:, :], sOne[:, :], SQ[:, i, :],
                                  start=(fc == 0),
                                  stop=(fc == NF - 1)).then_inc(pe, 1)
                on("tensor", p1_mm)
                for _ in range(2):
                    led.inc("pe")
                xmark(sl)
                pump(4)

            pe_stats = led.val("pe")

            def p1_sc(v, mu_dst=mu_dst, pe_stats=pe_stats):
                v.wait_ge(pe, pe_stats)
                v.tensor_scalar(mu_dst, ps[0][:, :], INV2048, None, ALU.mult)
                v.tensor_scalar(S[:, 1, :], ps[1][:, :], INV2048, None, ALU.mult)
                v.tensor_tensor(S[:, 2, :], mu_dst, mu_dst, ALU.mult)
                v.tensor_tensor(S[:, 2, :], S[:, 1, :], S[:, 2, :],
                                ALU.subtract)
                v.tensor_single_scalar(S[:, 4, :], S[:, 2, :], 1.0,
                                       ALU.mult).then_inc(ve, 1)
            on("vector", p1_sc)
            led.inc("ve")
            emit_rsqrt(S[:, 4, :], S[:, 3, :], EPS, S[:, 1, :])

            def p1_sc2(v):
                v.tensor_tensor(S[:, 1, :], S[:, 3, :], S[:, 3, :], ALU.mult)
                v.tensor_tensor(S[:, 1, :], S[:, 1, :], S[:, 2, :],
                                ALU.mult).then_inc(ve, 1)
            on("vector", p1_sc2)
            led.inc("ve")
            emit_rsqrt(S[:, 1, :], S[:, 4, :], EPS, S[:, 2, :])

            def p1_sc3(v, r12_dst=r12_dst):
                v.tensor_tensor(r12_dst, S[:, 3, :], S[:, 4, :],
                                ALU.mult).then_inc(ve, 1)
            on("vector", p1_sc3)
            ve_r12 = led.inc("ve")

            for g2 in range(8):
                sl = g2 % 2

                def p1_ld2(q, blk=blk, g2=g2, sl=sl):
                    xwait(q, sl)
                    dma_in(q, "xb%d" % sl, xbig[:, sl * 2:sl * 2 + 2, :],
                           xblk[blk, :, g2 * 2:(g2 + 1) * 2, :])
                    for i in range(2):
                        fc = g2 * 2 + i
                        dma_in(q, "kv%d" % (fc % 4), wkvst[:, fc % 4, :],
                               Wkv_t[fc, :, :])
                on("sync", p1_ld2)
                t_ld = led.val("ld_xb%d" % sl)
                t_kv = {i: led.val("ld_kv%d" % i) for i in range(4)}
                pe_base = led.val("pe")

                if blk == 2:
                    xn2dst = [B[:, g2 * 2, :], B[:, g2 * 2 + 1, :]]
                else:
                    xn2dst = [C[:, g2 * 2, RB:2 * RB],
                              C[:, g2 * 2 + 1, RB:2 * RB]]

                def p1_ap(v, g2=g2, sl=sl, t_ld=t_ld, ve_r12=ve_r12,
                          pe_base=pe_base, mu_dst=mu_dst, r12_dst=r12_dst,
                          xn2dst=xn2dst):
                    v.wait_ge(LD["xb%d" % sl], t_ld)
                    v.wait_ge(ve, ve_r12)
                    v.wait_ge(pe, pe_base)
                    for i in range(2):
                        fc = g2 * 2 + i
                        v.tensor_tensor(S[:, 0, :], xbig[:, sl * 2 + i, :],
                                        mu_dst, ALU.subtract)
                        v.scalar_tensor_tensor(
                            xn2dst[i], S[:, 0, :], sGN[:, fc:fc + 1],
                            r12_dst, ALU.mult, ALU.mult).then_inc(ve, 1)
                on("vector", p1_ap)
                ve_marks = [led.inc("ve") for _ in range(2)]

                def p1_kv(tn, g2=g2, t_kv=t_kv, ve_marks=ve_marks,
                          xn2dst=xn2dst):
                    for i in range(2):
                        fc = g2 * 2 + i
                        tn.wait_ge(LD["kv%d" % (fc % 4)], t_kv[fc % 4])
                        tn.wait_ge(ve, ve_marks[i])
                        tn.matmul(ps[2][:, :], wkvst[:, fc % 4, 0:128],
                                  xn2dst[i],
                                  start=(fc == 0), stop=(fc == NF - 1))
                        tn.matmul(ps[3][:, :], wkvst[:, fc % 4, 128:256],
                                  xn2dst[i],
                                  start=(fc == 0),
                                  stop=(fc == NF - 1)).then_inc(pe, 1)
                on("tensor", p1_kv)
                for _ in range(2):
                    led.inc("pe")
                xmark(sl)
                pump(4)

            pe_kv = led.val("pe")

            def p1_ev(v, blk=blk, pe_kv=pe_kv):
                v.wait_ge(pe, pe_kv)
                v.tensor_copy(kT[:, (blk - 2) * RB:(blk - 1) * RB],
                              ps[2][:, :])
                v.tensor_copy(SQ[:, 0, :], ps[3][:, :]).then_inc(ve, 1)
            on("vector", p1_ev)
            ve_ev = led.inc("ve")

            ve_trbase = led.val("ve")
            for jj in range(4):
                jc = (blk - 2) * 4 + jj

                def p1_tr(tn, jj=jj, ve_ev=ve_ev, ve_trbase=ve_trbase):
                    tn.wait_ge(ve, ve_ev)
                    if jj >= 1:
                        tn.wait_ge(ve, ve_trbase + jj)
                    tn.transpose(ps[4][:, 0:128],
                                 SQ[:, 0, jj * 128:(jj + 1) * 128],
                                 sI[:, :]).then_inc(pe, 1)
                on("tensor", p1_tr)
                pe_t = led.inc("pe")

                def p1_trev(v, jc=jc, pe_t=pe_t):
                    v.wait_ge(pe, pe_t)
                    v.tensor_copy(vv[:, jc, :], ps[4][:, 0:128]).then_inc(ve, 1)
                on("vector", p1_trev)
                led.inc("ve")

        pump(len(pw_units))

        ve_kl0 = led.val("ve")
        pe_kl0 = led.val("pe")
        klm = []
        for b4 in range(2):
            def kl_sq(s, b4=b4, ve_kl0=ve_kl0, pe_kl0=pe_kl0):
                if b4 == 0:
                    s.wait_ge(ve, ve_kl0)
                    s.wait_ge(pe, pe_kl0)
                s.activation(ptil[:, b4, :], kT[:, b4 * RB:(b4 + 1) * RB],
                             AF.Square).then_inc(ac, 1)
            on("scalar", kl_sq)
            ac_m = led.inc("ac")

            def kl_mm(tn, b4=b4, ac_m=ac_m, ve_kl0=ve_kl0):
                tn.wait_ge(ac, ac_m)
                tn.wait_ge(ve, ve_kl0)
                tn.matmul(ps[4 + b4][:, :], sOne[:, :], ptil[:, b4, :],
                          start=True, stop=True).then_inc(pe, 1)
            on("tensor", kl_mm)
            klm.append(led.inc("pe"))

        def kl_ev(v, pe_l=klm[1]):
            v.wait_ge(pe, pe_l)
            v.tensor_copy(xbig[:, 0, :], ps[4][:, :])
            v.tensor_copy(xbig[:, 1, :], ps[5][:, :])
            v.tensor_single_scalar(xbig[:, 0:2, :], xbig[:, 0:2, :],
                                   float(L2EPS), ALU.add).then_inc(ve, 1)
        on("vector", kl_ev)
        kl_v0 = led.inc("ve")

        def kl_ln(s, v0=kl_v0):
            b0 = led.val("ac")
            s.wait_ge(ve, v0)
            s.activation(xbig[:, 2:4, :], xbig[:, 0:2, :],
                         AF.Ln).then_inc(ac, 1)
            s.wait_ge(ac, b0 + 1)
            s.activation(xbig[:, 2:4, :], xbig[:, 2:4, :], AF.Exp,
                         scale=-0.5).then_inc(ac, 1)
        on("scalar", kl_ln)
        led.inc("ac")
        kl_a0 = led.inc("ac")

        def kl_nw(v, a0=kl_a0):
            v.wait_ge(ac, a0)
            v.tensor_tensor(xbig[:, 0:2, :], xbig[:, 0:2, :],
                            xbig[:, 2:4, :], ALU.mult)
            v.tensor_tensor(xbig[:, 0:2, :], xbig[:, 0:2, :],
                            xbig[:, 2:4, :], ALU.mult)
            v.tensor_scalar(xbig[:, 0:2, :], xbig[:, 0:2, :], -0.5, 1.5,
                            ALU.mult, ALU.add)
            v.tensor_tensor(xbig[:, 2:4, :], xbig[:, 2:4, :],
                            xbig[:, 0:2, :], ALU.mult)
            v.tensor_single_scalar(xbig[:, 2:4, :], xbig[:, 2:4, :], 4.0,
                                   ALU.mult)
            for b4 in range(2):
                ins = v.tensor_tensor(kT[:, b4 * RB:(b4 + 1) * RB],
                                      kT[:, b4 * RB:(b4 + 1) * RB],
                                      xbig[:, 2 + b4, :], ALU.mult)
            ins.then_inc(ve, 1)
        on("vector", kl_nw)
        led.inc("ve")

        emit_barrier()
        if "kT" in tap_dram:
            on("sync", lambda q: dma_out(q, tap_dram["kT"][:, :], kT[:, :]))
        if "v" in tap_dram:
            on("sync", lambda q: dma_out(q, tap_dram["v"][:, :, :], vv[:, :, :]))
        if "stat1" in tap_dram:
            def ts1(q):
                dma_out(q, tap_dram["stat1"][:, 0:2, :], Smu_o[:, :, :])
                dma_out(q, tap_dram["stat1"][:, 2:4, :], Sr12_o[:, :, :])
            on("sync", ts1)

        ve_pack = led.val("ve")

        def ag_pack(q, ve_pack=ve_pack):
            q.wait_ge(ve, ve_pack)
            dma_in(q, "agp", agin[:, 0:1024], kT[:, 0:1024])
            dma_in(q, "agp", agin[:, 1024:2048], vv[:, 0:8, :])
        on("sync", ag_pack)
        t_agp = led.val("ld_agp")

        def ag_cc(g, t_agp=t_agp):
            g.wait_ge(LD["agp"], t_agp)
            g.collective_compute(
                "AllGather", ALU.bypass,
                replica_groups=[[2 * i, 2 * i + 1] for i in range(4)],
                ins=[agin.ap().opt()],
                outs=[agout.ap().opt()]).then_inc(cc, 1)
        on("gpsimd", ag_cc)
        led.inc("cc")

        def ag_fetch(g):
            g.wait_ge(cc, 1)
            dma_in(g, "agf", kT[:, 0:1024], agout[0, :, 0:1024])
            dma_in(g, "agf", kT[:, 1024:2048], agout[1, :, 0:1024])
            dma_in(g, "agf", vv[:, 0:8, :], agout[0, :, 1024:2048])
            dma_in(g, "agf", vv[:, 8:16, :], agout[1, :, 1024:2048])
        on("gpsimd", ag_fetch)
        t_agf = led.val("ld_agf")

        def emit_tail(rc, oblk, xfree=None):
            pe_st = led.val("pe")

            def p5_sc(v, pe_st=pe_st):
                v.wait_ge(pe, pe_st)
                v.tensor_scalar(S[:, 1, :], ps[4][:, :], INV2048, None,
                                ALU.mult)
                v.tensor_scalar(S[:, 2, :], ps[5][:, :], INV2048, None,
                                ALU.mult)
                v.tensor_tensor(S[:, 0, :], S[:, 1, :], S[:, 1, :], ALU.mult)
                v.tensor_tensor(S[:, 2, :], S[:, 2, :], S[:, 0, :],
                                ALU.subtract).then_inc(ve, 1)
            on("vector", p5_sc)
            sc_m = led.inc("ve")
            emit_rsqrt(S[:, 2, :], S[:, 3, :], EPS, S[:, 4, :])

            resid_marks = []
            for g2 in range(8):
                sl = g2 % 2

                def p5_xl(g, oblk=oblk, g2=g2, sl=sl, xfree=xfree):
                    xwait(g, sl)
                    if xfree and g2 < 2:
                        g.wait_ge(ve, xfree)
                    dma_in(g, "xg%d" % sl, xbig[:, sl * 2:sl * 2 + 2, :],
                           xblk[oblk, :, g2 * 2:(g2 + 1) * 2, :])
                on("gpsimd", p5_xl)
                t_ld = led.val("ld_xg%d" % sl)

                def p5_res(v, g2=g2, sl=sl, rc=rc, t_ld=t_ld):
                    v.wait_ge(LD["xg%d" % sl], t_ld)
                    for i in range(2):
                        fc = g2 * 2 + i
                        ctile = C[:, fc, rc * RB:(rc + 1) * RB]
                        v.tensor_tensor(S[:, 0, :], ctile, S[:, 1, :],
                                        ALU.subtract)
                        v.scalar_tensor_tensor(S[:, 0, :], S[:, 0, :],
                                               sGO[:, fc:fc + 1], S[:, 3, :],
                                               ALU.mult, ALU.mult)
                        ins = v.tensor_tensor(ctile, xbig[:, sl * 2 + i, :],
                                              S[:, 0, :], ALU.add)
                    ins.then_inc(ve, 1)
                on("vector", p5_res)
                resid_marks.append(led.inc("ve"))
                xmark(sl)

            pe6_base = led.val("pe")
            for fc in range(NF):
                def p6_sq(s, fc=fc, rc=rc, pe6_base=pe6_base,
                          rm=resid_marks[fc // 2]):
                    s.wait_ge(ve, rm)
                    s.wait_ge(pe, pe6_base + max(0, fc - 1))
                    s.activation(ptil[:, fc % 2, :],
                                 C[:, fc, rc * RB:(rc + 1) * RB],
                                 AF.Square).then_inc(ac, 1)
                on("scalar", p6_sq)
                ac_m = led.inc("ac")

                def p6_smm(tn, fc=fc, rc=rc, ac_m=ac_m, sc_m=sc_m,
                           rm=resid_marks[fc // 2]):
                    if fc == 0:
                        tn.wait_ge(ve, sc_m)
                    tn.wait_ge(ve, rm)
                    tn.matmul(ps[4][:, :], sOne[:, :],
                              C[:, fc, rc * RB:(rc + 1) * RB],
                              start=(fc == 0), stop=(fc == NF - 1))
                    tn.wait_ge(ac, ac_m)
                    tn.matmul(ps[5][:, :], sOne[:, :], ptil[:, fc % 2, :],
                              start=(fc == 0),
                              stop=(fc == NF - 1)).then_inc(pe, 1)
                on("tensor", p6_smm)
                led.inc("pe")

            pe_st6 = led.val("pe")

            def p6_sc(v, rc=rc, pe_st6=pe_st6):
                v.wait_ge(pe, pe_st6)
                v.tensor_scalar(Smu3[:, rc, :], ps[4][:, :], INV2048, None,
                                ALU.mult)
                v.tensor_scalar(S[:, 2, :], ps[5][:, :], INV2048, None,
                                ALU.mult)
                v.tensor_tensor(S[:, 0, :], Smu3[:, rc, :], Smu3[:, rc, :],
                                ALU.mult)
                v.tensor_tensor(S[:, 2, :], S[:, 2, :], S[:, 0, :],
                                ALU.subtract).then_inc(ve, 1)
            on("vector", p6_sc)
            led.inc("ve")
            emit_rsqrt(S[:, 2, :], Srho3[:, rc, :], EPS, S[:, 4, :])

            for fc in range(NF):
                def p6_am(v, fc=fc, rc=rc, last=(fc == NF - 1)):
                    v.tensor_tensor(S[:, 0, :],
                                    C[:, fc, rc * RB:(rc + 1) * RB],
                                    Smu3[:, rc, :], ALU.subtract)
                    v.tensor_tensor(S[:, 0, :], S[:, 0, :], Srho3[:, rc, :],
                                    ALU.mult)
                    col = 64 + rc * NF + fc
                    ins = v.reduce_max(amcols[:, col:col + 1], S[:, 0, :],
                                       AX.X, apply_absolute_value=True)
                    if last:
                        ins.then_inc(ve, 1)
                on("vector", p6_am)
            led.inc("ve")

        pending_tail = []
        xbig_free = {}
        for rc in range(2):
            oblk = 2 + rc


            pe_qend_prev = None
            ve_evq_prev = None
            for hg in range(4):
                pe_base = led.val("pe")
                ve_done_prev = ve_evq_prev if hg else led.val("ve")
                pe_wst_prev = pe_qend_prev if hg else pe_base

                for fc in range(NF):
                    def p3_ld(q, hg=hg, fc=fc, pe_base=pe_base,
                              pe_wst_prev=pe_wst_prev):
                        if fc >= 2:
                            q.wait_ge(pe, pe_base + fc - 1)
                        else:
                            q.wait_ge(pe, pe_wst_prev)
                        dma_in(q, "w%d" % (fc % 2), wst[:, fc % 2, :],
                               Wq_t[hg, fc, :, :])
                    on("sync", p3_ld)
                    t_ld = led.val("ld_w%d" % (fc % 2))

                    def p3_mm(tn, fc=fc, rc=rc, t_ld=t_ld,
                              ve_done_prev=ve_done_prev):
                        tn.wait_ge(LD["w%d" % (fc % 2)], t_ld)
                        if fc == 0:
                            tn.wait_ge(ve, ve_done_prev)
                        xn2src = (B[:, fc, :] if rc == 0
                                  else C[:, fc, RB:2 * RB])
                        for hh in range(4):
                            ins = tn.matmul(
                                ps[hh][:, :],
                                wst[:, fc % 2, hh * 128:(hh + 1) * 128],
                                xn2src, start=(fc == 0),
                                stop=(fc == NF - 1))
                        ins.then_inc(pe, 1)
                    on("tensor", p3_mm)
                    led.inc("pe")
                pe_qend_prev = led.val("pe")

                pe_q = led.val("pe")

                def p3_evq(v, hg=hg, pe_q=pe_q):
                    v.wait_ge(pe, pe_q)
                    for hh in range(4):
                        ins = v.tensor_copy(A[:, hg * 4 + hh, :], ps[hh][:, :])
                    ins.then_inc(ve, 1)
                on("vector", p3_evq)
                ve_evq_prev = led.inc("ve")

                if rc == 0:
                    lad = [(xbig[:, 0:2, :], xbig[:, 2:4, :], 0),
                           (Smu_o[:, :, :], Sr12_o[:, :, :], 2)]
                    banks = (4, 5, 6, 7)
                else:
                    lad = [(B[:, 0:2, :], B[:, 2:4, :], 0),
                           (Smu_o[:, :, :], Sr12_o[:, :, :], 2)]
                    banks = (6, 7, 6, 7)
                ve_evq = led.val("ve")
                pe_now3 = led.val("pe")
                ve_lfree = led.val("ve")
                mm_marks = []
                v0m = []
                for li, (vio, rout, hb) in enumerate(lad):
                    for j in range(2):
                        hh = li * 2 + j
                        h = hg * 4 + hh

                        def p3_sq(s, h=h, hh=hh, ve_evq=ve_evq,
                                  pe_now3=pe_now3, mm_marks=tuple(mm_marks)):
                            if hh == 0:
                                s.wait_ge(ve, ve_evq)
                                s.wait_ge(pe, pe_now3)
                            if hh >= 2:
                                s.wait_ge(pe, mm_marks[hh - 2])
                            s.activation(SQ[:, hh % 2, :], A[:, h, :],
                                         AF.Square).then_inc(ac, 1)
                        on("scalar", p3_sq)
                        ac_m = led.inc("ac")

                        def p3_lmm(tn, ac_m=ac_m, hh=hh, bank=banks[hh],
                                   ve_lfree=ve_lfree,
                                   ev1=(v0m[0] if v0m else None)):
                            tn.wait_ge(ac, ac_m)
                            if hh == 0:
                                tn.wait_ge(ve, ve_lfree)
                            if hh == 2 and ev1 is not None and bank == banks[0]:
                                tn.wait_ge(ve, ev1)
                            tn.matmul(ps[bank][:, :], sOne[:, :],
                                      SQ[:, hh % 2, :],
                                      start=True, stop=True).then_inc(pe, 1)
                        on("tensor", p3_lmm)
                        mm_marks.append(led.inc("pe"))

                    def p3_ev(v, vio=vio, b0=banks[li * 2], b1=banks[li * 2 + 1],
                              pe_l=mm_marks[-1]):
                        v.wait_ge(pe, pe_l)
                        v.tensor_copy(vio[:, 0, :], ps[b0][:, :])
                        v.tensor_copy(vio[:, 1, :], ps[b1][:, :])
                        v.tensor_single_scalar(vio, vio,
                                               float(L2EPS / 256.0),
                                               ALU.add).then_inc(ve, 1)
                    on("vector", p3_ev)
                    v0m.append(led.inc("ve"))

                a0m = []
                for li, (vio, rout, hb) in enumerate(lad):
                    def p3_ln(s, vio=vio, rout=rout, v0=v0m[li]):
                        b0 = led.val("ac")
                        s.wait_ge(ve, v0)
                        s.activation(rout, vio, AF.Ln).then_inc(ac, 1)
                        s.wait_ge(ac, b0 + 1)
                        s.activation(rout, rout, AF.Exp,
                                     scale=-0.5).then_inc(ac, 1)
                    on("scalar", p3_ln)
                    led.inc("ac")
                    a0m.append(led.inc("ac"))

                for li, (vio, rout, hb) in enumerate(lad):
                    def p3_nw(v, vio=vio, rout=rout, hb=hb, hg=hg,
                              a0=a0m[li]):
                        v.wait_ge(ac, a0)
                        v.tensor_tensor(vio, vio, rout, ALU.mult)
                        v.tensor_tensor(vio, vio, rout, ALU.mult)
                        v.tensor_scalar(vio, vio, -0.5, 1.5,
                                        ALU.mult, ALU.add)
                        v.tensor_tensor(rout, rout, vio, ALU.mult)
                        v.tensor_single_scalar(rout, rout, 4.0, ALU.mult)
                        for hh in range(2):
                            h = hg * 4 + hb + hh
                            ins = v.tensor_tensor(A[:, h, :], A[:, h, :],
                                                  rout[:, hh, :], ALU.mult)
                        ins.then_inc(ve, 1)
                    on("vector", p3_nw)
                    led.inc("ve")

            if rc == 0:
                xbig_free[0] = led.val("ve")
            elif pending_tail:
                pending_tail.pop(0)()

            emit_barrier()
            if "qn" in tap_dram:
                def tqn(q, rc=rc):
                    dma_out(q, tap_dram["qn"][:, :, rc * RB:(rc + 1) * RB],
                            A[:, :, :])
                on("sync", tqn)

            for h in range(NH):
                pe_base = led.val("pe")
                ac_base = led.val("ac")
                ve_att_prev = led.val("ve")

                plan = ["sim0", "sim1"]
                for jc in range(NJ):
                    plan.append("sa%d" % jc)
                    if jc + 2 < NJ:
                        plan.append("sim%d" % (jc + 2))
                sim_mark, av_mark = {}, {}
                m = pe_base
                for stp in plan:
                    m += 1
                    if stp.startswith("sim"):
                        sim_mark[int(stp[3:])] = m
                    else:
                        av_mark[int(stp[2:])] = m

                def p4_pe(tn, h=h, rc=rc, pe_base=pe_base, ac_base=ac_base,
                          ve_att_prev=ve_att_prev, plan=plan):
                    if h == 0 and rc == 0:
                        tn.wait_ge(LD["agf"], t_agf)
                    for stp in plan:
                        if stp.startswith("sim"):
                            jc = int(stp[3:])
                            if jc < 2:
                                tn.wait_ge(ac, ac_base)
                            else:
                                tn.wait_ge(ac, ac_base + jc - 1)
                            tn.matmul(ps[jc % 2][:, :],
                                      kT[:, jc * 128:(jc + 1) * 128],
                                      A[:, h, :], start=True,
                                      stop=True).then_inc(pe, 1)
                        else:
                            jc = int(stp[2:])
                            if jc == 0:
                                tn.wait_ge(ve, ve_att_prev)
                            tn.wait_ge(ac, ac_base + jc + 1)
                            tn.matmul(ps[3][:, :], sOne[:, :],
                                      ptil[:, jc % 2, :],
                                      start=(jc == 0), stop=(jc == NJ - 1))
                            tn.matmul(ps[2][:, :], vv[:, jc, :],
                                      ptil[:, jc % 2, :],
                                      start=(jc == 0),
                                      stop=(jc == NJ - 1)).then_inc(pe, 1)
                on("tensor", p4_pe)
                for _ in range(2 * NJ):
                    led.inc("pe")
                assert led.val("pe") == m

                def p4_exp(s, sim_mark=sim_mark, av_mark=av_mark):
                    for jc in range(NJ):
                        s.wait_ge(pe, sim_mark[jc])
                        if jc >= 2:
                            s.wait_ge(pe, av_mark[jc - 2])
                        s.activation(ptil[:, jc % 2, :], ps[jc % 2][:, :],
                                     AF.Exp).then_inc(ac, 1)
                on("scalar", p4_exp)
                exp_marks = [led.inc("ac") for _ in range(NJ)]
                assert exp_marks[-1] == ac_base + NJ

                pe_last = av_mark[NJ - 1]

                def p4_cp(v, pe_last=pe_last):
                    v.wait_ge(pe, pe_last)
                    v.tensor_copy(S[:, 2, :], ps[3][:, :]).then_inc(ve, 1)
                on("vector", p4_cp)
                led.inc("ve")
                emit_recip(S[:, 2, :], S[:, 1, :], S[:, 4, :], S[:, 0, :])

                def p4_out(v, h=h):
                    v.tensor_tensor(B[:, h, :], ps[2][:, :], S[:, 1, :],
                                    ALU.mult).then_inc(ve, 1)
                on("vector", p4_out)
                led.inc("ve")

            emit_barrier()
            if "att" in tap_dram:
                def tat(q, rc=rc):
                    dma_out(q, tap_dram["att"][:, :, rc * RB:(rc + 1) * RB],
                            B[:, :, :])
                on("sync", tat)

            ve_p5start = led.val("ve")
            sqg = {}
            for fg in range(4):
                pe_base = led.val("pe")
                ve_prev = led.val("ve")
                for hdc in range(NF):
                    def p5_ld(q, fg=fg, hdc=hdc, pe_base=pe_base,
                              ve_prev=ve_prev):
                        if hdc >= 2:
                            q.wait_ge(pe, pe_base + hdc - 1)
                        if hdc == 0:
                            q.wait_ge(ve, ve_prev)
                        dma_in(q, "w%d" % (hdc % 2), wst[:, hdc % 2, :],
                               Wo_t[fg, hdc, :, :])
                    on("sync", p5_ld)
                    t_ld = led.val("ld_w%d" % (hdc % 2))

                    def p5_mm(tn, hdc=hdc, t_ld=t_ld):
                        tn.wait_ge(LD["w%d" % (hdc % 2)], t_ld)
                        for ff in range(4):
                            ins = tn.matmul(
                                ps[ff][:, :],
                                wst[:, hdc % 2, ff * 128:(ff + 1) * 128],
                                B[:, hdc, :], start=(hdc == 0),
                                stop=(hdc == NF - 1))
                        ins.then_inc(pe, 1)
                    on("tensor", p5_mm)
                    led.inc("pe")

                pe_fg = led.val("pe")

                def p5_ev(v, fg=fg, rc=rc, pe_fg=pe_fg):
                    v.wait_ge(pe, pe_fg)
                    for ff in range(4):
                        ins = v.tensor_copy(
                            C[:, fg * 4 + ff, rc * RB:(rc + 1) * RB],
                            ps[ff][:, :])
                    ins.then_inc(ve, 1)
                on("vector", p5_ev)
                ev_m = led.inc("ve")

                for ff in range(4):
                    fc = fg * 4 + ff

                    def p5_sq(s, fc=fc, rc=rc, ev_m=ev_m, g=sqg.get(fc - 2)):
                        s.wait_ge(ve, ev_m)
                        if g:
                            s.wait_ge(pe, g)
                        s.activation(SQ[:, fc % 2, :],
                                     C[:, fc, rc * RB:(rc + 1) * RB],
                                     AF.Square).then_inc(ac, 1)
                    on("scalar", p5_sq)
                    ac_m = led.inc("ac")

                    def p5_smm(tn, fc=fc, rc=rc, ac_m=ac_m, ev_m=ev_m,
                               ve_p5start=ve_p5start):
                        if fc == 0:
                            tn.wait_ge(ve, ve_p5start)
                        tn.wait_ge(ve, ev_m)
                        tn.matmul(ps[4][:, :], sOne[:, :],
                                  C[:, fc, rc * RB:(rc + 1) * RB],
                                  start=(fc == 0), stop=(fc == NF - 1))
                        tn.wait_ge(ac, ac_m)
                        tn.matmul(ps[5][:, :], sOne[:, :], SQ[:, fc % 2, :],
                                  start=(fc == 0),
                                  stop=(fc == NF - 1)).then_inc(pe, 1)
                    on("tensor", p5_smm)
                    sqg[fc] = led.inc("pe")

            if rc == 0:
                pending_tail.append(
                    lambda oblk=oblk: emit_tail(0, oblk, xbig_free[0]))
            else:
                emit_tail(1, oblk)

            emit_barrier()

        if "Cmid" in tap_dram:
            on("sync",
               lambda q: dma_out(q, tap_dram["Cmid"][:, :, :], C[:, :, :]))

        att_stack.close()
        xq = e(nc.sbuf_tensor("xq", [P, 2, NF, RB], FP8))

        ve_hop2 = led.val("ve")

        def p6_comb(v, ve_hop2=ve_hop2):
            v.wait_ge(ve, ve_hop2)
            v.reduce_max(amc[:, 1:2], amcols[:, 64:96], AX.X).then_inc(ve, 1)
        on("vector", p6_comb)
        led.inc("ve")
        ve_am2 = led.val("ve")

        def p6_tr(tn, ve_am2=ve_am2):
            tn.wait_ge(ve, ve_am2)
            tn.transpose(ps[7][0:1, 0:P], amc[:, 1:2], sI[:, :]).then_inc(pe, 1)
        on("tensor", p6_tr)
        pe_tr2 = led.inc("pe")

        def p6_loc(v, pe_tr2=pe_tr2):
            v.wait_ge(pe, pe_tr2)
            v.reduce_max(tiny[0:1, 1:2], ps[7][0:1, 0:P], AX.X).then_inc(ve, 1)
        on("vector", p6_loc)
        ve_loc = led.inc("ve")

        def p6_ccout(q, ve_loc=ve_loc):
            q.wait_ge(ve, ve_loc)
            dma_in(q, "cc2", ccin[:, :], tiny[0:1, 1:2])
        on("sync", p6_ccout)
        t_ldcc = led.val("ld_cc2")

        def p6_cc(g, t_ldcc=t_ldcc):
            g.wait_ge(LD["cc2"], t_ldcc)
            g.collective_compute(
                "AllReduce", ALU.max, replica_groups=[list(range(8))],
                ins=[ccin.ap().opt()], outs=[ccout.ap().opt()]).then_inc(cc, 1)
        on("gpsimd", p6_cc)
        led.inc("cc")

        def p6_ccin(g):
            g.wait_ge(cc, 2)
            dma_in(g, "cc3", tiny[0:1, 2:3], ccout[:, :])
        on("gpsimd", p6_ccin)
        t_ldg = led.val("ld_cc3")

        def p6_scale(v, t_ldg=t_ldg):
            b0 = led.val("ve")
            v.wait_ge(LD["cc3"], t_ldg)
            v.tensor_single_scalar(tiny[0:1, 3:4], tiny[0:1, 2:3], INV448,
                                   ALU.mult).then_inc(ve, 1)
            v.wait_ge(ve, b0 + 1)
            v.tensor_single_scalar(tiny[0:1, 3:4], tiny[0:1, 3:4], 1e-12,
                                   ALU.max).then_inc(ve, 1)
            v.wait_ge(ve, b0 + 2)
            v.reciprocal(S[0:1, 0, 0:1], tiny[0:1, 3:4]).then_inc(ve, 1)
            v.wait_ge(ve, b0 + 3)
            v.tensor_single_scalar(S[0:1, 0, 1:2], S[0:1, 0, 0:1], 0.5,
                                   ALU.mult).then_inc(ve, 1)
            v.wait_ge(ve, b0 + 4)
            v.tensor_tensor(S[0:1, 0, 2:3], tiny[0:1, 3:4], tiny[0:1, 0:1],
                            ALU.mult).then_inc(ve, 1)
            v.wait_ge(ve, b0 + 5)
            v.tensor_single_scalar(S[0:1, 0, 2:3], S[0:1, 0, 2:3], 4.0,
                                   ALU.mult).then_inc(ve, 1)
        on("vector", p6_scale)
        for _ in range(5):
            led.inc("ve")
        ve_row = led.inc("ve")

        def p6_bc(tn, ve_row=ve_row):
            tn.wait_ge(ve, ve_row)
            tn.matmul(ps[7][0:P, 0:3], sOne[0:1, 0:P], S[0:1, 0, 0:3],
                      start=True, stop=True).then_inc(pe, 1)
        on("tensor", p6_bc)
        pe_bc2 = led.inc("pe")

        def p6_bc_ev(v, pe_bc2=pe_bc2):
            v.wait_ge(pe, pe_bc2)
            v.tensor_copy(SCAL[:, 2:5], ps[7][0:P, 0:3]).then_inc(ve, 1)
        on("vector", p6_bc_ev)
        ve_scal_x = led.inc("ve")

        if "tiny" in tap_dram:
            on("sync", lambda q: dma_out(q, tap_dram["tiny"][:, :], tiny[:, :]))
        if "scal" in tap_dram:
            on("sync", lambda q: dma_out(q, tap_dram["scal"][:, :], SCAL[:, :]))

        ve_xq_rc = [None, None]
        for rc in range(2):
            for fc in range(NF):
                def p7_q(v, fc=fc, rc=rc, last=(fc == NF - 1)):
                    if rc == 0 and fc == 0:
                        v.wait_ge(ve, ve_scal_x)
                    v.tensor_tensor(S[:, 0, :], C[:, fc, rc * RB:(rc + 1) * RB],
                                    Smu3[:, rc, :], ALU.subtract)
                    v.tensor_tensor(S[:, 0, :], S[:, 0, :], Srho3[:, rc, :],
                                    ALU.mult)
                    ins = emit_quant(v, S[:, 0, :], xq[:, rc, fc, :],
                                     SCAL[:, 3:4])
                    if last:
                        ins.then_inc(ve, 1)
                on("vector", p7_q)
            ve_xq_rc[rc] = led.inc("ve")

            if "xq" in tap_dram:
                def txq(q, rc=rc, ve_xq=ve_xq_rc[rc]):
                    q.wait_ge(ve, ve_xq)
                    dma_out(q, tap_dram["xq"][:, :, rc * RB:(rc + 1) * RB],
                            xq[:, rc, :, :])
                on("sync", txq)

        w8_gate = {0: 0, 1: 0, 2: 0}
        fin_gate = {0: 0, 1: 0}
        for rc in range(2):

            for og in range(4):
                gidx = rc * 4 + og
                bb = 4 * (gidx % 2)
                pe_base = led.val("pe")
                ve_xq = ve_xq_rc[rc]
                for pr in range(NF // 2):
                    slot = (gidx * 8 + pr) % 3
                    g = w8_gate[slot]

                    def p7_ld(q, og=og, pr=pr, slot=slot, g=g):
                        if g:
                            q.wait_ge(pe, g)
                        dma_in(q, "w8%d" % slot, w8st[:, slot, :],
                               w8d[og, pr, :, :])
                    on("sync", p7_ld)
                    t_ld = led.val("ld_w8%d" % slot)

                    ve_bank = fin_gate[gidx % 2]

                    def p7_mm(tn, pr=pr, rc=rc, slot=slot, bb=bb, t_ld=t_ld,
                              ve_bank=ve_bank, ve_xq=ve_xq):
                        tn.wait_ge(LD["w8%d" % slot], t_ld)
                        if pr == 0:
                            tn.wait_ge(ve, ve_xq)
                            if ve_bank:
                                tn.wait_ge(ve, ve_bank)
                        for i in range(2):
                            fc = pr * 2 + i
                            for oi in range(4):
                                ins = tn.matmul(
                                    ps[bb + oi][:, :],
                                    w8st[:, slot,
                                         i * 512 + oi * 128:
                                         i * 512 + (oi + 1) * 128],
                                    xq[:, rc, fc, :], start=(fc == 0),
                                    stop=(fc == NF - 1))
                            ins.then_inc(pe, 1)
                    on("tensor", p7_mm)
                    led.inc("pe")
                    w8_gate[slot] = led.inc("pe")

                pe_og = led.val("pe")

                def p7_fin(v, og=og, rc=rc, bb=bb, pe_og=pe_og):
                    v.wait_ge(pe, pe_og)
                    for oi in range(4):
                        o = og * 4 + oi
                        ctile = C[:, o, rc * RB:(rc + 1) * RB]
                        v.tensor_scalar(qb1[:, :], ps[bb + oi][:, :],
                                        SCAL[:, 4:5], None, ALU.mult)
                        ins = v.tensor_tensor(ctile, ctile, qb1[:, :], ALU.add)
                    ins.then_inc(ve, 1)
                on("vector", p7_fin)
                ve_fin = led.inc("ve")
                fin_gate[gidx % 2] = ve_fin

                def p7_st(q, og=og, rc=rc, ve_fin=ve_fin):
                    q.wait_ge(ve, ve_fin)
                    for oi in range(4):
                        o = og * 4 + oi
                        dma_out(q, outT[o, :, rc * RB:(rc + 1) * RB],
                                C[:, o, rc * RB:(rc + 1) * RB])
                on("scalar", p7_st)

        if "C" in tap_dram:
            on("sync", lambda q: dma_out(q, tap_dram["C"][:, :, :], C[:, :, :]))

        so_total = led.val("so")

        def fin(q, so_total=so_total):
            q.wait_ge(so, so_total)
        on("sync", fin)

    return nc


def _layout_inputs(L0_attn_x, L0_attn_g_norm, L0_attn_Wq, L0_attn_Wkv,
                   L0_attn_Wout, L0_attn_g_out, L0_ffn_weight):
    x = np.ascontiguousarray(np.asarray(L0_attn_x, dtype=np.float32))
    Wq = np.ascontiguousarray(np.asarray(L0_attn_Wq, dtype=np.float32))
    Wkv = np.ascontiguousarray(np.asarray(L0_attn_Wkv, dtype=np.float32))
    Wout = np.ascontiguousarray(np.asarray(L0_attn_Wout, dtype=np.float32))
    Wt = np.asarray(L0_ffn_weight)
    gN = np.ascontiguousarray(np.asarray(L0_attn_g_norm, dtype=np.float32))
    gO = np.ascontiguousarray(np.asarray(L0_attn_g_out, dtype=np.float32))

    Wq_t = np.ascontiguousarray(Wq.reshape(NF, P, 4, 512).transpose(2, 0, 1, 3))
    Wkv_t = np.ascontiguousarray(Wkv.reshape(NF, P, 256))
    Wo_t = np.ascontiguousarray(Wout.reshape(NF, P, 4, 512).transpose(2, 0, 1, 3))
    WtT = np.ascontiguousarray(Wt.T).reshape(NF, P, 2048)
    gN2 = np.ascontiguousarray(gN.reshape(NF, P).T)
    gO2 = np.ascontiguousarray(gO.reshape(NF, P).T)
    iden = np.eye(P, dtype=np.float32)

    in_maps = []
    for c in range(8):
        b, half = c // 2, c % 2
        order = [2 * (1 - half), 2 * (1 - half) + 1, 2 * half, 2 * half + 1]
        xb = np.empty((NB, P, NF, RB), dtype=np.float32)
        for i, bi in enumerate(order):
            blkd = x[b, bi * RB:(bi + 1) * RB, :]
            xb[i] = blkd.T.reshape(NF, P, RB).transpose(1, 0, 2)
        in_maps.append(dict(xblk=xb, Wq_t=Wq_t, Wkv_t=Wkv_t, Wo_t=Wo_t,
                            WtT=WtT, gN2=gN2, gO2=gO2, iden=iden))
    return in_maps


def kernel(L0_attn_x, L0_attn_g_norm, L0_attn_Wq, L0_attn_Wkv, L0_attn_Wout,
           L0_attn_g_out, L0_ffn_weight):
    in_maps = _layout_inputs(L0_attn_x, L0_attn_g_norm, L0_attn_Wq,
                             L0_attn_Wkv, L0_attn_Wout, L0_attn_g_out,
                             L0_ffn_weight)
    nc = build_program(taps=tuple(_TAPS))
    res = run_bass_kernel_spmd(nc, in_maps, core_ids=list(range(8)))
    out = np.empty((4, 2048, 2048), dtype=np.float32)
    for c in range(8):
        b, half = c // 2, c % 2
        o = np.asarray(res.results[c]["outT"])
        out[b, half * 1024:(half + 1) * 1024, :] = (
            o.transpose(2, 0, 1).reshape(1024, 2048))
    if _TAPS:
        kernel._last_results = res
    return out


# version 24
# speedup vs baseline: 1.0723x; 1.0104x over previous
from contextlib import ExitStack

import numpy as np
import ml_dtypes
import concourse.bass as bass
import concourse.mybir as mybir
from concourse.bass_utils import run_bass_kernel_spmd

F32 = mybir.dt.float32
BF16 = mybir.dt.bfloat16
FP8 = mybir.dt.float8e4
U8 = mybir.dt.uint8
AF = mybir.ActivationFunctionType
ALU = mybir.AluOpType
AX = mybir.AxisListType

P = 128
NF = 16
NH = 16
NJ = 16
NB = 4
RB = 512
EPS = 1e-5
L2EPS = 1e-12
INV2048 = 1.0 / 2048.0
INV448 = 1.0 / 448.0

_TAPS = []


class Led:
    def __init__(self):
        self.n = {}

    def inc(self, s, k=1):
        self.n[s] = self.n.get(s, 0) + k
        return self.n[s]

    def val(self, s):
        return self.n.get(s, 0)


def build_program(taps=()):
    nc = bass.Bass("TRN2", target_bir_lowering=False, detect_race_conditions=False)

    xblk = nc.dram_tensor("xblk", [NB, P, NF, RB], F32, kind="ExternalInput")
    Wq_t = nc.dram_tensor("Wq_t", [4, NF, P, 512], F32, kind="ExternalInput")
    Wkv_t = nc.dram_tensor("Wkv_t", [NF, P, 256], F32, kind="ExternalInput")
    Wo_t = nc.dram_tensor("Wo_t", [4, NF, P, 512], F32, kind="ExternalInput")
    WtT = nc.dram_tensor("WtT", [NF, P, 2048], BF16, kind="ExternalInput")
    gN2 = nc.dram_tensor("gN2", [P, NF], F32, kind="ExternalInput")
    gO2 = nc.dram_tensor("gO2", [P, NF], F32, kind="ExternalInput")
    iden = nc.dram_tensor("iden", [P, P], F32, kind="ExternalInput")

    outT = nc.dram_tensor("outT", [NF, P, 1024], F32, kind="ExternalOutput")

    w8d = nc.dram_tensor("w8d", [4, NF // 2, P, 1024], FP8)
    ccin = nc.dram_tensor("ccin", [1, 1], F32)
    ccout = nc.dram_tensor("ccout", [1, 1], F32)
    agin = nc.dram_tensor("agin", [P, 2048], F32)
    agout = nc.dram_tensor("agout", [2, P, 2048], F32)

    tap_dram = {}
    for t in taps:
        shp, dt = {
            "kT": ([P, 2048], F32), "v": ([P, NJ, P], F32),
            "qn": ([P, NH, 2 * RB], F32), "att": ([P, NF, 2 * RB], F32),
            "C": ([P, NF, 1024], F32), "xq": ([P, NF, 2 * RB], BF16),
            "stat1": ([P, 8, RB], F32), "w8chk": ([4, NF, P, 512], BF16),
            "scal": ([P, 8], F32), "tiny": ([1, 8], F32),
            "Cmid": ([P, NF, 1024], F32),
        }[t]
        tap_dram[t] = nc.dram_tensor("tap_" + t, shp, dt, kind="ExternalOutput")

    led = Led()

    with ExitStack() as stx:
        e = stx.enter_context

        LDKEYS = ("xb0", "xb1", "xg0", "xg1", "w0", "w1",
                  "kv0", "kv1", "kv2", "kv3",
                  "wt0", "wt1", "wt2", "wt3",
                  "w80", "w81", "w82", "w83", "cst", "cc2",
                  "cc3", "agp", "agf")
        LD = {k: e(nc.semaphore("ld_" + k)) for k in LDKEYS}
        so = e(nc.semaphore("so"))
        pe = e(nc.semaphore("pe"))
        ve = e(nc.semaphore("ve"))
        ac = e(nc.semaphore("ac"))
        cc = e(nc.semaphore("cc"))
        SEMS = {"so": so, "pe": pe, "ve": ve, "ac": ac}
        for k in LDKEYS:
            SEMS["ld_" + k] = LD[k]

        sI = e(nc.sbuf_tensor("sI", [P, P], F32))
        sOne = e(nc.sbuf_tensor("sOne", [P, P], F32))
        sGN = e(nc.sbuf_tensor("sGN", [P, NF], F32))
        sGO = e(nc.sbuf_tensor("sGO", [P, NF], F32))
        SCAL = e(nc.sbuf_tensor("SCAL", [P, 8], F32))
        tiny = e(nc.sbuf_tensor("tiny", [1, 8], F32))
        amc = e(nc.sbuf_tensor("amc", [P, 2], F32))
        amcols = e(nc.sbuf_tensor("amcols", [P, 100], F32))
        A = e(nc.sbuf_tensor("A", [P, NH, RB], F32))
        B = e(nc.sbuf_tensor("B", [P, NF, RB], F32))
        C = e(nc.sbuf_tensor("C", [P, NF, 1024], F32))
        S = e(nc.sbuf_tensor("S", [P, 5, RB], F32))
        SQ = e(nc.sbuf_tensor("SQ", [P, 2, RB], F32))
        Smu_o = e(nc.sbuf_tensor("Smu_o", [P, 2, RB], F32))
        Sr12_o = e(nc.sbuf_tensor("Sr12_o", [P, 2, RB], F32))
        Smu3 = e(nc.sbuf_tensor("Smu3", [P, 2, RB], F32))
        Srho3 = e(nc.sbuf_tensor("Srho3", [P, 2, RB], F32))
        xbig = e(nc.sbuf_tensor("xbig", [P, 4, RB], F32))
        wst = e(nc.sbuf_tensor("wst", [P, 2, 512], F32))
        wkvst = e(nc.sbuf_tensor("wkvst", [P, 4, 256], F32))
        wtst = e(nc.sbuf_tensor("wtst", [P, 4, 512], BF16))
        w8st = e(nc.sbuf_tensor("w8st", [P, 3, 1024], FP8))
        qb1 = e(nc.sbuf_tensor("qb1", [P, 512], BF16))
        w8c = e(nc.sbuf_tensor("w8c", [P, 4, 512], FP8))

        att_stack = ExitStack()
        kT = att_stack.enter_context(nc.sbuf_tensor("kT", [P, 2048], F32))
        vv = att_stack.enter_context(nc.sbuf_tensor("vv", [P, NJ, P], F32))
        ptil = att_stack.enter_context(nc.sbuf_tensor("ptil", [P, 2, RB], F32))

        ps = [e(nc.psum_tensor(f"ps{i}", [P, 512], F32)) for i in range(8)]

        block = e(nc.Block())

        def on(engname, fn):
            getattr(block, engname)(fn)

        def dma_in(q, stream, out_ap, in_ap):
            q.dma_start(out=out_ap, in_=in_ap).then_inc(LD[stream], 16)
            return led.inc("ld_" + stream, 16)

        def dma_out(q, out_ap, in_ap):
            q.dma_start(out=out_ap, in_=in_ap).then_inc(so, 16)
            return led.inc("so", 16)

        def emit_barrier():
            bv = {s: led.val(s) for s in SEMS}

            def mk(eng):
                for nm, val in bv.items():
                    if val:
                        eng.wait_ge(SEMS[nm], val)
            for en in ("sync", "tensor", "vector", "scalar"):
                on(en, mk)

        def emit_rsqrt(vio, rout, epsb, t1):
            def dv0(v):
                v.tensor_single_scalar(vio, vio, float(epsb), ALU.add).then_inc(ve, 1)
            on("vector", dv0)
            v0 = led.inc("ve")

            def sc0(s):
                b0 = led.val("ac")
                s.wait_ge(ve, v0)
                s.activation(t1, vio, AF.Ln).then_inc(ac, 1)
                s.wait_ge(ac, b0 + 1)
                s.activation(rout, t1, AF.Exp, scale=-0.5).then_inc(ac, 1)
            on("scalar", sc0)
            led.inc("ac")
            a0 = led.inc("ac")

            def dv1(v):
                v.wait_ge(ac, a0)
                v.tensor_tensor(t1, rout, rout, ALU.mult)
                v.scalar_tensor_tensor(t1, t1, -0.5, vio, ALU.mult, ALU.mult)
                v.tensor_single_scalar(t1, t1, 1.5, ALU.add)
                v.tensor_tensor(rout, rout, t1, ALU.mult).then_inc(ve, 1)
            on("vector", dv1)
            return led.inc("ve")

        def emit_recip(vin, rout, t0, t1):
            def sc0(s):
                b0 = led.val("ac")
                s.wait_ge(ve, led.val("ve"))
                s.wait_ge(pe, led.val("pe"))
                s.activation(t0, vin, AF.Ln).then_inc(ac, 1)
                s.wait_ge(ac, b0 + 1)
                s.activation(rout, t0, AF.Exp, scale=-1.0).then_inc(ac, 1)
            on("scalar", sc0)
            led.inc("ac")
            a0 = led.inc("ac")

            def dv1(v):
                v.wait_ge(ac, a0)
                v.tensor_tensor(t1, vin, rout, ALU.mult)
                v.tensor_scalar(t1, t1, -1.0, 2.0, ALU.mult, ALU.add)
                v.tensor_tensor(rout, rout, t1, ALU.mult).then_inc(ve, 1)
            on("vector", dv1)
            return led.inc("ve")

        def emit_quant(v, src_ap, dst, rs2_col):
            return v.tensor_scalar(dst, src_ap, rs2_col, None, ALU.mult)

        def p0_sync(q):
            dma_in(q, "cst", sI[:, :], iden[:, :])
            dma_in(q, "cst", sGN[:, :], gN2[:, :])
            dma_in(q, "cst", sGO[:, :], gO2[:, :])
        on("sync", p0_sync)
        ld_cst = led.val("ld_cst")

        def p0_act(s):
            from concourse.hw_specs import get_activation_tables
            idx = None
            for i, fns in enumerate(get_activation_tables(nc.m.arch).values()):
                names = {f.name for f in fns}
                if {"Exp", "Ln", "Square"} <= names:
                    idx = i
                    break
            assert idx is not None
            s.add_instruction(mybir.InstLoadActFuncSet(
                name=nc.get_next_instruction_name(),
                act_func_set_id=idx, ins=[], outs=[]))
        on("scalar", p0_act)

        def p0_vec(v):
            v.memset(sOne[:, :], 1.0)
            v.memset(tiny[:, :], 0.0)
            v.memset(SCAL[:, :], 0.0)
            v.memset(amc[:, :], 0.0).then_inc(ve, 1)
        on("vector", p0_vec)
        led.inc("ve")

        pw_units = []
        wt_gate = {0: 0, 1: 0, 2: 0, 3: 0}
        w8c_gate = {0: 0, 1: 0, 2: 0, 3: 0}
        pw_state = {"scal": None}

        def mk_amax_unit(t, qq):
            def unit():
                slot = (t * 4 + qq) % 4
                g = wt_gate[slot]

                def ld_(q):
                    if g:
                        q.wait_ge(ve, g)
                    dma_in(q, "wt%d" % slot, wtst[:, slot, :],
                           WtT[t, :, qq * 512:(qq + 1) * 512])
                on("sync", ld_)
                t_ld = led.val("ld_wt%d" % slot)

                def red_(v):
                    v.wait_ge(LD["wt%d" % slot], t_ld)
                    v.reduce_max(amcols[:, t * 4 + qq:t * 4 + qq + 1],
                                 wtst[:, slot, :], AX.X,
                                 apply_absolute_value=True).then_inc(ve, 1)
                on("vector", red_)
                wt_gate[slot] = led.inc("ve")
            return unit

        def pw_combine_unit():
            ve_hop = led.val("ve")

            def pw_comb(v):
                v.wait_ge(ve, ve_hop)
                v.reduce_max(amc[:, 0:1], amcols[:, 0:64], AX.X).then_inc(ve, 1)
            on("vector", pw_comb)
            ve_am = led.inc("ve")

            def pw_tr(tn):
                tn.wait_ge(ve, ve_am)
                tn.wait_ge(LD["cst"], ld_cst)
                tn.transpose(ps[7][0:1, 0:P], amc[:, 0:1],
                             sI[:, :]).then_inc(pe, 1)
            on("tensor", pw_tr)
            pe_tr = led.inc("pe")

            def pw_sw(v):
                b0 = led.val("ve")
                v.wait_ge(pe, pe_tr)
                v.reduce_max(tiny[0:1, 0:1], ps[7][0:1, 0:P],
                             AX.X).then_inc(ve, 1)
                v.wait_ge(ve, b0 + 1)
                v.tensor_single_scalar(tiny[0:1, 0:1], tiny[0:1, 0:1], INV448,
                                       ALU.mult).then_inc(ve, 1)
                v.wait_ge(ve, b0 + 2)
                v.tensor_single_scalar(tiny[0:1, 0:1], tiny[0:1, 0:1], 1e-12,
                                       ALU.max).then_inc(ve, 1)
                v.wait_ge(ve, b0 + 3)
                v.reciprocal(amcols[0:1, 96:97], tiny[0:1, 0:1]).then_inc(ve, 1)
                v.wait_ge(ve, b0 + 4)
                v.tensor_single_scalar(amcols[0:1, 97:98], amcols[0:1, 96:97],
                                       0.5, ALU.mult).then_inc(ve, 1)
            on("vector", pw_sw)
            for _ in range(4):
                led.inc("ve")
            ve_sw = led.inc("ve")

            def pw_bc(tn):
                tn.wait_ge(ve, ve_sw)
                tn.matmul(ps[7][0:P, 0:2], sOne[0:1, 0:P], amcols[0:1, 96:98],
                          start=True, stop=True).then_inc(pe, 1)
            on("tensor", pw_bc)
            pe_bc = led.inc("pe")

            def pw_bc_ev(v):
                v.wait_ge(pe, pe_bc)
                v.tensor_copy(SCAL[:, 0:2], ps[7][0:P, 0:2]).then_inc(ve, 1)
            on("vector", pw_bc_ev)
            pw_state["scal"] = led.inc("ve")

        def mk_quant_unit(t, qq):
            def unit():
                slot = (t * 4 + qq) % 4
                g = wt_gate[slot]
                sog = w8c_gate[slot]

                def ld_(q):
                    if g:
                        q.wait_ge(ve, g)
                    dma_in(q, "wt%d" % slot, wtst[:, slot, :],
                           WtT[t, :, qq * 512:(qq + 1) * 512])
                on("sync", ld_)
                t_ld = led.val("ld_wt%d" % slot)

                def q_(v):
                    v.wait_ge(LD["wt%d" % slot], t_ld)
                    v.wait_ge(ve, pw_state["scal"])
                    if sog:
                        v.wait_ge(so, sog)
                    emit_quant(v, wtst[:, slot, :], w8c[:, slot, :],
                               SCAL[:, 1:2]).then_inc(ve, 1)
                on("vector", q_)
                wt_gate[slot] = ve_q = led.inc("ve")

                def st_(q):
                    q.wait_ge(ve, ve_q)
                    dma_out(q, w8d[qq, t // 2, :,
                                   (t % 2) * 512:(t % 2) * 512 + 512],
                            w8c[:, slot, :])
                on("sync", st_)
                w8c_gate[slot] = led.val("so")
            return unit

        for t in range(NF):
            for qq in range(4):
                pw_units.append(mk_amax_unit(t, qq))
        pw_units.append(pw_combine_unit)
        for t in range(NF):
            for qq in range(4):
                pw_units.append(mk_quant_unit(t, qq))

        def pump(n):
            for _ in range(n):
                if pw_units:
                    pw_units.pop(0)()

        xgate = [None, None]

        def xwait(q, sl):
            if xgate[sl] is not None:
                gp, ga, gv = xgate[sl]
                q.wait_ge(pe, gp)
                q.wait_ge(ac, ga)
                q.wait_ge(ve, gv)

        def xmark(sl):
            xgate[sl] = (led.val("pe"), led.val("ac"), led.val("ve"))

        for blk in (2, 3):
            mu_dst = Smu_o[:, blk - 2, :]
            r12_dst = Sr12_o[:, blk - 2, :]

            for g2 in range(8):
                sl = g2 % 2

                def p1_ld(q, blk=blk, g2=g2, sl=sl):
                    xwait(q, sl)
                    dma_in(q, "xb%d" % sl, xbig[:, sl * 2:sl * 2 + 2, :],
                           xblk[blk, :, g2 * 2:(g2 + 1) * 2, :])
                on("sync", p1_ld)
                t_ld = led.val("ld_xb%d" % sl)
                pe_base = led.val("pe")

                def p1_sq(s, sl=sl, t_ld=t_ld, pe_base=pe_base):
                    s.wait_ge(LD["xb%d" % sl], t_ld)
                    s.wait_ge(pe, pe_base)
                    for i in range(2):
                        s.activation(SQ[:, i, :], xbig[:, sl * 2 + i, :],
                                     AF.Square).then_inc(ac, 1)
                on("scalar", p1_sq)
                ac_marks = [led.inc("ac") for _ in range(2)]

                ve_emit = led.val("ve")

                def p1_mm(tn, g2=g2, sl=sl, t_ld=t_ld, ac_marks=ac_marks,
                          ve_emit=ve_emit):
                    tn.wait_ge(LD["xb%d" % sl], t_ld)
                    if g2 == 0:
                        tn.wait_ge(ve, ve_emit)
                    for i in range(2):
                        fc = g2 * 2 + i
                        tn.matmul(ps[0][:, :], sOne[:, :], xbig[:, sl * 2 + i, :],
                                  start=(fc == 0), stop=(fc == NF - 1))
                        tn.wait_ge(ac, ac_marks[i])
                        tn.matmul(ps[1][:, :], sOne[:, :], SQ[:, i, :],
                                  start=(fc == 0),
                                  stop=(fc == NF - 1)).then_inc(pe, 1)
                on("tensor", p1_mm)
                for _ in range(2):
                    led.inc("pe")
                xmark(sl)
                pump(4)

            pe_stats = led.val("pe")

            def p1_sc(v, mu_dst=mu_dst, pe_stats=pe_stats):
                v.wait_ge(pe, pe_stats)
                v.tensor_scalar(mu_dst, ps[0][:, :], INV2048, None, ALU.mult)
                v.tensor_scalar(S[:, 1, :], ps[1][:, :], INV2048, None, ALU.mult)
                v.tensor_tensor(S[:, 2, :], mu_dst, mu_dst, ALU.mult)
                v.tensor_tensor(S[:, 2, :], S[:, 1, :], S[:, 2, :],
                                ALU.subtract)
                v.tensor_single_scalar(S[:, 4, :], S[:, 2, :], 1.0,
                                       ALU.mult).then_inc(ve, 1)
            on("vector", p1_sc)
            led.inc("ve")
            emit_rsqrt(S[:, 4, :], S[:, 3, :], EPS, S[:, 1, :])

            def p1_sc2(v):
                v.tensor_tensor(S[:, 1, :], S[:, 3, :], S[:, 3, :], ALU.mult)
                v.tensor_tensor(S[:, 1, :], S[:, 1, :], S[:, 2, :],
                                ALU.mult).then_inc(ve, 1)
            on("vector", p1_sc2)
            led.inc("ve")
            emit_rsqrt(S[:, 1, :], S[:, 4, :], EPS, S[:, 2, :])

            def p1_sc3(v, r12_dst=r12_dst):
                v.tensor_tensor(r12_dst, S[:, 3, :], S[:, 4, :],
                                ALU.mult).then_inc(ve, 1)
            on("vector", p1_sc3)
            ve_r12 = led.inc("ve")

            for g2 in range(8):
                sl = g2 % 2

                def p1_ld2(q, blk=blk, g2=g2, sl=sl):
                    xwait(q, sl)
                    dma_in(q, "xb%d" % sl, xbig[:, sl * 2:sl * 2 + 2, :],
                           xblk[blk, :, g2 * 2:(g2 + 1) * 2, :])
                    for i in range(2):
                        fc = g2 * 2 + i
                        dma_in(q, "kv%d" % (fc % 4), wkvst[:, fc % 4, :],
                               Wkv_t[fc, :, :])
                on("sync", p1_ld2)
                t_ld = led.val("ld_xb%d" % sl)
                t_kv = {i: led.val("ld_kv%d" % i) for i in range(4)}
                pe_base = led.val("pe")

                if blk == 2:
                    xn2dst = [B[:, g2 * 2, :], B[:, g2 * 2 + 1, :]]
                else:
                    xn2dst = [C[:, g2 * 2, RB:2 * RB],
                              C[:, g2 * 2 + 1, RB:2 * RB]]

                def p1_ap(v, g2=g2, sl=sl, t_ld=t_ld, ve_r12=ve_r12,
                          pe_base=pe_base, mu_dst=mu_dst, r12_dst=r12_dst,
                          xn2dst=xn2dst):
                    v.wait_ge(LD["xb%d" % sl], t_ld)
                    v.wait_ge(ve, ve_r12)
                    v.wait_ge(pe, pe_base)
                    for i in range(2):
                        fc = g2 * 2 + i
                        v.tensor_tensor(S[:, 0, :], xbig[:, sl * 2 + i, :],
                                        mu_dst, ALU.subtract)
                        v.scalar_tensor_tensor(
                            xn2dst[i], S[:, 0, :], sGN[:, fc:fc + 1],
                            r12_dst, ALU.mult, ALU.mult).then_inc(ve, 1)
                on("vector", p1_ap)
                ve_marks = [led.inc("ve") for _ in range(2)]

                def p1_kv(tn, g2=g2, t_kv=t_kv, ve_marks=ve_marks,
                          xn2dst=xn2dst):
                    for i in range(2):
                        fc = g2 * 2 + i
                        tn.wait_ge(LD["kv%d" % (fc % 4)], t_kv[fc % 4])
                        tn.wait_ge(ve, ve_marks[i])
                        tn.matmul(ps[2][:, :], wkvst[:, fc % 4, 0:128],
                                  xn2dst[i],
                                  start=(fc == 0), stop=(fc == NF - 1))
                        tn.matmul(ps[3][:, :], wkvst[:, fc % 4, 128:256],
                                  xn2dst[i],
                                  start=(fc == 0),
                                  stop=(fc == NF - 1)).then_inc(pe, 1)
                on("tensor", p1_kv)
                for _ in range(2):
                    led.inc("pe")
                xmark(sl)
                pump(4)

            pe_kv = led.val("pe")

            def p1_ev(v, blk=blk, pe_kv=pe_kv):
                v.wait_ge(pe, pe_kv)
                v.tensor_copy(kT[:, (blk - 2) * RB:(blk - 1) * RB],
                              ps[2][:, :])
                v.tensor_copy(SQ[:, 0, :], ps[3][:, :]).then_inc(ve, 1)
            on("vector", p1_ev)
            ve_ev = led.inc("ve")

            ve_trbase = led.val("ve")
            for jj in range(4):
                jc = (blk - 2) * 4 + jj

                def p1_tr(tn, jj=jj, ve_ev=ve_ev, ve_trbase=ve_trbase):
                    tn.wait_ge(ve, ve_ev)
                    if jj >= 1:
                        tn.wait_ge(ve, ve_trbase + jj)
                    tn.transpose(ps[4][:, 0:128],
                                 SQ[:, 0, jj * 128:(jj + 1) * 128],
                                 sI[:, :]).then_inc(pe, 1)
                on("tensor", p1_tr)
                pe_t = led.inc("pe")

                def p1_trev(v, jc=jc, pe_t=pe_t):
                    v.wait_ge(pe, pe_t)
                    v.tensor_copy(vv[:, jc, :], ps[4][:, 0:128]).then_inc(ve, 1)
                on("vector", p1_trev)
                led.inc("ve")

        pump(len(pw_units))

        ve_kl0 = led.val("ve")
        pe_kl0 = led.val("pe")
        klm = []
        for b4 in range(2):
            def kl_sq(s, b4=b4, ve_kl0=ve_kl0, pe_kl0=pe_kl0):
                if b4 == 0:
                    s.wait_ge(ve, ve_kl0)
                    s.wait_ge(pe, pe_kl0)
                s.activation(ptil[:, b4, :], kT[:, b4 * RB:(b4 + 1) * RB],
                             AF.Square).then_inc(ac, 1)
            on("scalar", kl_sq)
            ac_m = led.inc("ac")

            def kl_mm(tn, b4=b4, ac_m=ac_m, ve_kl0=ve_kl0):
                tn.wait_ge(ac, ac_m)
                tn.wait_ge(ve, ve_kl0)
                tn.matmul(ps[4 + b4][:, :], sOne[:, :], ptil[:, b4, :],
                          start=True, stop=True).then_inc(pe, 1)
            on("tensor", kl_mm)
            klm.append(led.inc("pe"))

        def kl_ev(v, pe_l=klm[1]):
            v.wait_ge(pe, pe_l)
            v.tensor_copy(xbig[:, 0, :], ps[4][:, :])
            v.tensor_copy(xbig[:, 1, :], ps[5][:, :])
            v.tensor_single_scalar(xbig[:, 0:2, :], xbig[:, 0:2, :],
                                   float(L2EPS), ALU.add).then_inc(ve, 1)
        on("vector", kl_ev)
        kl_v0 = led.inc("ve")

        def kl_ln(s, v0=kl_v0):
            b0 = led.val("ac")
            s.wait_ge(ve, v0)
            s.activation(xbig[:, 2:4, :], xbig[:, 0:2, :],
                         AF.Ln).then_inc(ac, 1)
            s.wait_ge(ac, b0 + 1)
            s.activation(xbig[:, 2:4, :], xbig[:, 2:4, :], AF.Exp,
                         scale=-0.5).then_inc(ac, 1)
        on("scalar", kl_ln)
        led.inc("ac")
        kl_a0 = led.inc("ac")

        def kl_nw(v, a0=kl_a0):
            v.wait_ge(ac, a0)
            v.tensor_tensor(xbig[:, 0:2, :], xbig[:, 0:2, :],
                            xbig[:, 2:4, :], ALU.mult)
            v.tensor_tensor(xbig[:, 0:2, :], xbig[:, 0:2, :],
                            xbig[:, 2:4, :], ALU.mult)
            v.tensor_scalar(xbig[:, 0:2, :], xbig[:, 0:2, :], -0.5, 1.5,
                            ALU.mult, ALU.add)
            v.tensor_tensor(xbig[:, 2:4, :], xbig[:, 2:4, :],
                            xbig[:, 0:2, :], ALU.mult)
            v.tensor_single_scalar(xbig[:, 2:4, :], xbig[:, 2:4, :], 4.0,
                                   ALU.mult)
            for b4 in range(2):
                ins = v.tensor_tensor(kT[:, b4 * RB:(b4 + 1) * RB],
                                      kT[:, b4 * RB:(b4 + 1) * RB],
                                      xbig[:, 2 + b4, :], ALU.mult)
            ins.then_inc(ve, 1)
        on("vector", kl_nw)
        led.inc("ve")

        emit_barrier()
        if "kT" in tap_dram:
            on("sync", lambda q: dma_out(q, tap_dram["kT"][:, :], kT[:, :]))
        if "v" in tap_dram:
            on("sync", lambda q: dma_out(q, tap_dram["v"][:, :, :], vv[:, :, :]))
        if "stat1" in tap_dram:
            def ts1(q):
                dma_out(q, tap_dram["stat1"][:, 0:2, :], Smu_o[:, :, :])
                dma_out(q, tap_dram["stat1"][:, 2:4, :], Sr12_o[:, :, :])
            on("sync", ts1)

        ve_pack = led.val("ve")

        def ag_pack(q, ve_pack=ve_pack):
            q.wait_ge(ve, ve_pack)
            dma_in(q, "agp", agin[:, 0:1024], kT[:, 0:1024])
            dma_in(q, "agp", agin[:, 1024:2048], vv[:, 0:8, :])
        on("sync", ag_pack)
        t_agp = led.val("ld_agp")

        def ag_cc(g, t_agp=t_agp):
            g.wait_ge(LD["agp"], t_agp)
            g.collective_compute(
                "AllGather", ALU.bypass,
                replica_groups=[[2 * i, 2 * i + 1] for i in range(4)],
                ins=[agin.ap().opt()],
                outs=[agout.ap().opt()]).then_inc(cc, 1)
        on("gpsimd", ag_cc)
        led.inc("cc")

        def ag_fetch(g):
            g.wait_ge(cc, 1)
            dma_in(g, "agf", kT[:, 0:1024], agout[0, :, 0:1024])
            dma_in(g, "agf", kT[:, 1024:2048], agout[1, :, 0:1024])
            dma_in(g, "agf", vv[:, 0:8, :], agout[0, :, 1024:2048])
            dma_in(g, "agf", vv[:, 8:16, :], agout[1, :, 1024:2048])
        on("gpsimd", ag_fetch)
        t_agf = led.val("ld_agf")

        def emit_tail(rc, oblk, xfree=None):
            pe_st = led.val("pe")

            def p5_sc(v, pe_st=pe_st):
                v.wait_ge(pe, pe_st)
                v.tensor_scalar(S[:, 1, :], ps[4][:, :], INV2048, None,
                                ALU.mult)
                v.tensor_scalar(S[:, 2, :], ps[5][:, :], INV2048, None,
                                ALU.mult)
                v.tensor_tensor(S[:, 0, :], S[:, 1, :], S[:, 1, :], ALU.mult)
                v.tensor_tensor(S[:, 2, :], S[:, 2, :], S[:, 0, :],
                                ALU.subtract).then_inc(ve, 1)
            on("vector", p5_sc)
            sc_m = led.inc("ve")
            emit_rsqrt(S[:, 2, :], S[:, 3, :], EPS, S[:, 4, :])

            resid_marks = []
            for g2 in range(8):
                sl = g2 % 2

                def p5_xl(g, oblk=oblk, g2=g2, sl=sl, xfree=xfree):
                    xwait(g, sl)
                    if xfree and g2 < 2:
                        g.wait_ge(ve, xfree)
                    dma_in(g, "xg%d" % sl, xbig[:, sl * 2:sl * 2 + 2, :],
                           xblk[oblk, :, g2 * 2:(g2 + 1) * 2, :])
                on("gpsimd", p5_xl)
                t_ld = led.val("ld_xg%d" % sl)

                def p5_res(v, g2=g2, sl=sl, rc=rc, t_ld=t_ld):
                    v.wait_ge(LD["xg%d" % sl], t_ld)
                    for i in range(2):
                        fc = g2 * 2 + i
                        ctile = C[:, fc, rc * RB:(rc + 1) * RB]
                        v.tensor_tensor(S[:, 0, :], ctile, S[:, 1, :],
                                        ALU.subtract)
                        v.scalar_tensor_tensor(S[:, 0, :], S[:, 0, :],
                                               sGO[:, fc:fc + 1], S[:, 3, :],
                                               ALU.mult, ALU.mult)
                        ins = v.tensor_tensor(ctile, xbig[:, sl * 2 + i, :],
                                              S[:, 0, :], ALU.add)
                    ins.then_inc(ve, 1)
                on("vector", p5_res)
                resid_marks.append(led.inc("ve"))
                xmark(sl)

            pe6_base = led.val("pe")
            for fc in range(NF):
                def p6_sq(s, fc=fc, rc=rc, pe6_base=pe6_base,
                          rm=resid_marks[fc // 2]):
                    s.wait_ge(ve, rm)
                    s.wait_ge(pe, pe6_base + max(0, fc - 1))
                    s.activation(ptil[:, fc % 2, :],
                                 C[:, fc, rc * RB:(rc + 1) * RB],
                                 AF.Square).then_inc(ac, 1)
                on("scalar", p6_sq)
                ac_m = led.inc("ac")

                def p6_smm(tn, fc=fc, rc=rc, ac_m=ac_m, sc_m=sc_m,
                           rm=resid_marks[fc // 2]):
                    if fc == 0:
                        tn.wait_ge(ve, sc_m)
                    tn.wait_ge(ve, rm)
                    tn.matmul(ps[4][:, :], sOne[:, :],
                              C[:, fc, rc * RB:(rc + 1) * RB],
                              start=(fc == 0), stop=(fc == NF - 1))
                    tn.wait_ge(ac, ac_m)
                    tn.matmul(ps[5][:, :], sOne[:, :], ptil[:, fc % 2, :],
                              start=(fc == 0),
                              stop=(fc == NF - 1)).then_inc(pe, 1)
                on("tensor", p6_smm)
                led.inc("pe")

            pe_st6 = led.val("pe")

            def p6_sc(v, rc=rc, pe_st6=pe_st6):
                v.wait_ge(pe, pe_st6)
                v.tensor_scalar(Smu3[:, rc, :], ps[4][:, :], INV2048, None,
                                ALU.mult)
                v.tensor_scalar(S[:, 2, :], ps[5][:, :], INV2048, None,
                                ALU.mult)
                v.tensor_tensor(S[:, 0, :], Smu3[:, rc, :], Smu3[:, rc, :],
                                ALU.mult)
                v.tensor_tensor(S[:, 2, :], S[:, 2, :], S[:, 0, :],
                                ALU.subtract).then_inc(ve, 1)
            on("vector", p6_sc)
            led.inc("ve")
            emit_rsqrt(S[:, 2, :], Srho3[:, rc, :], EPS, S[:, 4, :])

            for fc in range(NF):
                def p6_am(v, fc=fc, rc=rc, last=(fc == NF - 1)):
                    v.tensor_tensor(S[:, 0, :],
                                    C[:, fc, rc * RB:(rc + 1) * RB],
                                    Smu3[:, rc, :], ALU.subtract)
                    v.tensor_tensor(S[:, 0, :], S[:, 0, :], Srho3[:, rc, :],
                                    ALU.mult)
                    col = 64 + rc * NF + fc
                    ins = v.reduce_max(amcols[:, col:col + 1], S[:, 0, :],
                                       AX.X, apply_absolute_value=True)
                    if last:
                        ins.then_inc(ve, 1)
                on("vector", p6_am)
            led.inc("ve")

        pending_tail = []
        xbig_free = {}
        for rc in range(2):
            oblk = 2 + rc


            pe_qend_prev = None
            ve_evq_prev = None
            for hg in range(4):
                pe_base = led.val("pe")
                ve_done_prev = ve_evq_prev if hg else led.val("ve")
                pe_wst_prev = pe_qend_prev if hg else pe_base

                for fc in range(NF):
                    def p3_ld(q, hg=hg, fc=fc, pe_base=pe_base,
                              pe_wst_prev=pe_wst_prev):
                        if fc >= 2:
                            q.wait_ge(pe, pe_base + fc - 1)
                        else:
                            q.wait_ge(pe, pe_wst_prev)
                        dma_in(q, "w%d" % (fc % 2), wst[:, fc % 2, :],
                               Wq_t[hg, fc, :, :])
                    on("sync", p3_ld)
                    t_ld = led.val("ld_w%d" % (fc % 2))

                    def p3_mm(tn, fc=fc, rc=rc, t_ld=t_ld,
                              ve_done_prev=ve_done_prev):
                        tn.wait_ge(LD["w%d" % (fc % 2)], t_ld)
                        if fc == 0:
                            tn.wait_ge(ve, ve_done_prev)
                        xn2src = (B[:, fc, :] if rc == 0
                                  else C[:, fc, RB:2 * RB])
                        for hh in range(4):
                            ins = tn.matmul(
                                ps[hh][:, :],
                                wst[:, fc % 2, hh * 128:(hh + 1) * 128],
                                xn2src, start=(fc == 0),
                                stop=(fc == NF - 1))
                        ins.then_inc(pe, 1)
                    on("tensor", p3_mm)
                    led.inc("pe")
                pe_qend_prev = led.val("pe")

                pe_q = led.val("pe")

                def p3_evq(v, hg=hg, pe_q=pe_q):
                    v.wait_ge(pe, pe_q)
                    for hh in range(4):
                        ins = v.tensor_copy(A[:, hg * 4 + hh, :], ps[hh][:, :])
                    ins.then_inc(ve, 1)
                on("vector", p3_evq)
                ve_evq_prev = led.inc("ve")

                if rc == 0:
                    lad = [(xbig[:, 0:2, :], xbig[:, 2:4, :], 0),
                           (Smu_o[:, :, :], Sr12_o[:, :, :], 2)]
                    banks = (4, 5, 6, 7)
                else:
                    lad = [(B[:, 0:2, :], B[:, 2:4, :], 0),
                           (Smu_o[:, :, :], Sr12_o[:, :, :], 2)]
                    banks = (6, 7, 6, 7)
                ve_evq = led.val("ve")
                pe_now3 = led.val("pe")
                ve_lfree = led.val("ve")
                mm_marks = []
                v0m = []
                for li, (vio, rout, hb) in enumerate(lad):
                    for j in range(2):
                        hh = li * 2 + j
                        h = hg * 4 + hh

                        def p3_sq(s, h=h, hh=hh, ve_evq=ve_evq,
                                  pe_now3=pe_now3, mm_marks=tuple(mm_marks)):
                            if hh == 0:
                                s.wait_ge(ve, ve_evq)
                                s.wait_ge(pe, pe_now3)
                            if hh >= 2:
                                s.wait_ge(pe, mm_marks[hh - 2])
                            s.activation(SQ[:, hh % 2, :], A[:, h, :],
                                         AF.Square).then_inc(ac, 1)
                        on("scalar", p3_sq)
                        ac_m = led.inc("ac")

                        def p3_lmm(tn, ac_m=ac_m, hh=hh, bank=banks[hh],
                                   ve_lfree=ve_lfree,
                                   ev1=(v0m[0] if v0m else None)):
                            tn.wait_ge(ac, ac_m)
                            if hh == 0:
                                tn.wait_ge(ve, ve_lfree)
                            if hh == 2 and ev1 is not None and bank == banks[0]:
                                tn.wait_ge(ve, ev1)
                            tn.matmul(ps[bank][:, :], sOne[:, :],
                                      SQ[:, hh % 2, :],
                                      start=True, stop=True).then_inc(pe, 1)
                        on("tensor", p3_lmm)
                        mm_marks.append(led.inc("pe"))

                    def p3_ev(v, vio=vio, b0=banks[li * 2], b1=banks[li * 2 + 1],
                              pe_l=mm_marks[-1]):
                        v.wait_ge(pe, pe_l)
                        v.tensor_copy(vio[:, 0, :], ps[b0][:, :])
                        v.tensor_copy(vio[:, 1, :], ps[b1][:, :])
                        v.tensor_single_scalar(vio, vio,
                                               float(L2EPS / 256.0),
                                               ALU.add).then_inc(ve, 1)
                    on("vector", p3_ev)
                    v0m.append(led.inc("ve"))

                a0m = []
                for li, (vio, rout, hb) in enumerate(lad):
                    def p3_ln(s, vio=vio, rout=rout, v0=v0m[li]):
                        b0 = led.val("ac")
                        s.wait_ge(ve, v0)
                        s.activation(rout, vio, AF.Ln).then_inc(ac, 1)
                        s.wait_ge(ac, b0 + 1)
                        s.activation(rout, rout, AF.Exp,
                                     scale=-0.5).then_inc(ac, 1)
                    on("scalar", p3_ln)
                    led.inc("ac")
                    a0m.append(led.inc("ac"))

                for li, (vio, rout, hb) in enumerate(lad):
                    def p3_nw(v, vio=vio, rout=rout, hb=hb, hg=hg,
                              a0=a0m[li]):
                        v.wait_ge(ac, a0)
                        v.tensor_tensor(vio, vio, rout, ALU.mult)
                        v.tensor_tensor(vio, vio, rout, ALU.mult)
                        v.tensor_scalar(vio, vio, -0.5, 1.5,
                                        ALU.mult, ALU.add)
                        v.tensor_tensor(rout, rout, vio, ALU.mult)
                        v.tensor_single_scalar(rout, rout, 4.0, ALU.mult)
                        for hh in range(2):
                            h = hg * 4 + hb + hh
                            ins = v.tensor_tensor(A[:, h, :], A[:, h, :],
                                                  rout[:, hh, :], ALU.mult)
                        ins.then_inc(ve, 1)
                    on("vector", p3_nw)
                    led.inc("ve")

            if rc == 0:
                xbig_free[0] = led.val("ve")
            elif pending_tail:
                pending_tail.pop(0)()

            emit_barrier()
            if "qn" in tap_dram:
                def tqn(q, rc=rc):
                    dma_out(q, tap_dram["qn"][:, :, rc * RB:(rc + 1) * RB],
                            A[:, :, :])
                on("sync", tqn)

            for h in range(NH):
                pe_base = led.val("pe")
                ac_base = led.val("ac")
                ve_att_prev = led.val("ve")

                plan = ["sim0", "sim1"]
                for jc in range(NJ):
                    plan.append("sa%d" % jc)
                    if jc + 2 < NJ:
                        plan.append("sim%d" % (jc + 2))
                sim_mark, av_mark = {}, {}
                m = pe_base
                for stp in plan:
                    m += 1
                    if stp.startswith("sim"):
                        sim_mark[int(stp[3:])] = m
                    else:
                        av_mark[int(stp[2:])] = m

                def p4_pe(tn, h=h, rc=rc, pe_base=pe_base, ac_base=ac_base,
                          ve_att_prev=ve_att_prev, plan=plan):
                    if h == 0 and rc == 0:
                        tn.wait_ge(LD["agf"], t_agf)
                    for stp in plan:
                        if stp.startswith("sim"):
                            jc = int(stp[3:])
                            if jc < 2:
                                tn.wait_ge(ac, ac_base)
                            else:
                                tn.wait_ge(ac, ac_base + jc - 1)
                            tn.matmul(ps[jc % 2][:, :],
                                      kT[:, jc * 128:(jc + 1) * 128],
                                      A[:, h, :], start=True,
                                      stop=True).then_inc(pe, 1)
                        else:
                            jc = int(stp[2:])
                            if jc == 0:
                                tn.wait_ge(ve, ve_att_prev)
                            tn.wait_ge(ac, ac_base + jc + 1)
                            tn.matmul(ps[3][:, :], sOne[:, :],
                                      ptil[:, jc % 2, :],
                                      start=(jc == 0), stop=(jc == NJ - 1))
                            tn.matmul(ps[2][:, :], vv[:, jc, :],
                                      ptil[:, jc % 2, :],
                                      start=(jc == 0),
                                      stop=(jc == NJ - 1)).then_inc(pe, 1)
                on("tensor", p4_pe)
                for _ in range(2 * NJ):
                    led.inc("pe")
                assert led.val("pe") == m

                def p4_exp(s, sim_mark=sim_mark, av_mark=av_mark):
                    for jc in range(NJ):
                        s.wait_ge(pe, sim_mark[jc])
                        if jc >= 2:
                            s.wait_ge(pe, av_mark[jc - 2])
                        s.activation(ptil[:, jc % 2, :], ps[jc % 2][:, :],
                                     AF.Exp).then_inc(ac, 1)
                on("scalar", p4_exp)
                exp_marks = [led.inc("ac") for _ in range(NJ)]
                assert exp_marks[-1] == ac_base + NJ

                pe_last = av_mark[NJ - 1]

                def p4_cp(v, pe_last=pe_last):
                    v.wait_ge(pe, pe_last)
                    v.tensor_copy(S[:, 2, :], ps[3][:, :]).then_inc(ve, 1)
                on("vector", p4_cp)
                led.inc("ve")
                emit_recip(S[:, 2, :], S[:, 1, :], S[:, 4, :], S[:, 0, :])

                def p4_out(v, h=h):
                    v.tensor_tensor(B[:, h, :], ps[2][:, :], S[:, 1, :],
                                    ALU.mult).then_inc(ve, 1)
                on("vector", p4_out)
                led.inc("ve")

            emit_barrier()
            if "att" in tap_dram:
                def tat(q, rc=rc):
                    dma_out(q, tap_dram["att"][:, :, rc * RB:(rc + 1) * RB],
                            B[:, :, :])
                on("sync", tat)

            ve_p5start = led.val("ve")
            sqg = {}
            for fg in range(4):
                pe_base = led.val("pe")
                ve_prev = led.val("ve")
                for hdc in range(NF):
                    def p5_ld(q, fg=fg, hdc=hdc, pe_base=pe_base,
                              ve_prev=ve_prev):
                        if hdc >= 2:
                            q.wait_ge(pe, pe_base + hdc - 1)
                        if hdc == 0:
                            q.wait_ge(ve, ve_prev)
                        dma_in(q, "w%d" % (hdc % 2), wst[:, hdc % 2, :],
                               Wo_t[fg, hdc, :, :])
                    on("sync", p5_ld)
                    t_ld = led.val("ld_w%d" % (hdc % 2))

                    def p5_mm(tn, hdc=hdc, t_ld=t_ld):
                        tn.wait_ge(LD["w%d" % (hdc % 2)], t_ld)
                        for ff in range(4):
                            ins = tn.matmul(
                                ps[ff][:, :],
                                wst[:, hdc % 2, ff * 128:(ff + 1) * 128],
                                B[:, hdc, :], start=(hdc == 0),
                                stop=(hdc == NF - 1))
                        ins.then_inc(pe, 1)
                    on("tensor", p5_mm)
                    led.inc("pe")

                pe_fg = led.val("pe")

                def p5_ev(v, fg=fg, rc=rc, pe_fg=pe_fg):
                    v.wait_ge(pe, pe_fg)
                    for ff in range(4):
                        ins = v.tensor_copy(
                            C[:, fg * 4 + ff, rc * RB:(rc + 1) * RB],
                            ps[ff][:, :])
                    ins.then_inc(ve, 1)
                on("vector", p5_ev)
                ev_m = led.inc("ve")

                for ff in range(4):
                    fc = fg * 4 + ff

                    def p5_sq(s, fc=fc, rc=rc, ev_m=ev_m, g=sqg.get(fc - 2)):
                        s.wait_ge(ve, ev_m)
                        if g:
                            s.wait_ge(pe, g)
                        s.activation(SQ[:, fc % 2, :],
                                     C[:, fc, rc * RB:(rc + 1) * RB],
                                     AF.Square).then_inc(ac, 1)
                    on("scalar", p5_sq)
                    ac_m = led.inc("ac")

                    def p5_smm(tn, fc=fc, rc=rc, ac_m=ac_m, ev_m=ev_m,
                               ve_p5start=ve_p5start):
                        if fc == 0:
                            tn.wait_ge(ve, ve_p5start)
                        tn.wait_ge(ve, ev_m)
                        tn.matmul(ps[4][:, :], sOne[:, :],
                                  C[:, fc, rc * RB:(rc + 1) * RB],
                                  start=(fc == 0), stop=(fc == NF - 1))
                        tn.wait_ge(ac, ac_m)
                        tn.matmul(ps[5][:, :], sOne[:, :], SQ[:, fc % 2, :],
                                  start=(fc == 0),
                                  stop=(fc == NF - 1)).then_inc(pe, 1)
                    on("tensor", p5_smm)
                    sqg[fc] = led.inc("pe")

            if rc == 0:
                pending_tail.append(
                    lambda oblk=oblk: emit_tail(0, oblk, xbig_free[0]))
            else:
                emit_tail(1, oblk)

            emit_barrier()

        if "Cmid" in tap_dram:
            on("sync",
               lambda q: dma_out(q, tap_dram["Cmid"][:, :, :], C[:, :, :]))

        att_stack.close()
        xq = e(nc.sbuf_tensor("xq", [P, 2, NF, RB], FP8))

        ve_hop2 = led.val("ve")

        def p6_comb(v, ve_hop2=ve_hop2):
            v.wait_ge(ve, ve_hop2)
            v.reduce_max(amc[:, 1:2], amcols[:, 64:96], AX.X).then_inc(ve, 1)
        on("vector", p6_comb)
        led.inc("ve")
        ve_am2 = led.val("ve")

        def p6_tr(tn, ve_am2=ve_am2):
            tn.wait_ge(ve, ve_am2)
            tn.transpose(ps[7][0:1, 0:P], amc[:, 1:2], sI[:, :]).then_inc(pe, 1)
        on("tensor", p6_tr)
        pe_tr2 = led.inc("pe")

        def p6_loc(v, pe_tr2=pe_tr2):
            v.wait_ge(pe, pe_tr2)
            v.reduce_max(tiny[0:1, 1:2], ps[7][0:1, 0:P], AX.X).then_inc(ve, 1)
        on("vector", p6_loc)
        ve_loc = led.inc("ve")

        def p6_ccout(q, ve_loc=ve_loc):
            q.wait_ge(ve, ve_loc)
            dma_in(q, "cc2", ccin[:, :], tiny[0:1, 1:2])
        on("sync", p6_ccout)
        t_ldcc = led.val("ld_cc2")

        def p6_cc(g, t_ldcc=t_ldcc):
            g.wait_ge(LD["cc2"], t_ldcc)
            g.collective_compute(
                "AllReduce", ALU.max, replica_groups=[list(range(8))],
                ins=[ccin.ap().opt()], outs=[ccout.ap().opt()]).then_inc(cc, 1)
        on("gpsimd", p6_cc)
        led.inc("cc")

        def p6_ccin(g):
            g.wait_ge(cc, 2)
            dma_in(g, "cc3", tiny[0:1, 2:3], ccout[:, :])
        on("gpsimd", p6_ccin)
        t_ldg = led.val("ld_cc3")

        def p6_scale(v, t_ldg=t_ldg):
            b0 = led.val("ve")
            v.wait_ge(LD["cc3"], t_ldg)
            v.tensor_single_scalar(tiny[0:1, 3:4], tiny[0:1, 2:3], INV448,
                                   ALU.mult).then_inc(ve, 1)
            v.wait_ge(ve, b0 + 1)
            v.tensor_single_scalar(tiny[0:1, 3:4], tiny[0:1, 3:4], 1e-12,
                                   ALU.max).then_inc(ve, 1)
            v.wait_ge(ve, b0 + 2)
            v.reciprocal(S[0:1, 0, 0:1], tiny[0:1, 3:4]).then_inc(ve, 1)
            v.wait_ge(ve, b0 + 3)
            v.tensor_single_scalar(S[0:1, 0, 1:2], S[0:1, 0, 0:1], 0.5,
                                   ALU.mult).then_inc(ve, 1)
            v.wait_ge(ve, b0 + 4)
            v.tensor_tensor(S[0:1, 0, 2:3], tiny[0:1, 3:4], tiny[0:1, 0:1],
                            ALU.mult).then_inc(ve, 1)
            v.wait_ge(ve, b0 + 5)
            v.tensor_single_scalar(S[0:1, 0, 2:3], S[0:1, 0, 2:3], 4.0,
                                   ALU.mult).then_inc(ve, 1)
        on("vector", p6_scale)
        for _ in range(5):
            led.inc("ve")
        ve_row = led.inc("ve")

        def p6_bc(tn, ve_row=ve_row):
            tn.wait_ge(ve, ve_row)
            tn.matmul(ps[7][0:P, 0:3], sOne[0:1, 0:P], S[0:1, 0, 0:3],
                      start=True, stop=True).then_inc(pe, 1)
        on("tensor", p6_bc)
        pe_bc2 = led.inc("pe")

        def p6_bc_ev(v, pe_bc2=pe_bc2):
            v.wait_ge(pe, pe_bc2)
            v.tensor_copy(SCAL[:, 2:5], ps[7][0:P, 0:3]).then_inc(ve, 1)
        on("vector", p6_bc_ev)
        ve_scal_x = led.inc("ve")

        if "tiny" in tap_dram:
            on("sync", lambda q: dma_out(q, tap_dram["tiny"][:, :], tiny[:, :]))
        if "scal" in tap_dram:
            on("sync", lambda q: dma_out(q, tap_dram["scal"][:, :], SCAL[:, :]))

        ve_xq_rc = [None, None]
        for rc in range(2):
            for fc in range(NF):
                def p7_q(v, fc=fc, rc=rc, last=(fc == NF - 1)):
                    if rc == 0 and fc == 0:
                        v.wait_ge(ve, ve_scal_x)
                    v.tensor_tensor(S[:, 0, :], C[:, fc, rc * RB:(rc + 1) * RB],
                                    Smu3[:, rc, :], ALU.subtract)
                    v.tensor_tensor(S[:, 0, :], S[:, 0, :], Srho3[:, rc, :],
                                    ALU.mult)
                    ins = emit_quant(v, S[:, 0, :], xq[:, rc, fc, :],
                                     SCAL[:, 3:4])
                    if last:
                        ins.then_inc(ve, 1)
                on("vector", p7_q)
            ve_xq_rc[rc] = led.inc("ve")

            if "xq" in tap_dram:
                def txq(q, rc=rc, ve_xq=ve_xq_rc[rc]):
                    q.wait_ge(ve, ve_xq)
                    dma_out(q, tap_dram["xq"][:, :, rc * RB:(rc + 1) * RB],
                            xq[:, rc, :, :])
                on("sync", txq)

        w8_gate = {0: 0, 1: 0, 2: 0}
        fin_gate = {0: 0, 1: 0}
        for rc in range(2):

            for og in range(4):
                gidx = rc * 4 + og
                bb = 4 * (gidx % 2)
                pe_base = led.val("pe")
                ve_xq = ve_xq_rc[rc]
                for pr in range(NF // 2):
                    slot = (gidx * 8 + pr) % 3
                    g = w8_gate[slot]

                    def p7_ld(q, og=og, pr=pr, slot=slot, g=g):
                        if g:
                            q.wait_ge(pe, g)
                        dma_in(q, "w8%d" % slot, w8st[:, slot, :],
                               w8d[og, pr, :, :])
                    on("sync", p7_ld)
                    t_ld = led.val("ld_w8%d" % slot)

                    ve_bank = fin_gate[gidx % 2]

                    def p7_mm(tn, pr=pr, rc=rc, slot=slot, bb=bb, t_ld=t_ld,
                              ve_bank=ve_bank, ve_xq=ve_xq):
                        tn.wait_ge(LD["w8%d" % slot], t_ld)
                        if pr == 0:
                            tn.wait_ge(ve, ve_xq)
                            if ve_bank:
                                tn.wait_ge(ve, ve_bank)
                        for i in range(2):
                            fc = pr * 2 + i
                            for oi in range(4):
                                ins = tn.matmul(
                                    ps[bb + oi][:, :],
                                    w8st[:, slot,
                                         i * 512 + oi * 128:
                                         i * 512 + (oi + 1) * 128],
                                    xq[:, rc, fc, :], start=(fc == 0),
                                    stop=(fc == NF - 1))
                            ins.then_inc(pe, 1)
                    on("tensor", p7_mm)
                    led.inc("pe")
                    w8_gate[slot] = led.inc("pe")

                pe_og = led.val("pe")

                def p7_fin(v, og=og, rc=rc, bb=bb, pe_og=pe_og):
                    v.wait_ge(pe, pe_og)
                    for oi in range(4):
                        o = og * 4 + oi
                        ctile = C[:, o, rc * RB:(rc + 1) * RB]
                        v.tensor_scalar(qb1[:, :], ps[bb + oi][:, :],
                                        SCAL[:, 4:5], None, ALU.mult)
                        ins = v.tensor_tensor(ctile, ctile, qb1[:, :], ALU.add)
                    ins.then_inc(ve, 1)
                on("vector", p7_fin)
                ve_fin = led.inc("ve")
                fin_gate[gidx % 2] = ve_fin

                def p7_st(q, og=og, rc=rc, ve_fin=ve_fin):
                    q.wait_ge(ve, ve_fin)
                    for oi in range(4):
                        o = og * 4 + oi
                        dma_out(q, outT[o, :, rc * RB:(rc + 1) * RB],
                                C[:, o, rc * RB:(rc + 1) * RB])
                on("scalar", p7_st)

        if "C" in tap_dram:
            on("sync", lambda q: dma_out(q, tap_dram["C"][:, :, :], C[:, :, :]))

        so_total = led.val("so")

        def fin(q, so_total=so_total):
            q.wait_ge(so, so_total)
        on("sync", fin)

    return nc


def _layout_inputs(L0_attn_x, L0_attn_g_norm, L0_attn_Wq, L0_attn_Wkv,
                   L0_attn_Wout, L0_attn_g_out, L0_ffn_weight):
    x = np.ascontiguousarray(np.asarray(L0_attn_x, dtype=np.float32))
    Wq = np.ascontiguousarray(np.asarray(L0_attn_Wq, dtype=np.float32))
    Wkv = np.ascontiguousarray(np.asarray(L0_attn_Wkv, dtype=np.float32))
    Wout = np.ascontiguousarray(np.asarray(L0_attn_Wout, dtype=np.float32))
    Wt = np.asarray(L0_ffn_weight)
    gN = np.ascontiguousarray(np.asarray(L0_attn_g_norm, dtype=np.float32))
    gO = np.ascontiguousarray(np.asarray(L0_attn_g_out, dtype=np.float32))

    Wq_t = np.ascontiguousarray(Wq.reshape(NF, P, 4, 512).transpose(2, 0, 1, 3))
    Wkv_t = np.ascontiguousarray(Wkv.reshape(NF, P, 256))
    Wo_t = np.ascontiguousarray(Wout.reshape(NF, P, 4, 512).transpose(2, 0, 1, 3))
    WtT = np.ascontiguousarray(Wt.T).reshape(NF, P, 2048)
    gN2 = np.ascontiguousarray(gN.reshape(NF, P).T)
    gO2 = np.ascontiguousarray(gO.reshape(NF, P).T)
    iden = np.eye(P, dtype=np.float32)

    in_maps = []
    for c in range(8):
        b, half = c // 2, c % 2
        order = [2 * (1 - half), 2 * (1 - half) + 1, 2 * half, 2 * half + 1]
        xb = np.empty((NB, P, NF, RB), dtype=np.float32)
        for i, bi in enumerate(order):
            blkd = x[b, bi * RB:(bi + 1) * RB, :]
            xb[i] = blkd.T.reshape(NF, P, RB).transpose(1, 0, 2)
        in_maps.append(dict(xblk=xb, Wq_t=Wq_t, Wkv_t=Wkv_t, Wo_t=Wo_t,
                            WtT=WtT, gN2=gN2, gO2=gO2, iden=iden))
    return in_maps


def kernel(L0_attn_x, L0_attn_g_norm, L0_attn_Wq, L0_attn_Wkv, L0_attn_Wout,
           L0_attn_g_out, L0_ffn_weight):
    in_maps = _layout_inputs(L0_attn_x, L0_attn_g_norm, L0_attn_Wq,
                             L0_attn_Wkv, L0_attn_Wout, L0_attn_g_out,
                             L0_ffn_weight)
    nc = build_program(taps=tuple(_TAPS))
    res = run_bass_kernel_spmd(nc, in_maps, core_ids=list(range(8)))
    out = np.empty((4, 2048, 2048), dtype=np.float32)
    for c in range(8):
        b, half = c // 2, c % 2
        o = np.asarray(res.results[c]["outT"])
        out[b, half * 1024:(half + 1) * 1024, :] = (
            o.transpose(2, 0, 1).reshape(1024, 2048))
    if _TAPS:
        kernel._last_results = res
    return out


# version 25
# speedup vs baseline: 1.1410x; 1.0640x over previous
from contextlib import ExitStack

import numpy as np
import ml_dtypes
import concourse.bass as bass
import concourse.mybir as mybir
from concourse.bass_utils import run_bass_kernel_spmd

F32 = mybir.dt.float32
BF16 = mybir.dt.bfloat16
FP8 = mybir.dt.float8e4
U8 = mybir.dt.uint8
AF = mybir.ActivationFunctionType
ALU = mybir.AluOpType
AX = mybir.AxisListType

P = 128
NF = 16
NH = 16
NJ = 16
NB = 4
RB = 512
EPS = 1e-5
L2EPS = 1e-12
INV2048 = 1.0 / 2048.0
INV448 = 1.0 / 448.0

_TAPS = []


class Led:
    def __init__(self):
        self.n = {}

    def inc(self, s, k=1):
        self.n[s] = self.n.get(s, 0) + k
        return self.n[s]

    def val(self, s):
        return self.n.get(s, 0)


def build_program(taps=()):
    nc = bass.Bass("TRN2", target_bir_lowering=False, detect_race_conditions=False)

    xblk = nc.dram_tensor("xblk", [NB, P, NF, RB], F32, kind="ExternalInput")
    Wq_t = nc.dram_tensor("Wq_t", [4, NF, P, 512], F32, kind="ExternalInput")
    Wkv_t = nc.dram_tensor("Wkv_t", [NF, P, 256], F32, kind="ExternalInput")
    Wo_t = nc.dram_tensor("Wo_t", [4, NF, P, 512], F32, kind="ExternalInput")
    WtT = nc.dram_tensor("WtT", [NF, P, 2048], BF16, kind="ExternalInput")
    gN2 = nc.dram_tensor("gN2", [P, NF], F32, kind="ExternalInput")
    gO2 = nc.dram_tensor("gO2", [P, NF], F32, kind="ExternalInput")
    iden = nc.dram_tensor("iden", [P, P], F32, kind="ExternalInput")

    outT = nc.dram_tensor("outT", [NF, P, 1024], F32, kind="ExternalOutput")

    w8d = nc.dram_tensor("w8d", [4, NF // 2, P, 1024], FP8)
    ccin = nc.dram_tensor("ccin", [1, 1], F32)
    ccout = nc.dram_tensor("ccout", [1, 1], F32)
    agin = nc.dram_tensor("agin", [P, 2048], F32)
    agout = nc.dram_tensor("agout", [2, P, 2048], F32)

    tap_dram = {}
    for t in taps:
        shp, dt = {
            "kT": ([P, 2048], F32), "v": ([P, NJ, P], F32),
            "qn": ([P, NH, 2 * RB], F32), "att": ([P, NF, 2 * RB], F32),
            "C": ([P, NF, 1024], F32), "xq": ([P, NF, 2 * RB], BF16),
            "stat1": ([P, 8, RB], F32), "w8chk": ([4, NF, P, 512], BF16),
            "scal": ([P, 8], F32), "tiny": ([1, 8], F32),
            "Cmid": ([P, NF, 1024], F32),
        }[t]
        tap_dram[t] = nc.dram_tensor("tap_" + t, shp, dt, kind="ExternalOutput")

    led = Led()

    with ExitStack() as stx:
        e = stx.enter_context

        LDKEYS = ("xb0", "xb1", "xg0", "xg1", "w0", "w1",
                  "kv0", "kv1", "kv2", "kv3",
                  "wt0", "wt1", "wt2", "wt3",
                  "w80", "w81", "w82", "w83", "cst", "cc2",
                  "cc3", "agp", "agf")
        LD = {k: e(nc.semaphore("ld_" + k)) for k in LDKEYS}
        so = e(nc.semaphore("so"))
        pe = e(nc.semaphore("pe"))
        ve = e(nc.semaphore("ve"))
        ac = e(nc.semaphore("ac"))
        cc = e(nc.semaphore("cc"))
        SEMS = {"so": so, "pe": pe, "ve": ve, "ac": ac}
        for k in LDKEYS:
            SEMS["ld_" + k] = LD[k]

        sI = e(nc.sbuf_tensor("sI", [P, P], F32))
        sOne = e(nc.sbuf_tensor("sOne", [P, P], F32))
        sGN = e(nc.sbuf_tensor("sGN", [P, NF], F32))
        sGO = e(nc.sbuf_tensor("sGO", [P, NF], F32))
        SCAL = e(nc.sbuf_tensor("SCAL", [P, 8], F32))
        tiny = e(nc.sbuf_tensor("tiny", [1, 8], F32))
        amc = e(nc.sbuf_tensor("amc", [P, 2], F32))
        amcols = e(nc.sbuf_tensor("amcols", [P, 100], F32))
        A = e(nc.sbuf_tensor("A", [P, NH, RB], F32))
        B = e(nc.sbuf_tensor("B", [P, NF, RB], F32))
        C = e(nc.sbuf_tensor("C", [P, NF, 1024], F32))
        S = e(nc.sbuf_tensor("S", [P, 5, RB], F32))
        SQ = e(nc.sbuf_tensor("SQ", [P, 2, RB], F32))
        Smu_o = e(nc.sbuf_tensor("Smu_o", [P, 2, RB], F32))
        Sr12_o = e(nc.sbuf_tensor("Sr12_o", [P, 2, RB], F32))
        Smu3 = e(nc.sbuf_tensor("Smu3", [P, 2, RB], F32))
        Srho3 = e(nc.sbuf_tensor("Srho3", [P, 2, RB], F32))
        xbig = e(nc.sbuf_tensor("xbig", [P, 4, RB], F32))
        wst = e(nc.sbuf_tensor("wst", [P, 2, 512], F32))
        wkvst = e(nc.sbuf_tensor("wkvst", [P, 4, 256], F32))
        wtst = e(nc.sbuf_tensor("wtst", [P, 4, 512], BF16))
        w8st = e(nc.sbuf_tensor("w8st", [P, 3, 1024], FP8))
        qb1 = e(nc.sbuf_tensor("qb1", [P, 512], BF16))
        w8c = e(nc.sbuf_tensor("w8c", [P, 4, 512], FP8))

        att_stack = ExitStack()
        kT = att_stack.enter_context(nc.sbuf_tensor("kT", [P, 2048], F32))
        vv = att_stack.enter_context(nc.sbuf_tensor("vv", [P, NJ, P], F32))
        ptil = att_stack.enter_context(nc.sbuf_tensor("ptil", [P, 2, RB], F32))

        ps = [e(nc.psum_tensor(f"ps{i}", [P, 512], F32)) for i in range(8)]

        block = e(nc.Block())

        def on(engname, fn):
            getattr(block, engname)(fn)

        def dma_in(q, stream, out_ap, in_ap):
            q.dma_start(out=out_ap, in_=in_ap).then_inc(LD[stream], 16)
            return led.inc("ld_" + stream, 16)

        def dma_out(q, out_ap, in_ap):
            q.dma_start(out=out_ap, in_=in_ap).then_inc(so, 16)
            return led.inc("so", 16)

        def emit_barrier():
            bv = {s: led.val(s) for s in SEMS}

            def mk(eng):
                for nm, val in bv.items():
                    if val:
                        eng.wait_ge(SEMS[nm], val)
            for en in ("sync", "tensor", "vector", "scalar"):
                on(en, mk)

        def emit_rsqrt(vio, rout, epsb, t1):
            def dv0(v):
                v.tensor_single_scalar(vio, vio, float(epsb), ALU.add).then_inc(ve, 1)
            on("vector", dv0)
            v0 = led.inc("ve")

            def sc0(s):
                b0 = led.val("ac")
                s.wait_ge(ve, v0)
                s.activation(t1, vio, AF.Ln).then_inc(ac, 1)
                s.wait_ge(ac, b0 + 1)
                s.activation(rout, t1, AF.Exp, scale=-0.5).then_inc(ac, 1)
            on("scalar", sc0)
            led.inc("ac")
            a0 = led.inc("ac")

            def dv1(v):
                v.wait_ge(ac, a0)
                v.tensor_tensor(t1, rout, rout, ALU.mult)
                v.scalar_tensor_tensor(t1, t1, -0.5, vio, ALU.mult, ALU.mult)
                v.tensor_single_scalar(t1, t1, 1.5, ALU.add)
                v.tensor_tensor(rout, rout, t1, ALU.mult).then_inc(ve, 1)
            on("vector", dv1)
            return led.inc("ve")

        def emit_recip(vin, rout, t0, t1):
            def sc0(s):
                b0 = led.val("ac")
                s.wait_ge(ve, led.val("ve"))
                s.wait_ge(pe, led.val("pe"))
                s.activation(t0, vin, AF.Ln).then_inc(ac, 1)
                s.wait_ge(ac, b0 + 1)
                s.activation(rout, t0, AF.Exp, scale=-1.0).then_inc(ac, 1)
            on("scalar", sc0)
            led.inc("ac")
            a0 = led.inc("ac")

            def dv1(v):
                v.wait_ge(ac, a0)
                v.tensor_tensor(t1, vin, rout, ALU.mult)
                v.tensor_scalar(t1, t1, -1.0, 2.0, ALU.mult, ALU.add)
                v.tensor_tensor(rout, rout, t1, ALU.mult).then_inc(ve, 1)
            on("vector", dv1)
            return led.inc("ve")

        def emit_quant(v, src_ap, dst, rs2_col):
            return v.tensor_scalar(dst, src_ap, rs2_col, None, ALU.mult)

        def p0_sync(q):
            dma_in(q, "cst", sI[:, :], iden[:, :])
            dma_in(q, "cst", sGN[:, :], gN2[:, :])
            dma_in(q, "cst", sGO[:, :], gO2[:, :])
        on("sync", p0_sync)
        ld_cst = led.val("ld_cst")

        def p0_act(s):
            from concourse.hw_specs import get_activation_tables
            idx = None
            for i, fns in enumerate(get_activation_tables(nc.m.arch).values()):
                names = {f.name for f in fns}
                if {"Exp", "Ln", "Square"} <= names:
                    idx = i
                    break
            assert idx is not None
            s.add_instruction(mybir.InstLoadActFuncSet(
                name=nc.get_next_instruction_name(),
                act_func_set_id=idx, ins=[], outs=[]))
        on("scalar", p0_act)

        def p0_vec(v):
            v.memset(sOne[:, :], 1.0)
            v.memset(tiny[:, :], 0.0)
            v.memset(SCAL[:, :], 0.0)
            v.memset(amc[:, :], 0.0).then_inc(ve, 1)
        on("vector", p0_vec)
        led.inc("ve")

        pw_units = []
        wt_gate = {0: 0, 1: 0, 2: 0, 3: 0}
        w8c_gate = {0: 0, 1: 0, 2: 0, 3: 0}
        pw_state = {"scal": None}

        def mk_amax_unit(t, qq):
            def unit():
                slot = (t * 4 + qq) % 4
                g = wt_gate[slot]

                def ld_(q):
                    if g:
                        q.wait_ge(ve, g)
                    dma_in(q, "wt%d" % slot, wtst[:, slot, :],
                           WtT[t, :, qq * 512:(qq + 1) * 512])
                on("sync", ld_)
                t_ld = led.val("ld_wt%d" % slot)

                def red_(v):
                    v.wait_ge(LD["wt%d" % slot], t_ld)
                    v.reduce_max(amcols[:, t * 4 + qq:t * 4 + qq + 1],
                                 wtst[:, slot, :], AX.X,
                                 apply_absolute_value=True).then_inc(ve, 1)
                on("vector", red_)
                wt_gate[slot] = led.inc("ve")
            return unit

        def pw_combine_unit():
            ve_hop = led.val("ve")

            def pw_comb(v):
                v.wait_ge(ve, ve_hop)
                v.reduce_max(amc[:, 0:1], amcols[:, 0:64], AX.X).then_inc(ve, 1)
            on("vector", pw_comb)
            ve_am = led.inc("ve")

            def pw_tr(tn):
                tn.wait_ge(ve, ve_am)
                tn.wait_ge(LD["cst"], ld_cst)
                tn.transpose(ps[7][0:1, 0:P], amc[:, 0:1],
                             sI[:, :]).then_inc(pe, 1)
            on("tensor", pw_tr)
            pe_tr = led.inc("pe")

            def pw_sw(v):
                b0 = led.val("ve")
                v.wait_ge(pe, pe_tr)
                v.reduce_max(tiny[0:1, 0:1], ps[7][0:1, 0:P],
                             AX.X).then_inc(ve, 1)
                v.wait_ge(ve, b0 + 1)
                v.tensor_single_scalar(tiny[0:1, 0:1], tiny[0:1, 0:1], INV448,
                                       ALU.mult).then_inc(ve, 1)
                v.wait_ge(ve, b0 + 2)
                v.tensor_single_scalar(tiny[0:1, 0:1], tiny[0:1, 0:1], 1e-12,
                                       ALU.max).then_inc(ve, 1)
                v.wait_ge(ve, b0 + 3)
                v.reciprocal(amcols[0:1, 96:97], tiny[0:1, 0:1]).then_inc(ve, 1)
                v.wait_ge(ve, b0 + 4)
                v.tensor_single_scalar(amcols[0:1, 97:98], amcols[0:1, 96:97],
                                       0.5, ALU.mult).then_inc(ve, 1)
            on("vector", pw_sw)
            for _ in range(4):
                led.inc("ve")
            ve_sw = led.inc("ve")

            def pw_bc(tn):
                tn.wait_ge(ve, ve_sw)
                tn.matmul(ps[7][0:P, 0:2], sOne[0:1, 0:P], amcols[0:1, 96:98],
                          start=True, stop=True).then_inc(pe, 1)
            on("tensor", pw_bc)
            pe_bc = led.inc("pe")

            def pw_bc_ev(v):
                v.wait_ge(pe, pe_bc)
                v.tensor_copy(SCAL[:, 0:2], ps[7][0:P, 0:2]).then_inc(ve, 1)
            on("vector", pw_bc_ev)
            pw_state["scal"] = led.inc("ve")

        def mk_quant_unit(t, qq):
            def unit():
                slot = (t * 4 + qq) % 4
                g = wt_gate[slot]
                sog = w8c_gate[slot]

                def ld_(q):
                    if g:
                        q.wait_ge(ve, g)
                    dma_in(q, "wt%d" % slot, wtst[:, slot, :],
                           WtT[t, :, qq * 512:(qq + 1) * 512])
                on("sync", ld_)
                t_ld = led.val("ld_wt%d" % slot)

                def q_(v):
                    v.wait_ge(LD["wt%d" % slot], t_ld)
                    v.wait_ge(ve, pw_state["scal"])
                    if sog:
                        v.wait_ge(so, sog)
                    emit_quant(v, wtst[:, slot, :], w8c[:, slot, :],
                               SCAL[:, 1:2]).then_inc(ve, 1)
                on("vector", q_)
                wt_gate[slot] = ve_q = led.inc("ve")

                def st_(s):
                    s.wait_ge(ve, ve_q)
                    dma_out(s, w8d[qq, t // 2, :,
                                   (t % 2) * 512:(t % 2) * 512 + 512],
                            w8c[:, slot, :])
                on("scalar", st_)
                w8c_gate[slot] = led.val("so")
            return unit

        for t in range(NF):
            for qq in range(4):
                pw_units.append(mk_amax_unit(t, qq))
        pw_units.append(pw_combine_unit)
        for t in range(NF):
            for qq in range(4):
                pw_units.append(mk_quant_unit(t, qq))

        def pump(n):
            for _ in range(n):
                if pw_units:
                    pw_units.pop(0)()

        xgate = [None, None]

        def xwait(q, sl):
            if xgate[sl] is not None:
                gp, ga, gv = xgate[sl]
                q.wait_ge(pe, gp)
                q.wait_ge(ac, ga)
                q.wait_ge(ve, gv)

        def xmark(sl):
            xgate[sl] = (led.val("pe"), led.val("ac"), led.val("ve"))

        for blk in (2, 3):
            mu_dst = Smu_o[:, blk - 2, :]
            r12_dst = Sr12_o[:, blk - 2, :]

            for g2 in range(8):
                sl = g2 % 2

                def p1_ld(q, blk=blk, g2=g2, sl=sl):
                    xwait(q, sl)
                    dma_in(q, "xb%d" % sl, xbig[:, sl * 2:sl * 2 + 2, :],
                           xblk[blk, :, g2 * 2:(g2 + 1) * 2, :])
                on("sync", p1_ld)
                t_ld = led.val("ld_xb%d" % sl)
                pe_base = led.val("pe")

                def p1_sq(s, sl=sl, t_ld=t_ld, pe_base=pe_base):
                    s.wait_ge(LD["xb%d" % sl], t_ld)
                    s.wait_ge(pe, pe_base)
                    for i in range(2):
                        s.activation(SQ[:, i, :], xbig[:, sl * 2 + i, :],
                                     AF.Square).then_inc(ac, 1)
                on("scalar", p1_sq)
                ac_marks = [led.inc("ac") for _ in range(2)]

                ve_emit = led.val("ve")

                def p1_mm(tn, g2=g2, sl=sl, t_ld=t_ld, ac_marks=ac_marks,
                          ve_emit=ve_emit):
                    tn.wait_ge(LD["xb%d" % sl], t_ld)
                    if g2 == 0:
                        tn.wait_ge(ve, ve_emit)
                    for i in range(2):
                        fc = g2 * 2 + i
                        tn.matmul(ps[0][:, :], sOne[:, :], xbig[:, sl * 2 + i, :],
                                  start=(fc == 0), stop=(fc == NF - 1))
                        tn.wait_ge(ac, ac_marks[i])
                        tn.matmul(ps[1][:, :], sOne[:, :], SQ[:, i, :],
                                  start=(fc == 0),
                                  stop=(fc == NF - 1)).then_inc(pe, 1)
                on("tensor", p1_mm)
                for _ in range(2):
                    led.inc("pe")
                xmark(sl)
                pump(4)

            pe_stats = led.val("pe")

            def p1_sc(v, mu_dst=mu_dst, pe_stats=pe_stats):
                v.wait_ge(pe, pe_stats)
                v.tensor_scalar(mu_dst, ps[0][:, :], INV2048, None, ALU.mult)
                v.tensor_scalar(S[:, 1, :], ps[1][:, :], INV2048, None, ALU.mult)
                v.tensor_tensor(S[:, 2, :], mu_dst, mu_dst, ALU.mult)
                v.tensor_tensor(S[:, 2, :], S[:, 1, :], S[:, 2, :],
                                ALU.subtract)
                v.tensor_single_scalar(S[:, 4, :], S[:, 2, :], 1.0,
                                       ALU.mult).then_inc(ve, 1)
            on("vector", p1_sc)
            led.inc("ve")
            emit_rsqrt(S[:, 4, :], S[:, 3, :], EPS, S[:, 1, :])

            def p1_sc2(v):
                v.tensor_tensor(S[:, 1, :], S[:, 3, :], S[:, 3, :], ALU.mult)
                v.tensor_tensor(S[:, 1, :], S[:, 1, :], S[:, 2, :],
                                ALU.mult).then_inc(ve, 1)
            on("vector", p1_sc2)
            led.inc("ve")
            emit_rsqrt(S[:, 1, :], S[:, 4, :], EPS, S[:, 2, :])

            def p1_sc3(v, r12_dst=r12_dst):
                v.tensor_tensor(r12_dst, S[:, 3, :], S[:, 4, :],
                                ALU.mult).then_inc(ve, 1)
            on("vector", p1_sc3)
            ve_r12 = led.inc("ve")

            for g2 in range(8):
                sl = g2 % 2

                def p1_ld2(q, blk=blk, g2=g2, sl=sl):
                    xwait(q, sl)
                    dma_in(q, "xb%d" % sl, xbig[:, sl * 2:sl * 2 + 2, :],
                           xblk[blk, :, g2 * 2:(g2 + 1) * 2, :])
                    for i in range(2):
                        fc = g2 * 2 + i
                        dma_in(q, "kv%d" % (fc % 4), wkvst[:, fc % 4, :],
                               Wkv_t[fc, :, :])
                on("sync", p1_ld2)
                t_ld = led.val("ld_xb%d" % sl)
                t_kv = {i: led.val("ld_kv%d" % i) for i in range(4)}
                pe_base = led.val("pe")

                if blk == 2:
                    xn2dst = [B[:, g2 * 2, :], B[:, g2 * 2 + 1, :]]
                else:
                    xn2dst = [C[:, g2 * 2, RB:2 * RB],
                              C[:, g2 * 2 + 1, RB:2 * RB]]

                def p1_ap(v, g2=g2, sl=sl, t_ld=t_ld, ve_r12=ve_r12,
                          pe_base=pe_base, mu_dst=mu_dst, r12_dst=r12_dst,
                          xn2dst=xn2dst):
                    v.wait_ge(LD["xb%d" % sl], t_ld)
                    v.wait_ge(ve, ve_r12)
                    v.wait_ge(pe, pe_base)
                    for i in range(2):
                        fc = g2 * 2 + i
                        v.tensor_tensor(S[:, 0, :], xbig[:, sl * 2 + i, :],
                                        mu_dst, ALU.subtract)
                        v.scalar_tensor_tensor(
                            xn2dst[i], S[:, 0, :], sGN[:, fc:fc + 1],
                            r12_dst, ALU.mult, ALU.mult).then_inc(ve, 1)
                on("vector", p1_ap)
                ve_marks = [led.inc("ve") for _ in range(2)]

                def p1_kv(tn, g2=g2, t_kv=t_kv, ve_marks=ve_marks,
                          xn2dst=xn2dst):
                    for i in range(2):
                        fc = g2 * 2 + i
                        tn.wait_ge(LD["kv%d" % (fc % 4)], t_kv[fc % 4])
                        tn.wait_ge(ve, ve_marks[i])
                        tn.matmul(ps[2][:, :], wkvst[:, fc % 4, 0:128],
                                  xn2dst[i],
                                  start=(fc == 0), stop=(fc == NF - 1))
                        tn.matmul(ps[3][:, :], wkvst[:, fc % 4, 128:256],
                                  xn2dst[i],
                                  start=(fc == 0),
                                  stop=(fc == NF - 1)).then_inc(pe, 1)
                on("tensor", p1_kv)
                for _ in range(2):
                    led.inc("pe")
                xmark(sl)
                pump(4)

            pe_kv = led.val("pe")

            def p1_ev(v, blk=blk, pe_kv=pe_kv):
                v.wait_ge(pe, pe_kv)
                v.tensor_copy(kT[:, (blk - 2) * RB:(blk - 1) * RB],
                              ps[2][:, :])
                v.tensor_copy(SQ[:, 0, :], ps[3][:, :]).then_inc(ve, 1)
            on("vector", p1_ev)
            ve_ev = led.inc("ve")

            ve_trbase = led.val("ve")
            for jj in range(4):
                jc = (blk - 2) * 4 + jj

                def p1_tr(tn, jj=jj, ve_ev=ve_ev, ve_trbase=ve_trbase):
                    tn.wait_ge(ve, ve_ev)
                    if jj >= 1:
                        tn.wait_ge(ve, ve_trbase + jj)
                    tn.transpose(ps[4][:, 0:128],
                                 SQ[:, 0, jj * 128:(jj + 1) * 128],
                                 sI[:, :]).then_inc(pe, 1)
                on("tensor", p1_tr)
                pe_t = led.inc("pe")

                def p1_trev(v, jc=jc, pe_t=pe_t):
                    v.wait_ge(pe, pe_t)
                    v.tensor_copy(vv[:, jc, :], ps[4][:, 0:128]).then_inc(ve, 1)
                on("vector", p1_trev)
                led.inc("ve")

        pump(len(pw_units))

        ve_kl0 = led.val("ve")
        pe_kl0 = led.val("pe")
        klm = []
        for b4 in range(2):
            def kl_sq(s, b4=b4, ve_kl0=ve_kl0, pe_kl0=pe_kl0):
                if b4 == 0:
                    s.wait_ge(ve, ve_kl0)
                    s.wait_ge(pe, pe_kl0)
                s.activation(ptil[:, b4, :], kT[:, b4 * RB:(b4 + 1) * RB],
                             AF.Square).then_inc(ac, 1)
            on("scalar", kl_sq)
            ac_m = led.inc("ac")

            def kl_mm(tn, b4=b4, ac_m=ac_m, ve_kl0=ve_kl0):
                tn.wait_ge(ac, ac_m)
                tn.wait_ge(ve, ve_kl0)
                tn.matmul(ps[4 + b4][:, :], sOne[:, :], ptil[:, b4, :],
                          start=True, stop=True).then_inc(pe, 1)
            on("tensor", kl_mm)
            klm.append(led.inc("pe"))

        def kl_ev(v, pe_l=klm[1]):
            v.wait_ge(pe, pe_l)
            v.tensor_copy(xbig[:, 0, :], ps[4][:, :])
            v.tensor_copy(xbig[:, 1, :], ps[5][:, :])
            v.tensor_single_scalar(xbig[:, 0:2, :], xbig[:, 0:2, :],
                                   float(L2EPS), ALU.add).then_inc(ve, 1)
        on("vector", kl_ev)
        kl_v0 = led.inc("ve")

        def kl_ln(s, v0=kl_v0):
            b0 = led.val("ac")
            s.wait_ge(ve, v0)
            s.activation(xbig[:, 2:4, :], xbig[:, 0:2, :],
                         AF.Ln).then_inc(ac, 1)
            s.wait_ge(ac, b0 + 1)
            s.activation(xbig[:, 2:4, :], xbig[:, 2:4, :], AF.Exp,
                         scale=-0.5).then_inc(ac, 1)
        on("scalar", kl_ln)
        led.inc("ac")
        kl_a0 = led.inc("ac")

        def kl_nw(v, a0=kl_a0):
            v.wait_ge(ac, a0)
            v.tensor_tensor(xbig[:, 0:2, :], xbig[:, 0:2, :],
                            xbig[:, 2:4, :], ALU.mult)
            v.tensor_tensor(xbig[:, 0:2, :], xbig[:, 0:2, :],
                            xbig[:, 2:4, :], ALU.mult)
            v.tensor_scalar(xbig[:, 0:2, :], xbig[:, 0:2, :], -0.5, 1.5,
                            ALU.mult, ALU.add)
            v.tensor_tensor(xbig[:, 2:4, :], xbig[:, 2:4, :],
                            xbig[:, 0:2, :], ALU.mult)
            v.tensor_single_scalar(xbig[:, 2:4, :], xbig[:, 2:4, :], 4.0,
                                   ALU.mult)
            for b4 in range(2):
                ins = v.tensor_tensor(kT[:, b4 * RB:(b4 + 1) * RB],
                                      kT[:, b4 * RB:(b4 + 1) * RB],
                                      xbig[:, 2 + b4, :], ALU.mult)
            ins.then_inc(ve, 1)
        on("vector", kl_nw)
        led.inc("ve")

        emit_barrier()
        if "kT" in tap_dram:
            on("sync", lambda q: dma_out(q, tap_dram["kT"][:, :], kT[:, :]))
        if "v" in tap_dram:
            on("sync", lambda q: dma_out(q, tap_dram["v"][:, :, :], vv[:, :, :]))
        if "stat1" in tap_dram:
            def ts1(q):
                dma_out(q, tap_dram["stat1"][:, 0:2, :], Smu_o[:, :, :])
                dma_out(q, tap_dram["stat1"][:, 2:4, :], Sr12_o[:, :, :])
            on("sync", ts1)

        ve_pack = led.val("ve")

        def ag_pack(q, ve_pack=ve_pack):
            q.wait_ge(ve, ve_pack)
            dma_in(q, "agp", agin[:, 0:1024], kT[:, 0:1024])
            dma_in(q, "agp", agin[:, 1024:2048], vv[:, 0:8, :])
        on("sync", ag_pack)
        t_agp = led.val("ld_agp")

        def ag_cc(g, t_agp=t_agp):
            g.wait_ge(LD["agp"], t_agp)
            g.collective_compute(
                "AllGather", ALU.bypass,
                replica_groups=[[2 * i, 2 * i + 1] for i in range(4)],
                ins=[agin.ap().opt()],
                outs=[agout.ap().opt()]).then_inc(cc, 1)
        on("gpsimd", ag_cc)
        led.inc("cc")

        def ag_fetch(g):
            g.wait_ge(cc, 1)
            dma_in(g, "agf", kT[:, 0:1024], agout[0, :, 0:1024])
            dma_in(g, "agf", kT[:, 1024:2048], agout[1, :, 0:1024])
            dma_in(g, "agf", vv[:, 0:8, :], agout[0, :, 1024:2048])
            dma_in(g, "agf", vv[:, 8:16, :], agout[1, :, 1024:2048])
        on("gpsimd", ag_fetch)
        t_agf = led.val("ld_agf")

        def emit_tail(rc, oblk, xfree=None):
            pe_st = led.val("pe")

            def p5_sc(v, pe_st=pe_st):
                v.wait_ge(pe, pe_st)
                v.tensor_scalar(S[:, 1, :], ps[4][:, :], INV2048, None,
                                ALU.mult)
                v.tensor_scalar(S[:, 2, :], ps[5][:, :], INV2048, None,
                                ALU.mult)
                v.tensor_tensor(S[:, 0, :], S[:, 1, :], S[:, 1, :], ALU.mult)
                v.tensor_tensor(S[:, 2, :], S[:, 2, :], S[:, 0, :],
                                ALU.subtract).then_inc(ve, 1)
            on("vector", p5_sc)
            sc_m = led.inc("ve")
            emit_rsqrt(S[:, 2, :], S[:, 3, :], EPS, S[:, 4, :])

            resid_marks = []
            for g2 in range(8):
                sl = g2 % 2

                def p5_xl(g, oblk=oblk, g2=g2, sl=sl, xfree=xfree):
                    xwait(g, sl)
                    if xfree and g2 < 2:
                        g.wait_ge(ve, xfree)
                    dma_in(g, "xg%d" % sl, xbig[:, sl * 2:sl * 2 + 2, :],
                           xblk[oblk, :, g2 * 2:(g2 + 1) * 2, :])
                on("gpsimd", p5_xl)
                t_ld = led.val("ld_xg%d" % sl)

                def p5_res(v, g2=g2, sl=sl, rc=rc, t_ld=t_ld):
                    v.wait_ge(LD["xg%d" % sl], t_ld)
                    for i in range(2):
                        fc = g2 * 2 + i
                        ctile = C[:, fc, rc * RB:(rc + 1) * RB]
                        v.tensor_tensor(S[:, 0, :], ctile, S[:, 1, :],
                                        ALU.subtract)
                        v.scalar_tensor_tensor(S[:, 0, :], S[:, 0, :],
                                               sGO[:, fc:fc + 1], S[:, 3, :],
                                               ALU.mult, ALU.mult)
                        ins = v.tensor_tensor(ctile, xbig[:, sl * 2 + i, :],
                                              S[:, 0, :], ALU.add)
                    ins.then_inc(ve, 1)
                on("vector", p5_res)
                resid_marks.append(led.inc("ve"))
                xmark(sl)

            pe6_base = led.val("pe")
            for fc in range(NF):
                def p6_sq(s, fc=fc, rc=rc, pe6_base=pe6_base,
                          rm=resid_marks[fc // 2]):
                    s.wait_ge(ve, rm)
                    s.wait_ge(pe, pe6_base + max(0, fc - 1))
                    s.activation(ptil[:, fc % 2, :],
                                 C[:, fc, rc * RB:(rc + 1) * RB],
                                 AF.Square).then_inc(ac, 1)
                on("scalar", p6_sq)
                ac_m = led.inc("ac")

                def p6_smm(tn, fc=fc, rc=rc, ac_m=ac_m, sc_m=sc_m,
                           rm=resid_marks[fc // 2]):
                    if fc == 0:
                        tn.wait_ge(ve, sc_m)
                    tn.wait_ge(ve, rm)
                    tn.matmul(ps[4][:, :], sOne[:, :],
                              C[:, fc, rc * RB:(rc + 1) * RB],
                              start=(fc == 0), stop=(fc == NF - 1))
                    tn.wait_ge(ac, ac_m)
                    tn.matmul(ps[5][:, :], sOne[:, :], ptil[:, fc % 2, :],
                              start=(fc == 0),
                              stop=(fc == NF - 1)).then_inc(pe, 1)
                on("tensor", p6_smm)
                led.inc("pe")

            pe_st6 = led.val("pe")

            def p6_sc(v, rc=rc, pe_st6=pe_st6):
                v.wait_ge(pe, pe_st6)
                v.tensor_scalar(Smu3[:, rc, :], ps[4][:, :], INV2048, None,
                                ALU.mult)
                v.tensor_scalar(S[:, 2, :], ps[5][:, :], INV2048, None,
                                ALU.mult)
                v.tensor_tensor(S[:, 0, :], Smu3[:, rc, :], Smu3[:, rc, :],
                                ALU.mult)
                v.tensor_tensor(S[:, 2, :], S[:, 2, :], S[:, 0, :],
                                ALU.subtract).then_inc(ve, 1)
            on("vector", p6_sc)
            led.inc("ve")
            emit_rsqrt(S[:, 2, :], Srho3[:, rc, :], EPS, S[:, 4, :])

            for fc in range(NF):
                def p6_am(v, fc=fc, rc=rc, last=(fc == NF - 1)):
                    v.tensor_tensor(S[:, 0, :],
                                    C[:, fc, rc * RB:(rc + 1) * RB],
                                    Smu3[:, rc, :], ALU.subtract)
                    v.tensor_tensor(S[:, 0, :], S[:, 0, :], Srho3[:, rc, :],
                                    ALU.mult)
                    col = 64 + rc * NF + fc
                    ins = v.reduce_max(amcols[:, col:col + 1], S[:, 0, :],
                                       AX.X, apply_absolute_value=True)
                    if last:
                        ins.then_inc(ve, 1)
                on("vector", p6_am)
            led.inc("ve")

        pending_tail = []
        xbig_free = {}
        for rc in range(2):
            oblk = 2 + rc


            pe_qend_prev = None
            ve_evq_prev = None
            for hg in range(4):
                pe_base = led.val("pe")
                ve_done_prev = ve_evq_prev if hg else led.val("ve")
                pe_wst_prev = pe_qend_prev if hg else pe_base

                for fc in range(NF):
                    def p3_ld(q, hg=hg, fc=fc, pe_base=pe_base,
                              pe_wst_prev=pe_wst_prev):
                        if fc >= 2:
                            q.wait_ge(pe, pe_base + fc - 1)
                        else:
                            q.wait_ge(pe, pe_wst_prev)
                        dma_in(q, "w%d" % (fc % 2), wst[:, fc % 2, :],
                               Wq_t[hg, fc, :, :])
                    on("sync", p3_ld)
                    t_ld = led.val("ld_w%d" % (fc % 2))

                    def p3_mm(tn, fc=fc, rc=rc, t_ld=t_ld,
                              ve_done_prev=ve_done_prev):
                        tn.wait_ge(LD["w%d" % (fc % 2)], t_ld)
                        if fc == 0:
                            tn.wait_ge(ve, ve_done_prev)
                        xn2src = (B[:, fc, :] if rc == 0
                                  else C[:, fc, RB:2 * RB])
                        for hh in range(4):
                            ins = tn.matmul(
                                ps[hh][:, :],
                                wst[:, fc % 2, hh * 128:(hh + 1) * 128],
                                xn2src, start=(fc == 0),
                                stop=(fc == NF - 1))
                        ins.then_inc(pe, 1)
                    on("tensor", p3_mm)
                    led.inc("pe")
                pe_qend_prev = led.val("pe")

                pe_q = led.val("pe")

                def p3_evq(v, hg=hg, pe_q=pe_q):
                    v.wait_ge(pe, pe_q)
                    for hh in range(4):
                        ins = v.tensor_copy(A[:, hg * 4 + hh, :], ps[hh][:, :])
                    ins.then_inc(ve, 1)
                on("vector", p3_evq)
                ve_evq_prev = led.inc("ve")

                if rc == 0:
                    lad = [(xbig[:, 0:2, :], xbig[:, 2:4, :], 0),
                           (Smu_o[:, :, :], Sr12_o[:, :, :], 2)]
                    banks = (4, 5, 6, 7)
                else:
                    lad = [(B[:, 0:2, :], B[:, 2:4, :], 0),
                           (Smu_o[:, :, :], Sr12_o[:, :, :], 2)]
                    banks = (6, 7, 6, 7)
                ve_evq = led.val("ve")
                pe_now3 = led.val("pe")
                ve_lfree = led.val("ve")
                mm_marks = []
                v0m = []
                for li, (vio, rout, hb) in enumerate(lad):
                    for j in range(2):
                        hh = li * 2 + j
                        h = hg * 4 + hh

                        def p3_sq(s, h=h, hh=hh, ve_evq=ve_evq,
                                  pe_now3=pe_now3, mm_marks=tuple(mm_marks)):
                            if hh == 0:
                                s.wait_ge(ve, ve_evq)
                                s.wait_ge(pe, pe_now3)
                            if hh >= 2:
                                s.wait_ge(pe, mm_marks[hh - 2])
                            s.activation(SQ[:, hh % 2, :], A[:, h, :],
                                         AF.Square).then_inc(ac, 1)
                        on("scalar", p3_sq)
                        ac_m = led.inc("ac")

                        def p3_lmm(tn, ac_m=ac_m, hh=hh, bank=banks[hh],
                                   ve_lfree=ve_lfree,
                                   ev1=(v0m[0] if v0m else None)):
                            tn.wait_ge(ac, ac_m)
                            if hh == 0:
                                tn.wait_ge(ve, ve_lfree)
                            if hh == 2 and ev1 is not None and bank == banks[0]:
                                tn.wait_ge(ve, ev1)
                            tn.matmul(ps[bank][:, :], sOne[:, :],
                                      SQ[:, hh % 2, :],
                                      start=True, stop=True).then_inc(pe, 1)
                        on("tensor", p3_lmm)
                        mm_marks.append(led.inc("pe"))

                    def p3_ev(v, vio=vio, b0=banks[li * 2], b1=banks[li * 2 + 1],
                              pe_l=mm_marks[-1]):
                        v.wait_ge(pe, pe_l)
                        v.tensor_copy(vio[:, 0, :], ps[b0][:, :])
                        v.tensor_copy(vio[:, 1, :], ps[b1][:, :])
                        v.tensor_single_scalar(vio, vio,
                                               float(L2EPS / 256.0),
                                               ALU.add).then_inc(ve, 1)
                    on("vector", p3_ev)
                    v0m.append(led.inc("ve"))

                a0m = []
                for li, (vio, rout, hb) in enumerate(lad):
                    def p3_ln(s, vio=vio, rout=rout, v0=v0m[li]):
                        b0 = led.val("ac")
                        s.wait_ge(ve, v0)
                        s.activation(rout, vio, AF.Ln).then_inc(ac, 1)
                        s.wait_ge(ac, b0 + 1)
                        s.activation(rout, rout, AF.Exp,
                                     scale=-0.5).then_inc(ac, 1)
                    on("scalar", p3_ln)
                    led.inc("ac")
                    a0m.append(led.inc("ac"))

                for li, (vio, rout, hb) in enumerate(lad):
                    def p3_nw(v, vio=vio, rout=rout, hb=hb, hg=hg,
                              a0=a0m[li]):
                        v.wait_ge(ac, a0)
                        v.tensor_tensor(vio, vio, rout, ALU.mult)
                        v.tensor_tensor(vio, vio, rout, ALU.mult)
                        v.tensor_scalar(vio, vio, -0.5, 1.5,
                                        ALU.mult, ALU.add)
                        v.tensor_tensor(rout, rout, vio, ALU.mult)
                        v.tensor_single_scalar(rout, rout, 4.0, ALU.mult)
                        for hh in range(2):
                            h = hg * 4 + hb + hh
                            ins = v.tensor_tensor(A[:, h, :], A[:, h, :],
                                                  rout[:, hh, :], ALU.mult)
                        ins.then_inc(ve, 1)
                    on("vector", p3_nw)
                    led.inc("ve")

            if rc == 0:
                xbig_free[0] = led.val("ve")
            elif pending_tail:
                pending_tail.pop(0)()

            emit_barrier()
            if "qn" in tap_dram:
                def tqn(q, rc=rc):
                    dma_out(q, tap_dram["qn"][:, :, rc * RB:(rc + 1) * RB],
                            A[:, :, :])
                on("sync", tqn)

            for h in range(NH):
                pe_base = led.val("pe")
                ac_base = led.val("ac")
                ve_att_prev = led.val("ve")

                plan = ["sim0", "sim1"]
                for jc in range(NJ):
                    plan.append("sa%d" % jc)
                    if jc + 2 < NJ:
                        plan.append("sim%d" % (jc + 2))
                sim_mark, av_mark = {}, {}
                m = pe_base
                for stp in plan:
                    m += 1
                    if stp.startswith("sim"):
                        sim_mark[int(stp[3:])] = m
                    else:
                        av_mark[int(stp[2:])] = m

                def p4_pe(tn, h=h, rc=rc, pe_base=pe_base, ac_base=ac_base,
                          ve_att_prev=ve_att_prev, plan=plan):
                    if h == 0 and rc == 0:
                        tn.wait_ge(LD["agf"], t_agf)
                    for stp in plan:
                        if stp.startswith("sim"):
                            jc = int(stp[3:])
                            if jc < 2:
                                tn.wait_ge(ac, ac_base)
                            else:
                                tn.wait_ge(ac, ac_base + jc - 1)
                            tn.matmul(ps[jc % 2][:, :],
                                      kT[:, jc * 128:(jc + 1) * 128],
                                      A[:, h, :], start=True,
                                      stop=True).then_inc(pe, 1)
                        else:
                            jc = int(stp[2:])
                            if jc == 0:
                                tn.wait_ge(ve, ve_att_prev)
                            tn.wait_ge(ac, ac_base + jc + 1)
                            tn.matmul(ps[3][:, :], sOne[:, :],
                                      ptil[:, jc % 2, :],
                                      start=(jc == 0), stop=(jc == NJ - 1))
                            tn.matmul(ps[2][:, :], vv[:, jc, :],
                                      ptil[:, jc % 2, :],
                                      start=(jc == 0),
                                      stop=(jc == NJ - 1)).then_inc(pe, 1)
                on("tensor", p4_pe)
                for _ in range(2 * NJ):
                    led.inc("pe")
                assert led.val("pe") == m

                def p4_exp(s, sim_mark=sim_mark, av_mark=av_mark):
                    for jc in range(NJ):
                        s.wait_ge(pe, sim_mark[jc])
                        if jc >= 2:
                            s.wait_ge(pe, av_mark[jc - 2])
                        s.activation(ptil[:, jc % 2, :], ps[jc % 2][:, :],
                                     AF.Exp).then_inc(ac, 1)
                on("scalar", p4_exp)
                exp_marks = [led.inc("ac") for _ in range(NJ)]
                assert exp_marks[-1] == ac_base + NJ

                pe_last = av_mark[NJ - 1]

                def p4_cp(v, pe_last=pe_last):
                    v.wait_ge(pe, pe_last)
                    v.tensor_copy(S[:, 2, :], ps[3][:, :]).then_inc(ve, 1)
                on("vector", p4_cp)
                led.inc("ve")
                emit_recip(S[:, 2, :], S[:, 1, :], S[:, 4, :], S[:, 0, :])

                def p4_out(v, h=h):
                    v.tensor_tensor(B[:, h, :], ps[2][:, :], S[:, 1, :],
                                    ALU.mult).then_inc(ve, 1)
                on("vector", p4_out)
                led.inc("ve")

            emit_barrier()
            if "att" in tap_dram:
                def tat(q, rc=rc):
                    dma_out(q, tap_dram["att"][:, :, rc * RB:(rc + 1) * RB],
                            B[:, :, :])
                on("sync", tat)

            ve_p5start = led.val("ve")
            sqg = {}
            for fg in range(4):
                pe_base = led.val("pe")
                ve_prev = led.val("ve")
                for hdc in range(NF):
                    def p5_ld(q, fg=fg, hdc=hdc, pe_base=pe_base,
                              ve_prev=ve_prev):
                        if hdc >= 2:
                            q.wait_ge(pe, pe_base + hdc - 1)
                        if hdc == 0:
                            q.wait_ge(ve, ve_prev)
                        dma_in(q, "w%d" % (hdc % 2), wst[:, hdc % 2, :],
                               Wo_t[fg, hdc, :, :])
                    on("sync", p5_ld)
                    t_ld = led.val("ld_w%d" % (hdc % 2))

                    def p5_mm(tn, hdc=hdc, t_ld=t_ld):
                        tn.wait_ge(LD["w%d" % (hdc % 2)], t_ld)
                        for ff in range(4):
                            ins = tn.matmul(
                                ps[ff][:, :],
                                wst[:, hdc % 2, ff * 128:(ff + 1) * 128],
                                B[:, hdc, :], start=(hdc == 0),
                                stop=(hdc == NF - 1))
                        ins.then_inc(pe, 1)
                    on("tensor", p5_mm)
                    led.inc("pe")

                pe_fg = led.val("pe")

                def p5_ev(v, fg=fg, rc=rc, pe_fg=pe_fg):
                    v.wait_ge(pe, pe_fg)
                    for ff in range(4):
                        ins = v.tensor_copy(
                            C[:, fg * 4 + ff, rc * RB:(rc + 1) * RB],
                            ps[ff][:, :])
                    ins.then_inc(ve, 1)
                on("vector", p5_ev)
                ev_m = led.inc("ve")

                for ff in range(4):
                    fc = fg * 4 + ff

                    def p5_sq(s, fc=fc, rc=rc, ev_m=ev_m, g=sqg.get(fc - 2)):
                        s.wait_ge(ve, ev_m)
                        if g:
                            s.wait_ge(pe, g)
                        s.activation(SQ[:, fc % 2, :],
                                     C[:, fc, rc * RB:(rc + 1) * RB],
                                     AF.Square).then_inc(ac, 1)
                    on("scalar", p5_sq)
                    ac_m = led.inc("ac")

                    def p5_smm(tn, fc=fc, rc=rc, ac_m=ac_m, ev_m=ev_m,
                               ve_p5start=ve_p5start):
                        if fc == 0:
                            tn.wait_ge(ve, ve_p5start)
                        tn.wait_ge(ve, ev_m)
                        tn.matmul(ps[4][:, :], sOne[:, :],
                                  C[:, fc, rc * RB:(rc + 1) * RB],
                                  start=(fc == 0), stop=(fc == NF - 1))
                        tn.wait_ge(ac, ac_m)
                        tn.matmul(ps[5][:, :], sOne[:, :], SQ[:, fc % 2, :],
                                  start=(fc == 0),
                                  stop=(fc == NF - 1)).then_inc(pe, 1)
                    on("tensor", p5_smm)
                    sqg[fc] = led.inc("pe")

            if rc == 0:
                pending_tail.append(
                    lambda oblk=oblk: emit_tail(0, oblk, xbig_free[0]))
            else:
                emit_tail(1, oblk)

            emit_barrier()

        if "Cmid" in tap_dram:
            on("sync",
               lambda q: dma_out(q, tap_dram["Cmid"][:, :, :], C[:, :, :]))

        att_stack.close()
        xq = e(nc.sbuf_tensor("xq", [P, 2, NF, RB], FP8))

        ve_hop2 = led.val("ve")

        def p6_comb(v, ve_hop2=ve_hop2):
            v.wait_ge(ve, ve_hop2)
            v.reduce_max(amc[:, 1:2], amcols[:, 64:96], AX.X).then_inc(ve, 1)
        on("vector", p6_comb)
        led.inc("ve")
        ve_am2 = led.val("ve")

        def p6_tr(tn, ve_am2=ve_am2):
            tn.wait_ge(ve, ve_am2)
            tn.transpose(ps[7][0:1, 0:P], amc[:, 1:2], sI[:, :]).then_inc(pe, 1)
        on("tensor", p6_tr)
        pe_tr2 = led.inc("pe")

        def p6_loc(v, pe_tr2=pe_tr2):
            v.wait_ge(pe, pe_tr2)
            v.reduce_max(tiny[0:1, 1:2], ps[7][0:1, 0:P], AX.X).then_inc(ve, 1)
        on("vector", p6_loc)
        ve_loc = led.inc("ve")

        def p6_ccout(q, ve_loc=ve_loc):
            q.wait_ge(ve, ve_loc)
            dma_in(q, "cc2", ccin[:, :], tiny[0:1, 1:2])
        on("sync", p6_ccout)
        t_ldcc = led.val("ld_cc2")

        def p6_cc(g, t_ldcc=t_ldcc):
            g.wait_ge(LD["cc2"], t_ldcc)
            g.collective_compute(
                "AllReduce", ALU.max, replica_groups=[list(range(8))],
                ins=[ccin.ap().opt()], outs=[ccout.ap().opt()]).then_inc(cc, 1)
        on("gpsimd", p6_cc)
        led.inc("cc")

        def p6_ccin(g):
            g.wait_ge(cc, 2)
            dma_in(g, "cc3", tiny[0:1, 2:3], ccout[:, :])
        on("gpsimd", p6_ccin)
        t_ldg = led.val("ld_cc3")

        def p6_scale(v, t_ldg=t_ldg):
            b0 = led.val("ve")
            v.wait_ge(LD["cc3"], t_ldg)
            v.tensor_single_scalar(tiny[0:1, 3:4], tiny[0:1, 2:3], INV448,
                                   ALU.mult).then_inc(ve, 1)
            v.wait_ge(ve, b0 + 1)
            v.tensor_single_scalar(tiny[0:1, 3:4], tiny[0:1, 3:4], 1e-12,
                                   ALU.max).then_inc(ve, 1)
            v.wait_ge(ve, b0 + 2)
            v.reciprocal(S[0:1, 0, 0:1], tiny[0:1, 3:4]).then_inc(ve, 1)
            v.wait_ge(ve, b0 + 3)
            v.tensor_single_scalar(S[0:1, 0, 1:2], S[0:1, 0, 0:1], 0.5,
                                   ALU.mult).then_inc(ve, 1)
            v.wait_ge(ve, b0 + 4)
            v.tensor_tensor(S[0:1, 0, 2:3], tiny[0:1, 3:4], tiny[0:1, 0:1],
                            ALU.mult).then_inc(ve, 1)
            v.wait_ge(ve, b0 + 5)
            v.tensor_single_scalar(S[0:1, 0, 2:3], S[0:1, 0, 2:3], 4.0,
                                   ALU.mult).then_inc(ve, 1)
        on("vector", p6_scale)
        for _ in range(5):
            led.inc("ve")
        ve_row = led.inc("ve")

        def p6_bc(tn, ve_row=ve_row):
            tn.wait_ge(ve, ve_row)
            tn.matmul(ps[7][0:P, 0:3], sOne[0:1, 0:P], S[0:1, 0, 0:3],
                      start=True, stop=True).then_inc(pe, 1)
        on("tensor", p6_bc)
        pe_bc2 = led.inc("pe")

        def p6_bc_ev(v, pe_bc2=pe_bc2):
            v.wait_ge(pe, pe_bc2)
            v.tensor_copy(SCAL[:, 2:5], ps[7][0:P, 0:3]).then_inc(ve, 1)
        on("vector", p6_bc_ev)
        ve_scal_x = led.inc("ve")

        if "tiny" in tap_dram:
            on("sync", lambda q: dma_out(q, tap_dram["tiny"][:, :], tiny[:, :]))
        if "scal" in tap_dram:
            on("sync", lambda q: dma_out(q, tap_dram["scal"][:, :], SCAL[:, :]))

        ve_xq_rc = [None, None]
        for rc in range(2):
            for fc in range(NF):
                def p7_q(v, fc=fc, rc=rc, last=(fc == NF - 1)):
                    if rc == 0 and fc == 0:
                        v.wait_ge(ve, ve_scal_x)
                    v.tensor_tensor(S[:, 0, :], C[:, fc, rc * RB:(rc + 1) * RB],
                                    Smu3[:, rc, :], ALU.subtract)
                    v.tensor_tensor(S[:, 0, :], S[:, 0, :], Srho3[:, rc, :],
                                    ALU.mult)
                    ins = emit_quant(v, S[:, 0, :], xq[:, rc, fc, :],
                                     SCAL[:, 3:4])
                    if last:
                        ins.then_inc(ve, 1)
                on("vector", p7_q)
            ve_xq_rc[rc] = led.inc("ve")

            if "xq" in tap_dram:
                def txq(q, rc=rc, ve_xq=ve_xq_rc[rc]):
                    q.wait_ge(ve, ve_xq)
                    dma_out(q, tap_dram["xq"][:, :, rc * RB:(rc + 1) * RB],
                            xq[:, rc, :, :])
                on("sync", txq)

        w8_gate = {0: 0, 1: 0, 2: 0}
        fin_gate = {0: 0, 1: 0}
        for rc in range(2):

            for og in range(4):
                gidx = rc * 4 + og
                bb = 4 * (gidx % 2)
                pe_base = led.val("pe")
                ve_xq = ve_xq_rc[rc]
                for pr in range(NF // 2):
                    slot = (gidx * 8 + pr) % 3
                    g = w8_gate[slot]

                    def p7_ld(q, og=og, pr=pr, slot=slot, g=g):
                        if g:
                            q.wait_ge(pe, g)
                        dma_in(q, "w8%d" % slot, w8st[:, slot, :],
                               w8d[og, pr, :, :])
                    on("sync", p7_ld)
                    t_ld = led.val("ld_w8%d" % slot)

                    ve_bank = fin_gate[gidx % 2]

                    def p7_mm(tn, pr=pr, rc=rc, slot=slot, bb=bb, t_ld=t_ld,
                              ve_bank=ve_bank, ve_xq=ve_xq):
                        tn.wait_ge(LD["w8%d" % slot], t_ld)
                        if pr == 0:
                            tn.wait_ge(ve, ve_xq)
                            if ve_bank:
                                tn.wait_ge(ve, ve_bank)
                        for i in range(2):
                            fc = pr * 2 + i
                            for oi in range(4):
                                ins = tn.matmul(
                                    ps[bb + oi][:, :],
                                    w8st[:, slot,
                                         i * 512 + oi * 128:
                                         i * 512 + (oi + 1) * 128],
                                    xq[:, rc, fc, :], start=(fc == 0),
                                    stop=(fc == NF - 1))
                            ins.then_inc(pe, 1)
                    on("tensor", p7_mm)
                    led.inc("pe")
                    w8_gate[slot] = led.inc("pe")

                pe_og = led.val("pe")

                def p7_fin(v, og=og, rc=rc, bb=bb, pe_og=pe_og):
                    v.wait_ge(pe, pe_og)
                    for oi in range(4):
                        o = og * 4 + oi
                        ctile = C[:, o, rc * RB:(rc + 1) * RB]
                        v.tensor_scalar(qb1[:, :], ps[bb + oi][:, :],
                                        SCAL[:, 4:5], None, ALU.mult)
                        ins = v.tensor_tensor(ctile, ctile, qb1[:, :], ALU.add)
                    ins.then_inc(ve, 1)
                on("vector", p7_fin)
                ve_fin = led.inc("ve")
                fin_gate[gidx % 2] = ve_fin

                def p7_st(q, og=og, rc=rc, ve_fin=ve_fin):
                    q.wait_ge(ve, ve_fin)
                    for oi in range(4):
                        o = og * 4 + oi
                        dma_out(q, outT[o, :, rc * RB:(rc + 1) * RB],
                                C[:, o, rc * RB:(rc + 1) * RB])
                on("scalar", p7_st)

        if "C" in tap_dram:
            on("sync", lambda q: dma_out(q, tap_dram["C"][:, :, :], C[:, :, :]))

        so_total = led.val("so")

        def fin(q, so_total=so_total):
            q.wait_ge(so, so_total)
        on("sync", fin)

    return nc


def _layout_inputs(L0_attn_x, L0_attn_g_norm, L0_attn_Wq, L0_attn_Wkv,
                   L0_attn_Wout, L0_attn_g_out, L0_ffn_weight):
    x = np.ascontiguousarray(np.asarray(L0_attn_x, dtype=np.float32))
    Wq = np.ascontiguousarray(np.asarray(L0_attn_Wq, dtype=np.float32))
    Wkv = np.ascontiguousarray(np.asarray(L0_attn_Wkv, dtype=np.float32))
    Wout = np.ascontiguousarray(np.asarray(L0_attn_Wout, dtype=np.float32))
    Wt = np.asarray(L0_ffn_weight)
    gN = np.ascontiguousarray(np.asarray(L0_attn_g_norm, dtype=np.float32))
    gO = np.ascontiguousarray(np.asarray(L0_attn_g_out, dtype=np.float32))

    Wq_t = np.ascontiguousarray(Wq.reshape(NF, P, 4, 512).transpose(2, 0, 1, 3))
    Wkv_t = np.ascontiguousarray(Wkv.reshape(NF, P, 256))
    Wo_t = np.ascontiguousarray(Wout.reshape(NF, P, 4, 512).transpose(2, 0, 1, 3))
    WtT = np.ascontiguousarray(Wt.T).reshape(NF, P, 2048)
    gN2 = np.ascontiguousarray(gN.reshape(NF, P).T)
    gO2 = np.ascontiguousarray(gO.reshape(NF, P).T)
    iden = np.eye(P, dtype=np.float32)

    in_maps = []
    for c in range(8):
        b, half = c // 2, c % 2
        order = [2 * (1 - half), 2 * (1 - half) + 1, 2 * half, 2 * half + 1]
        xb = np.empty((NB, P, NF, RB), dtype=np.float32)
        for i, bi in enumerate(order):
            blkd = x[b, bi * RB:(bi + 1) * RB, :]
            xb[i] = blkd.T.reshape(NF, P, RB).transpose(1, 0, 2)
        in_maps.append(dict(xblk=xb, Wq_t=Wq_t, Wkv_t=Wkv_t, Wo_t=Wo_t,
                            WtT=WtT, gN2=gN2, gO2=gO2, iden=iden))
    return in_maps


def kernel(L0_attn_x, L0_attn_g_norm, L0_attn_Wq, L0_attn_Wkv, L0_attn_Wout,
           L0_attn_g_out, L0_ffn_weight):
    in_maps = _layout_inputs(L0_attn_x, L0_attn_g_norm, L0_attn_Wq,
                             L0_attn_Wkv, L0_attn_Wout, L0_attn_g_out,
                             L0_ffn_weight)
    nc = build_program(taps=tuple(_TAPS))
    res = run_bass_kernel_spmd(nc, in_maps, core_ids=list(range(8)))
    out = np.empty((4, 2048, 2048), dtype=np.float32)
    for c in range(8):
        b, half = c // 2, c % 2
        o = np.asarray(res.results[c]["outT"])
        out[b, half * 1024:(half + 1) * 1024, :] = (
            o.transpose(2, 0, 1).reshape(1024, 2048))
    if _TAPS:
        kernel._last_results = res
    return out


# version 26 (final)
# speedup vs baseline: 1.1514x; 1.0091x over previous
from contextlib import ExitStack

import numpy as np
import ml_dtypes
import concourse.bass as bass
import concourse.mybir as mybir
from concourse.bass_utils import run_bass_kernel_spmd

F32 = mybir.dt.float32
BF16 = mybir.dt.bfloat16
FP8 = mybir.dt.float8e4
U8 = mybir.dt.uint8
AF = mybir.ActivationFunctionType
ALU = mybir.AluOpType
AX = mybir.AxisListType

P = 128
NF = 16
NH = 16
NJ = 16
NB = 4
RB = 512
EPS = 1e-5
L2EPS = 1e-12
INV2048 = 1.0 / 2048.0
INV448 = 1.0 / 448.0

_TAPS = []


class Led:
    def __init__(self):
        self.n = {}

    def inc(self, s, k=1):
        self.n[s] = self.n.get(s, 0) + k
        return self.n[s]

    def val(self, s):
        return self.n.get(s, 0)


def build_program(taps=()):
    nc = bass.Bass("TRN2", target_bir_lowering=False, detect_race_conditions=False)

    xblk = nc.dram_tensor("xblk", [NB, P, NF, RB], F32, kind="ExternalInput")
    Wq_t = nc.dram_tensor("Wq_t", [4, NF, P, 512], F32, kind="ExternalInput")
    Wkv_t = nc.dram_tensor("Wkv_t", [NF, P, 256], F32, kind="ExternalInput")
    Wo_t = nc.dram_tensor("Wo_t", [4, NF, P, 512], F32, kind="ExternalInput")
    WtT = nc.dram_tensor("WtT", [NF, P, 2048], BF16, kind="ExternalInput")
    gN2 = nc.dram_tensor("gN2", [P, NF], F32, kind="ExternalInput")
    gO2 = nc.dram_tensor("gO2", [P, NF], F32, kind="ExternalInput")
    iden = nc.dram_tensor("iden", [P, P], F32, kind="ExternalInput")

    outT = nc.dram_tensor("outT", [NF, P, 1024], F32, kind="ExternalOutput")

    w8d = nc.dram_tensor("w8d", [4, NF // 2, P, 1024], FP8)
    ccin = nc.dram_tensor("ccin", [1, 1], F32)
    ccout = nc.dram_tensor("ccout", [1, 1], F32)
    agin = nc.dram_tensor("agin", [P, 2048], F32)
    agout = nc.dram_tensor("agout", [2, P, 2048], F32)

    tap_dram = {}
    for t in taps:
        shp, dt = {
            "kT": ([P, 2048], F32), "v": ([P, NJ, P], F32),
            "qn": ([P, NH, 2 * RB], F32), "att": ([P, NF, 2 * RB], F32),
            "C": ([P, NF, 1024], F32), "xq": ([P, NF, 2 * RB], BF16),
            "stat1": ([P, 8, RB], F32), "w8chk": ([4, NF, P, 512], BF16),
            "scal": ([P, 8], F32), "tiny": ([1, 8], F32),
            "Cmid": ([P, NF, 1024], F32),
        }[t]
        tap_dram[t] = nc.dram_tensor("tap_" + t, shp, dt, kind="ExternalOutput")

    led = Led()

    with ExitStack() as stx:
        e = stx.enter_context

        LDKEYS = ("xb0", "xb1", "xg0", "xg1", "w0", "w1",
                  "kv0", "kv1", "kv2", "kv3",
                  "wt0", "wt1", "wt2", "wt3",
                  "w80", "w81", "w82", "w83", "cst", "cc2",
                  "cc3", "agp", "agf")
        LD = {k: e(nc.semaphore("ld_" + k)) for k in LDKEYS}
        so = e(nc.semaphore("so"))
        pe = e(nc.semaphore("pe"))
        ve = e(nc.semaphore("ve"))
        ac = e(nc.semaphore("ac"))
        cc = e(nc.semaphore("cc"))
        SEMS = {"so": so, "pe": pe, "ve": ve, "ac": ac}
        for k in LDKEYS:
            SEMS["ld_" + k] = LD[k]

        sI = e(nc.sbuf_tensor("sI", [P, P], F32))
        sOne = e(nc.sbuf_tensor("sOne", [P, P], F32))
        sGN = e(nc.sbuf_tensor("sGN", [P, NF], F32))
        sGO = e(nc.sbuf_tensor("sGO", [P, NF], F32))
        SCAL = e(nc.sbuf_tensor("SCAL", [P, 8], F32))
        tiny = e(nc.sbuf_tensor("tiny", [1, 8], F32))
        amc = e(nc.sbuf_tensor("amc", [P, 2], F32))
        amcols = e(nc.sbuf_tensor("amcols", [P, 100], F32))
        A = e(nc.sbuf_tensor("A", [P, NH, RB], F32))
        B = e(nc.sbuf_tensor("B", [P, NF, RB], F32))
        C = e(nc.sbuf_tensor("C", [P, NF, 1024], F32))
        S = e(nc.sbuf_tensor("S", [P, 5, RB], F32))
        SQ = e(nc.sbuf_tensor("SQ", [P, 2, RB], F32))
        Smu_o = e(nc.sbuf_tensor("Smu_o", [P, 2, RB], F32))
        Sr12_o = e(nc.sbuf_tensor("Sr12_o", [P, 2, RB], F32))
        Smu3 = e(nc.sbuf_tensor("Smu3", [P, 2, RB], F32))
        Srho3 = e(nc.sbuf_tensor("Srho3", [P, 2, RB], F32))
        xbig = e(nc.sbuf_tensor("xbig", [P, 4, RB], F32))
        wst = e(nc.sbuf_tensor("wst", [P, 2, 512], F32))
        wkvst = e(nc.sbuf_tensor("wkvst", [P, 4, 256], F32))
        wtst = e(nc.sbuf_tensor("wtst", [P, 4, 512], BF16))
        w8st = e(nc.sbuf_tensor("w8st", [P, 3, 1024], FP8))
        qb1 = e(nc.sbuf_tensor("qb1", [P, 512], BF16))
        w8c = e(nc.sbuf_tensor("w8c", [P, 4, 512], FP8))

        att_stack = ExitStack()
        kT = att_stack.enter_context(nc.sbuf_tensor("kT", [P, 2048], F32))
        vv = att_stack.enter_context(nc.sbuf_tensor("vv", [P, NJ, P], F32))
        ptil = att_stack.enter_context(nc.sbuf_tensor("ptil", [P, 2, RB], F32))

        ps = [e(nc.psum_tensor(f"ps{i}", [P, 512], F32)) for i in range(8)]

        block = e(nc.Block())

        def on(engname, fn):
            getattr(block, engname)(fn)

        def dma_in(q, stream, out_ap, in_ap):
            q.dma_start(out=out_ap, in_=in_ap).then_inc(LD[stream], 16)
            return led.inc("ld_" + stream, 16)

        def dma_out(q, out_ap, in_ap):
            q.dma_start(out=out_ap, in_=in_ap).then_inc(so, 16)
            return led.inc("so", 16)

        def emit_barrier():
            bv = {s: led.val(s) for s in SEMS}

            def mk(eng):
                for nm, val in bv.items():
                    if val:
                        eng.wait_ge(SEMS[nm], val)
            for en in ("sync", "tensor", "vector", "scalar"):
                on(en, mk)

        def emit_rsqrt(vio, rout, epsb, t1):
            def dv0(v):
                v.tensor_single_scalar(vio, vio, float(epsb), ALU.add).then_inc(ve, 1)
            on("vector", dv0)
            v0 = led.inc("ve")

            def sc0(s):
                b0 = led.val("ac")
                s.wait_ge(ve, v0)
                s.activation(t1, vio, AF.Ln).then_inc(ac, 1)
                s.wait_ge(ac, b0 + 1)
                s.activation(rout, t1, AF.Exp, scale=-0.5).then_inc(ac, 1)
            on("scalar", sc0)
            led.inc("ac")
            a0 = led.inc("ac")

            def dv1(v):
                v.wait_ge(ac, a0)
                v.tensor_tensor(t1, rout, rout, ALU.mult)
                v.scalar_tensor_tensor(t1, t1, -0.5, vio, ALU.mult, ALU.mult)
                v.tensor_single_scalar(t1, t1, 1.5, ALU.add)
                v.tensor_tensor(rout, rout, t1, ALU.mult).then_inc(ve, 1)
            on("vector", dv1)
            return led.inc("ve")

        def emit_recip(vin, rout, t0, t1):
            def sc0(s):
                b0 = led.val("ac")
                s.wait_ge(ve, led.val("ve"))
                s.wait_ge(pe, led.val("pe"))
                s.activation(t0, vin, AF.Ln).then_inc(ac, 1)
                s.wait_ge(ac, b0 + 1)
                s.activation(rout, t0, AF.Exp, scale=-1.0).then_inc(ac, 1)
            on("scalar", sc0)
            led.inc("ac")
            a0 = led.inc("ac")

            def dv1(v):
                v.wait_ge(ac, a0)
                v.tensor_tensor(t1, vin, rout, ALU.mult)
                v.tensor_scalar(t1, t1, -1.0, 2.0, ALU.mult, ALU.add)
                v.tensor_tensor(rout, rout, t1, ALU.mult).then_inc(ve, 1)
            on("vector", dv1)
            return led.inc("ve")

        def emit_quant(v, src_ap, dst, rs2_col):
            return v.tensor_scalar(dst, src_ap, rs2_col, None, ALU.mult)

        def p0_sync(q):
            dma_in(q, "cst", sI[:, :], iden[:, :])
            dma_in(q, "cst", sGN[:, :], gN2[:, :])
            dma_in(q, "cst", sGO[:, :], gO2[:, :])
        on("sync", p0_sync)
        ld_cst = led.val("ld_cst")

        def p0_act(s):
            from concourse.hw_specs import get_activation_tables
            idx = None
            for i, fns in enumerate(get_activation_tables(nc.m.arch).values()):
                names = {f.name for f in fns}
                if {"Exp", "Ln", "Square"} <= names:
                    idx = i
                    break
            assert idx is not None
            s.add_instruction(mybir.InstLoadActFuncSet(
                name=nc.get_next_instruction_name(),
                act_func_set_id=idx, ins=[], outs=[]))
        on("scalar", p0_act)

        def p0_vec(v):
            v.memset(sOne[:, :], 1.0)
            v.memset(tiny[:, :], 0.0)
            v.memset(SCAL[:, :], 0.0)
            v.memset(amc[:, :], 0.0).then_inc(ve, 1)
        on("vector", p0_vec)
        led.inc("ve")

        pw_units = []
        wt_gate = {0: 0, 1: 0, 2: 0, 3: 0}
        w8c_gate = {0: 0, 1: 0, 2: 0, 3: 0}
        pw_state = {"scal": None}

        def mk_amax_unit(t, qq):
            def unit():
                slot = (t * 4 + qq) % 4
                g = wt_gate[slot]

                def ld_(q):
                    if g:
                        q.wait_ge(ve, g)
                    dma_in(q, "wt%d" % slot, wtst[:, slot, :],
                           WtT[t, :, qq * 512:(qq + 1) * 512])
                on("sync", ld_)
                t_ld = led.val("ld_wt%d" % slot)

                def red_(v):
                    v.wait_ge(LD["wt%d" % slot], t_ld)
                    v.reduce_max(amcols[:, t * 4 + qq:t * 4 + qq + 1],
                                 wtst[:, slot, :], AX.X,
                                 apply_absolute_value=True).then_inc(ve, 1)
                on("vector", red_)
                wt_gate[slot] = led.inc("ve")
            return unit

        def pw_combine_unit():
            ve_hop = led.val("ve")

            def pw_comb(v):
                v.wait_ge(ve, ve_hop)
                v.reduce_max(amc[:, 0:1], amcols[:, 0:64], AX.X).then_inc(ve, 1)
            on("vector", pw_comb)
            ve_am = led.inc("ve")

            def pw_tr(tn):
                tn.wait_ge(ve, ve_am)
                tn.wait_ge(LD["cst"], ld_cst)
                tn.transpose(ps[7][0:1, 0:P], amc[:, 0:1],
                             sI[:, :]).then_inc(pe, 1)
            on("tensor", pw_tr)
            pe_tr = led.inc("pe")

            def pw_sw(v):
                b0 = led.val("ve")
                v.wait_ge(pe, pe_tr)
                v.reduce_max(tiny[0:1, 0:1], ps[7][0:1, 0:P],
                             AX.X).then_inc(ve, 1)
                v.wait_ge(ve, b0 + 1)
                v.tensor_single_scalar(tiny[0:1, 0:1], tiny[0:1, 0:1], INV448,
                                       ALU.mult).then_inc(ve, 1)
                v.wait_ge(ve, b0 + 2)
                v.tensor_single_scalar(tiny[0:1, 0:1], tiny[0:1, 0:1], 1e-12,
                                       ALU.max).then_inc(ve, 1)
                v.wait_ge(ve, b0 + 3)
                v.reciprocal(amcols[0:1, 96:97], tiny[0:1, 0:1]).then_inc(ve, 1)
                v.wait_ge(ve, b0 + 4)
                v.tensor_single_scalar(amcols[0:1, 97:98], amcols[0:1, 96:97],
                                       0.5, ALU.mult).then_inc(ve, 1)
            on("vector", pw_sw)
            for _ in range(4):
                led.inc("ve")
            ve_sw = led.inc("ve")

            def pw_bc(tn):
                tn.wait_ge(ve, ve_sw)
                tn.matmul(ps[7][0:P, 0:2], sOne[0:1, 0:P], amcols[0:1, 96:98],
                          start=True, stop=True).then_inc(pe, 1)
            on("tensor", pw_bc)
            pe_bc = led.inc("pe")

            def pw_bc_ev(v):
                v.wait_ge(pe, pe_bc)
                v.tensor_copy(SCAL[:, 0:2], ps[7][0:P, 0:2]).then_inc(ve, 1)
            on("vector", pw_bc_ev)
            pw_state["scal"] = led.inc("ve")

        def mk_quant_unit(t, qq):
            def unit():
                slot = (t * 4 + qq) % 4
                g = wt_gate[slot]
                sog = w8c_gate[slot]

                def ld_(q):
                    if g:
                        q.wait_ge(ve, g)
                    dma_in(q, "wt%d" % slot, wtst[:, slot, :],
                           WtT[t, :, qq * 512:(qq + 1) * 512])
                on("sync", ld_)
                t_ld = led.val("ld_wt%d" % slot)

                def q_(v):
                    v.wait_ge(LD["wt%d" % slot], t_ld)
                    v.wait_ge(ve, pw_state["scal"])
                    if sog:
                        v.wait_ge(so, sog)
                    emit_quant(v, wtst[:, slot, :], w8c[:, slot, :],
                               SCAL[:, 1:2]).then_inc(ve, 1)
                on("vector", q_)
                wt_gate[slot] = ve_q = led.inc("ve")

                def st_(s):
                    s.wait_ge(ve, ve_q)
                    dma_out(s, w8d[qq, t // 2, :,
                                   (t % 2) * 512:(t % 2) * 512 + 512],
                            w8c[:, slot, :])
                on("scalar", st_)
                w8c_gate[slot] = led.val("so")
            return unit

        for t in range(NF):
            for qq in range(4):
                pw_units.append(mk_amax_unit(t, qq))
        pw_units.append(pw_combine_unit)
        for t in range(NF):
            for qq in range(4):
                pw_units.append(mk_quant_unit(t, qq))

        def pump(n):
            for _ in range(n):
                if pw_units:
                    pw_units.pop(0)()

        xgate = [None, None]

        def xwait(q, sl):
            if xgate[sl] is not None:
                gp, ga, gv = xgate[sl]
                q.wait_ge(pe, gp)
                q.wait_ge(ac, ga)
                q.wait_ge(ve, gv)

        def xmark(sl):
            xgate[sl] = (led.val("pe"), led.val("ac"), led.val("ve"))

        for blk in (2, 3):
            mu_dst = Smu_o[:, blk - 2, :]
            r12_dst = Sr12_o[:, blk - 2, :]

            for g2 in range(8):
                sl = g2 % 2

                def p1_ld(q, blk=blk, g2=g2, sl=sl):
                    xwait(q, sl)
                    dma_in(q, "xb%d" % sl, xbig[:, sl * 2:sl * 2 + 2, :],
                           xblk[blk, :, g2 * 2:(g2 + 1) * 2, :])
                on("sync", p1_ld)
                t_ld = led.val("ld_xb%d" % sl)
                pe_base = led.val("pe")

                def p1_sq(s, sl=sl, t_ld=t_ld, pe_base=pe_base):
                    s.wait_ge(LD["xb%d" % sl], t_ld)
                    s.wait_ge(pe, pe_base)
                    for i in range(2):
                        s.activation(SQ[:, i, :], xbig[:, sl * 2 + i, :],
                                     AF.Square).then_inc(ac, 1)
                on("scalar", p1_sq)
                ac_marks = [led.inc("ac") for _ in range(2)]

                ve_emit = led.val("ve")

                def p1_mm(tn, g2=g2, sl=sl, t_ld=t_ld, ac_marks=ac_marks,
                          ve_emit=ve_emit):
                    tn.wait_ge(LD["xb%d" % sl], t_ld)
                    if g2 == 0:
                        tn.wait_ge(ve, ve_emit)
                    for i in range(2):
                        fc = g2 * 2 + i
                        tn.matmul(ps[0][:, :], sOne[:, :], xbig[:, sl * 2 + i, :],
                                  start=(fc == 0), stop=(fc == NF - 1))
                        tn.wait_ge(ac, ac_marks[i])
                        tn.matmul(ps[1][:, :], sOne[:, :], SQ[:, i, :],
                                  start=(fc == 0),
                                  stop=(fc == NF - 1)).then_inc(pe, 1)
                on("tensor", p1_mm)
                for _ in range(2):
                    led.inc("pe")
                xmark(sl)
                pump(4)

            pe_stats = led.val("pe")

            def p1_sc(v, mu_dst=mu_dst, pe_stats=pe_stats):
                v.wait_ge(pe, pe_stats)
                v.tensor_scalar(mu_dst, ps[0][:, :], INV2048, None, ALU.mult)
                v.tensor_scalar(S[:, 1, :], ps[1][:, :], INV2048, None, ALU.mult)
                v.tensor_tensor(S[:, 2, :], mu_dst, mu_dst, ALU.mult)
                v.tensor_tensor(S[:, 2, :], S[:, 1, :], S[:, 2, :],
                                ALU.subtract)
                v.tensor_single_scalar(S[:, 4, :], S[:, 2, :], 1.0,
                                       ALU.mult).then_inc(ve, 1)
            on("vector", p1_sc)
            led.inc("ve")
            emit_rsqrt(S[:, 4, :], S[:, 3, :], EPS, S[:, 1, :])

            def p1_sc2(v):
                v.tensor_tensor(S[:, 1, :], S[:, 3, :], S[:, 3, :], ALU.mult)
                v.tensor_tensor(S[:, 1, :], S[:, 1, :], S[:, 2, :],
                                ALU.mult).then_inc(ve, 1)
            on("vector", p1_sc2)
            led.inc("ve")
            emit_rsqrt(S[:, 1, :], S[:, 4, :], EPS, S[:, 2, :])

            def p1_sc3(v, r12_dst=r12_dst):
                v.tensor_tensor(r12_dst, S[:, 3, :], S[:, 4, :],
                                ALU.mult).then_inc(ve, 1)
            on("vector", p1_sc3)
            ve_r12 = led.inc("ve")

            for g2 in range(8):
                sl = g2 % 2

                def p1_ld2(q, blk=blk, g2=g2, sl=sl):
                    xwait(q, sl)
                    dma_in(q, "xb%d" % sl, xbig[:, sl * 2:sl * 2 + 2, :],
                           xblk[blk, :, g2 * 2:(g2 + 1) * 2, :])
                    for i in range(2):
                        fc = g2 * 2 + i
                        dma_in(q, "kv%d" % (fc % 4), wkvst[:, fc % 4, :],
                               Wkv_t[fc, :, :])
                on("sync", p1_ld2)
                t_ld = led.val("ld_xb%d" % sl)
                t_kv = {i: led.val("ld_kv%d" % i) for i in range(4)}
                pe_base = led.val("pe")

                if blk == 2:
                    xn2dst = [B[:, g2 * 2, :], B[:, g2 * 2 + 1, :]]
                else:
                    xn2dst = [C[:, g2 * 2, RB:2 * RB],
                              C[:, g2 * 2 + 1, RB:2 * RB]]

                def p1_ap(v, g2=g2, sl=sl, t_ld=t_ld, ve_r12=ve_r12,
                          pe_base=pe_base, mu_dst=mu_dst, r12_dst=r12_dst,
                          xn2dst=xn2dst):
                    v.wait_ge(LD["xb%d" % sl], t_ld)
                    v.wait_ge(ve, ve_r12)
                    v.wait_ge(pe, pe_base)
                    for i in range(2):
                        fc = g2 * 2 + i
                        v.tensor_tensor(S[:, 0, :], xbig[:, sl * 2 + i, :],
                                        mu_dst, ALU.subtract)
                        v.scalar_tensor_tensor(
                            xn2dst[i], S[:, 0, :], sGN[:, fc:fc + 1],
                            r12_dst, ALU.mult, ALU.mult).then_inc(ve, 1)
                on("vector", p1_ap)
                ve_marks = [led.inc("ve") for _ in range(2)]

                def p1_kv(tn, g2=g2, t_kv=t_kv, ve_marks=ve_marks,
                          xn2dst=xn2dst):
                    for i in range(2):
                        fc = g2 * 2 + i
                        tn.wait_ge(LD["kv%d" % (fc % 4)], t_kv[fc % 4])
                        tn.wait_ge(ve, ve_marks[i])
                        tn.matmul(ps[2][:, :], wkvst[:, fc % 4, 0:128],
                                  xn2dst[i],
                                  start=(fc == 0), stop=(fc == NF - 1))
                        tn.matmul(ps[3][:, :], wkvst[:, fc % 4, 128:256],
                                  xn2dst[i],
                                  start=(fc == 0),
                                  stop=(fc == NF - 1)).then_inc(pe, 1)
                on("tensor", p1_kv)
                for _ in range(2):
                    led.inc("pe")
                xmark(sl)
                pump(4)

            pe_kv = led.val("pe")

            def p1_ev(v, blk=blk, pe_kv=pe_kv):
                v.wait_ge(pe, pe_kv)
                v.tensor_copy(kT[:, (blk - 2) * RB:(blk - 1) * RB],
                              ps[2][:, :])
                v.tensor_copy(SQ[:, 0, :], ps[3][:, :]).then_inc(ve, 1)
            on("vector", p1_ev)
            ve_ev = led.inc("ve")

            ve_trbase = led.val("ve")
            for jj in range(4):
                jc = (blk - 2) * 4 + jj

                def p1_tr(tn, jj=jj, ve_ev=ve_ev, ve_trbase=ve_trbase):
                    tn.wait_ge(ve, ve_ev)
                    if jj >= 1:
                        tn.wait_ge(ve, ve_trbase + jj)
                    tn.transpose(ps[4][:, 0:128],
                                 SQ[:, 0, jj * 128:(jj + 1) * 128],
                                 sI[:, :]).then_inc(pe, 1)
                on("tensor", p1_tr)
                pe_t = led.inc("pe")

                def p1_trev(v, jc=jc, pe_t=pe_t):
                    v.wait_ge(pe, pe_t)
                    v.tensor_copy(vv[:, jc, :], ps[4][:, 0:128]).then_inc(ve, 1)
                on("vector", p1_trev)
                led.inc("ve")

        pump(len(pw_units))

        ve_kl0 = led.val("ve")
        pe_kl0 = led.val("pe")
        klm = []
        for b4 in range(2):
            def kl_sq(s, b4=b4, ve_kl0=ve_kl0, pe_kl0=pe_kl0):
                if b4 == 0:
                    s.wait_ge(ve, ve_kl0)
                    s.wait_ge(pe, pe_kl0)
                s.activation(ptil[:, b4, :], kT[:, b4 * RB:(b4 + 1) * RB],
                             AF.Square).then_inc(ac, 1)
            on("scalar", kl_sq)
            ac_m = led.inc("ac")

            def kl_mm(tn, b4=b4, ac_m=ac_m, ve_kl0=ve_kl0):
                tn.wait_ge(ac, ac_m)
                tn.wait_ge(ve, ve_kl0)
                tn.matmul(ps[4 + b4][:, :], sOne[:, :], ptil[:, b4, :],
                          start=True, stop=True).then_inc(pe, 1)
            on("tensor", kl_mm)
            klm.append(led.inc("pe"))

        def kl_ev(v, pe_l=klm[1]):
            v.wait_ge(pe, pe_l)
            v.tensor_copy(xbig[:, 0, :], ps[4][:, :])
            v.tensor_copy(xbig[:, 1, :], ps[5][:, :])
            v.tensor_single_scalar(xbig[:, 0:2, :], xbig[:, 0:2, :],
                                   float(L2EPS), ALU.add).then_inc(ve, 1)
        on("vector", kl_ev)
        kl_v0 = led.inc("ve")

        def kl_ln(s, v0=kl_v0):
            b0 = led.val("ac")
            s.wait_ge(ve, v0)
            s.activation(xbig[:, 2:4, :], xbig[:, 0:2, :],
                         AF.Ln).then_inc(ac, 1)
            s.wait_ge(ac, b0 + 1)
            s.activation(xbig[:, 2:4, :], xbig[:, 2:4, :], AF.Exp,
                         scale=-0.5).then_inc(ac, 1)
        on("scalar", kl_ln)
        led.inc("ac")
        kl_a0 = led.inc("ac")

        def kl_nw(v, a0=kl_a0):
            v.wait_ge(ac, a0)
            v.tensor_tensor(xbig[:, 0:2, :], xbig[:, 0:2, :],
                            xbig[:, 2:4, :], ALU.mult)
            v.tensor_tensor(xbig[:, 0:2, :], xbig[:, 0:2, :],
                            xbig[:, 2:4, :], ALU.mult)
            v.tensor_scalar(xbig[:, 0:2, :], xbig[:, 0:2, :], -0.5, 1.5,
                            ALU.mult, ALU.add)
            v.tensor_tensor(xbig[:, 2:4, :], xbig[:, 2:4, :],
                            xbig[:, 0:2, :], ALU.mult)
            v.tensor_single_scalar(xbig[:, 2:4, :], xbig[:, 2:4, :], 4.0,
                                   ALU.mult)
            for b4 in range(2):
                ins = v.tensor_tensor(kT[:, b4 * RB:(b4 + 1) * RB],
                                      kT[:, b4 * RB:(b4 + 1) * RB],
                                      xbig[:, 2 + b4, :], ALU.mult)
            ins.then_inc(ve, 1)
        on("vector", kl_nw)
        led.inc("ve")

        emit_barrier()
        if "kT" in tap_dram:
            on("sync", lambda q: dma_out(q, tap_dram["kT"][:, :], kT[:, :]))
        if "v" in tap_dram:
            on("sync", lambda q: dma_out(q, tap_dram["v"][:, :, :], vv[:, :, :]))
        if "stat1" in tap_dram:
            def ts1(q):
                dma_out(q, tap_dram["stat1"][:, 0:2, :], Smu_o[:, :, :])
                dma_out(q, tap_dram["stat1"][:, 2:4, :], Sr12_o[:, :, :])
            on("sync", ts1)

        ve_pack = led.val("ve")

        def ag_pack(q, ve_pack=ve_pack):
            q.wait_ge(ve, ve_pack)
            dma_in(q, "agp", agin[:, 0:1024], kT[:, 0:1024])
            dma_in(q, "agp", agin[:, 1024:2048], vv[:, 0:8, :])
        on("sync", ag_pack)
        t_agp = led.val("ld_agp")

        def ag_cc(g, t_agp=t_agp):
            g.wait_ge(LD["agp"], t_agp)
            g.collective_compute(
                "AllGather", ALU.bypass,
                replica_groups=[[2 * i, 2 * i + 1] for i in range(4)],
                ins=[agin.ap().opt()],
                outs=[agout.ap().opt()]).then_inc(cc, 1)
        on("gpsimd", ag_cc)
        led.inc("cc")

        def ag_fetch(g):
            g.wait_ge(cc, 1)
            dma_in(g, "agf", kT[:, 0:1024], agout[0, :, 0:1024])
            dma_in(g, "agf", kT[:, 1024:2048], agout[1, :, 0:1024])
            dma_in(g, "agf", vv[:, 0:8, :], agout[0, :, 1024:2048])
            dma_in(g, "agf", vv[:, 8:16, :], agout[1, :, 1024:2048])
        on("gpsimd", ag_fetch)
        t_agf = led.val("ld_agf")

        def emit_tail(rc, oblk, xfree=None, pe_sums=None):
            pe_st = pe_sums if pe_sums is not None else led.val("pe")

            def p5_sc(v, pe_st=pe_st):
                v.wait_ge(pe, pe_st)
                v.tensor_scalar(S[:, 1, :], ps[4][:, :], INV2048, None,
                                ALU.mult)
                v.tensor_scalar(S[:, 2, :], ps[5][:, :], INV2048, None,
                                ALU.mult)
                v.tensor_tensor(S[:, 0, :], S[:, 1, :], S[:, 1, :], ALU.mult)
                v.tensor_tensor(S[:, 2, :], S[:, 2, :], S[:, 0, :],
                                ALU.subtract).then_inc(ve, 1)
            on("vector", p5_sc)
            sc_m = led.inc("ve")
            emit_rsqrt(S[:, 2, :], S[:, 3, :], EPS, S[:, 4, :])

            resid_marks = []
            for g2 in range(8):
                sl = g2 % 2

                def p5_xl(g, oblk=oblk, g2=g2, sl=sl, xfree=xfree):
                    xwait(g, sl)
                    if xfree and g2 < 2:
                        g.wait_ge(ve, xfree)
                    dma_in(g, "xg%d" % sl, xbig[:, sl * 2:sl * 2 + 2, :],
                           xblk[oblk, :, g2 * 2:(g2 + 1) * 2, :])
                on("gpsimd", p5_xl)
                t_ld = led.val("ld_xg%d" % sl)

                def p5_res(v, g2=g2, sl=sl, rc=rc, t_ld=t_ld):
                    v.wait_ge(LD["xg%d" % sl], t_ld)
                    for i in range(2):
                        fc = g2 * 2 + i
                        ctile = C[:, fc, rc * RB:(rc + 1) * RB]
                        v.tensor_tensor(S[:, 0, :], ctile, S[:, 1, :],
                                        ALU.subtract)
                        v.scalar_tensor_tensor(S[:, 0, :], S[:, 0, :],
                                               sGO[:, fc:fc + 1], S[:, 3, :],
                                               ALU.mult, ALU.mult)
                        ins = v.tensor_tensor(ctile, xbig[:, sl * 2 + i, :],
                                              S[:, 0, :], ALU.add)
                    ins.then_inc(ve, 1)
                on("vector", p5_res)
                resid_marks.append(led.inc("ve"))
                xmark(sl)

            pe6_base = led.val("pe")
            for fc in range(NF):
                def p6_sq(s, fc=fc, rc=rc, pe6_base=pe6_base,
                          rm=resid_marks[fc // 2]):
                    s.wait_ge(ve, rm)
                    s.wait_ge(pe, pe6_base + max(0, fc - 1))
                    s.activation(ptil[:, fc % 2, :],
                                 C[:, fc, rc * RB:(rc + 1) * RB],
                                 AF.Square).then_inc(ac, 1)
                on("scalar", p6_sq)
                ac_m = led.inc("ac")

                def p6_smm(tn, fc=fc, rc=rc, ac_m=ac_m, sc_m=sc_m,
                           rm=resid_marks[fc // 2]):
                    if fc == 0:
                        tn.wait_ge(ve, sc_m)
                    tn.wait_ge(ve, rm)
                    tn.matmul(ps[4][:, :], sOne[:, :],
                              C[:, fc, rc * RB:(rc + 1) * RB],
                              start=(fc == 0), stop=(fc == NF - 1))
                    tn.wait_ge(ac, ac_m)
                    tn.matmul(ps[5][:, :], sOne[:, :], ptil[:, fc % 2, :],
                              start=(fc == 0),
                              stop=(fc == NF - 1)).then_inc(pe, 1)
                on("tensor", p6_smm)
                led.inc("pe")

            pe_st6 = led.val("pe")

            def p6_sc(v, rc=rc, pe_st6=pe_st6):
                v.wait_ge(pe, pe_st6)
                v.tensor_scalar(Smu3[:, rc, :], ps[4][:, :], INV2048, None,
                                ALU.mult)
                v.tensor_scalar(S[:, 2, :], ps[5][:, :], INV2048, None,
                                ALU.mult)
                v.tensor_tensor(S[:, 0, :], Smu3[:, rc, :], Smu3[:, rc, :],
                                ALU.mult)
                v.tensor_tensor(S[:, 2, :], S[:, 2, :], S[:, 0, :],
                                ALU.subtract).then_inc(ve, 1)
            on("vector", p6_sc)
            led.inc("ve")
            emit_rsqrt(S[:, 2, :], Srho3[:, rc, :], EPS, S[:, 4, :])

            for fc in range(NF):
                def p6_am(v, fc=fc, rc=rc, last=(fc == NF - 1)):
                    v.tensor_tensor(S[:, 0, :],
                                    C[:, fc, rc * RB:(rc + 1) * RB],
                                    Smu3[:, rc, :], ALU.subtract)
                    v.tensor_tensor(S[:, 0, :], S[:, 0, :], Srho3[:, rc, :],
                                    ALU.mult)
                    col = 64 + rc * NF + fc
                    ins = v.reduce_max(amcols[:, col:col + 1], S[:, 0, :],
                                       AX.X, apply_absolute_value=True)
                    if last:
                        ins.then_inc(ve, 1)
                on("vector", p6_am)
            led.inc("ve")

        pending_tail = []
        xbig_free = {}
        for rc in range(2):
            oblk = 2 + rc


            pe_qend_prev = None
            ve_evq_prev = None
            nw_marks = {}
            for hg in range(4):
                pe_base = led.val("pe")
                ve_done_prev = ve_evq_prev if hg else led.val("ve")
                pe_wst_prev = pe_qend_prev if hg else pe_base

                for fc in range(NF):
                    def p3_ld(q, hg=hg, fc=fc, pe_base=pe_base,
                              pe_wst_prev=pe_wst_prev):
                        if fc >= 2:
                            q.wait_ge(pe, pe_base + fc - 1)
                        else:
                            q.wait_ge(pe, pe_wst_prev)
                        dma_in(q, "w%d" % (fc % 2), wst[:, fc % 2, :],
                               Wq_t[hg, fc, :, :])
                    on("sync", p3_ld)
                    t_ld = led.val("ld_w%d" % (fc % 2))

                    def p3_mm(tn, fc=fc, rc=rc, t_ld=t_ld,
                              ve_done_prev=ve_done_prev):
                        tn.wait_ge(LD["w%d" % (fc % 2)], t_ld)
                        if fc == 0:
                            tn.wait_ge(ve, ve_done_prev)
                        xn2src = (B[:, fc, :] if rc == 0
                                  else C[:, fc, RB:2 * RB])
                        for hh in range(4):
                            ins = tn.matmul(
                                ps[hh][:, :],
                                wst[:, fc % 2, hh * 128:(hh + 1) * 128],
                                xn2src, start=(fc == 0),
                                stop=(fc == NF - 1))
                        ins.then_inc(pe, 1)
                    on("tensor", p3_mm)
                    led.inc("pe")
                pe_qend_prev = led.val("pe")

                pe_q = led.val("pe")

                def p3_evq(v, hg=hg, pe_q=pe_q):
                    v.wait_ge(pe, pe_q)
                    for hh in range(4):
                        ins = v.tensor_copy(A[:, hg * 4 + hh, :], ps[hh][:, :])
                    ins.then_inc(ve, 1)
                on("vector", p3_evq)
                ve_evq_prev = led.inc("ve")

                if rc == 0:
                    lad = [(xbig[:, 0:2, :], xbig[:, 2:4, :], 0),
                           (Smu_o[:, :, :], Sr12_o[:, :, :], 2)]
                    banks = (4, 5, 6, 7)
                else:
                    lad = [(B[:, 0:2, :], B[:, 2:4, :], 0),
                           (Smu_o[:, :, :], Sr12_o[:, :, :], 2)]
                    banks = (6, 7, 6, 7)
                ve_evq = led.val("ve")
                pe_now3 = led.val("pe")
                ve_lfree = led.val("ve")
                mm_marks = []
                v0m = []
                for li, (vio, rout, hb) in enumerate(lad):
                    for j in range(2):
                        hh = li * 2 + j
                        h = hg * 4 + hh

                        def p3_sq(s, h=h, hh=hh, ve_evq=ve_evq,
                                  pe_now3=pe_now3, mm_marks=tuple(mm_marks)):
                            if hh == 0:
                                s.wait_ge(ve, ve_evq)
                                s.wait_ge(pe, pe_now3)
                            if hh >= 2:
                                s.wait_ge(pe, mm_marks[hh - 2])
                            s.activation(SQ[:, hh % 2, :], A[:, h, :],
                                         AF.Square).then_inc(ac, 1)
                        on("scalar", p3_sq)
                        ac_m = led.inc("ac")

                        def p3_lmm(tn, ac_m=ac_m, hh=hh, bank=banks[hh],
                                   ve_lfree=ve_lfree,
                                   ev1=(v0m[0] if v0m else None)):
                            tn.wait_ge(ac, ac_m)
                            if hh == 0:
                                tn.wait_ge(ve, ve_lfree)
                            if hh == 2 and ev1 is not None and bank == banks[0]:
                                tn.wait_ge(ve, ev1)
                            tn.matmul(ps[bank][:, :], sOne[:, :],
                                      SQ[:, hh % 2, :],
                                      start=True, stop=True).then_inc(pe, 1)
                        on("tensor", p3_lmm)
                        mm_marks.append(led.inc("pe"))

                    def p3_ev(v, vio=vio, b0=banks[li * 2], b1=banks[li * 2 + 1],
                              pe_l=mm_marks[-1]):
                        v.wait_ge(pe, pe_l)
                        v.tensor_copy(vio[:, 0, :], ps[b0][:, :])
                        v.tensor_copy(vio[:, 1, :], ps[b1][:, :])
                        v.tensor_single_scalar(vio, vio,
                                               float(L2EPS / 256.0),
                                               ALU.add).then_inc(ve, 1)
                    on("vector", p3_ev)
                    v0m.append(led.inc("ve"))

                a0m = []
                for li, (vio, rout, hb) in enumerate(lad):
                    def p3_ln(s, vio=vio, rout=rout, v0=v0m[li]):
                        b0 = led.val("ac")
                        s.wait_ge(ve, v0)
                        s.activation(rout, vio, AF.Ln).then_inc(ac, 1)
                        s.wait_ge(ac, b0 + 1)
                        s.activation(rout, rout, AF.Exp,
                                     scale=-0.5).then_inc(ac, 1)
                    on("scalar", p3_ln)
                    led.inc("ac")
                    a0m.append(led.inc("ac"))

                for li, (vio, rout, hb) in enumerate(lad):
                    def p3_nw(v, vio=vio, rout=rout, hb=hb, hg=hg,
                              a0=a0m[li]):
                        v.wait_ge(ac, a0)
                        v.tensor_tensor(vio, vio, rout, ALU.mult)
                        v.tensor_tensor(vio, vio, rout, ALU.mult)
                        v.tensor_scalar(vio, vio, -0.5, 1.5,
                                        ALU.mult, ALU.add)
                        v.tensor_tensor(rout, rout, vio, ALU.mult)
                        v.tensor_single_scalar(rout, rout, 4.0, ALU.mult)
                        for hh in range(2):
                            h = hg * 4 + hb + hh
                            ins = v.tensor_tensor(A[:, h, :], A[:, h, :],
                                                  rout[:, hh, :], ALU.mult)
                        ins.then_inc(ve, 1)
                    on("vector", p3_nw)
                    nw_marks[(hg, li)] = led.inc("ve")

            ve_evq_last = ve_evq_prev
            if rc == 0:
                xbig_free[0] = led.val("ve")
            elif pending_tail:
                pending_tail.pop(0)()

            if tap_dram:
                emit_barrier()
            if "qn" in tap_dram:
                def tqn(q, rc=rc):
                    dma_out(q, tap_dram["qn"][:, :, rc * RB:(rc + 1) * RB],
                            A[:, :, :])
                on("sync", tqn)

            for h in range(NH):
                pe_base = led.val("pe")
                ac_base = led.val("ac")
                ve_att_prev = led.val("ve")

                plan = ["sim0", "sim1"]
                for jc in range(NJ):
                    plan.append("sa%d" % jc)
                    if jc + 2 < NJ:
                        plan.append("sim%d" % (jc + 2))
                sim_mark, av_mark = {}, {}
                m = pe_base
                for stp in plan:
                    m += 1
                    if stp.startswith("sim"):
                        sim_mark[int(stp[3:])] = m
                    else:
                        av_mark[int(stp[2:])] = m

                qn_mark = nw_marks[(h // 4, (h % 4) // 2)]

                def p4_pe(tn, h=h, rc=rc, pe_base=pe_base, ac_base=ac_base,
                          ve_att_prev=ve_att_prev, plan=plan,
                          qn_mark=qn_mark, ve_evq_last=ve_evq_last):
                    if h == 0 and rc == 0:
                        tn.wait_ge(LD["agf"], t_agf)
                    if h == 0:
                        tn.wait_ge(ve, ve_evq_last)
                    tn.wait_ge(ve, qn_mark)
                    for stp in plan:
                        if stp.startswith("sim"):
                            jc = int(stp[3:])
                            if h > 0:
                                if jc < 2:
                                    tn.wait_ge(ac, ac_base)
                                else:
                                    tn.wait_ge(ac, ac_base + jc - 1)
                            tn.matmul(ps[jc % 2][:, :],
                                      kT[:, jc * 128:(jc + 1) * 128],
                                      A[:, h, :], start=True,
                                      stop=True).then_inc(pe, 1)
                        else:
                            jc = int(stp[2:])
                            if jc == 0:
                                tn.wait_ge(ve, ve_evq_last if h == 0
                                           else ve_att_prev)
                            tn.wait_ge(ac, ac_base + jc + 1)
                            tn.matmul(ps[3][:, :], sOne[:, :],
                                      ptil[:, jc % 2, :],
                                      start=(jc == 0), stop=(jc == NJ - 1))
                            tn.matmul(ps[2][:, :], vv[:, jc, :],
                                      ptil[:, jc % 2, :],
                                      start=(jc == 0),
                                      stop=(jc == NJ - 1)).then_inc(pe, 1)
                on("tensor", p4_pe)
                for _ in range(2 * NJ):
                    led.inc("pe")
                assert led.val("pe") == m

                def p4_exp(s, sim_mark=sim_mark, av_mark=av_mark):
                    for jc in range(NJ):
                        s.wait_ge(pe, sim_mark[jc])
                        if jc >= 2:
                            s.wait_ge(pe, av_mark[jc - 2])
                        s.activation(ptil[:, jc % 2, :], ps[jc % 2][:, :],
                                     AF.Exp).then_inc(ac, 1)
                on("scalar", p4_exp)
                exp_marks = [led.inc("ac") for _ in range(NJ)]
                assert exp_marks[-1] == ac_base + NJ

                pe_last = av_mark[NJ - 1]

                def p4_cp(v, pe_last=pe_last):
                    v.wait_ge(pe, pe_last)
                    v.tensor_copy(S[:, 2, :], ps[3][:, :]).then_inc(ve, 1)
                on("vector", p4_cp)
                led.inc("ve")
                emit_recip(S[:, 2, :], S[:, 1, :], S[:, 4, :], S[:, 0, :])

                def p4_out(v, h=h):
                    v.tensor_tensor(B[:, h, :], ps[2][:, :], S[:, 1, :],
                                    ALU.mult).then_inc(ve, 1)
                on("vector", p4_out)
                led.inc("ve")

            emit_barrier()
            if "att" in tap_dram:
                def tat(q, rc=rc):
                    dma_out(q, tap_dram["att"][:, :, rc * RB:(rc + 1) * RB],
                            B[:, :, :])
                on("sync", tat)

            ve_p5start = led.val("ve")
            sqg = {}
            for fg in range(4):
                pe_base = led.val("pe")
                ve_prev = led.val("ve")
                for hdc in range(NF):
                    def p5_ld(q, fg=fg, hdc=hdc, pe_base=pe_base,
                              ve_prev=ve_prev):
                        if hdc >= 2:
                            q.wait_ge(pe, pe_base + hdc - 1)
                        if hdc == 0:
                            q.wait_ge(ve, ve_prev)
                        dma_in(q, "w%d" % (hdc % 2), wst[:, hdc % 2, :],
                               Wo_t[fg, hdc, :, :])
                    on("sync", p5_ld)
                    t_ld = led.val("ld_w%d" % (hdc % 2))

                    def p5_mm(tn, hdc=hdc, t_ld=t_ld):
                        tn.wait_ge(LD["w%d" % (hdc % 2)], t_ld)
                        for ff in range(4):
                            ins = tn.matmul(
                                ps[ff][:, :],
                                wst[:, hdc % 2, ff * 128:(ff + 1) * 128],
                                B[:, hdc, :], start=(hdc == 0),
                                stop=(hdc == NF - 1))
                        ins.then_inc(pe, 1)
                    on("tensor", p5_mm)
                    led.inc("pe")

                pe_fg = led.val("pe")

                def p5_ev(v, fg=fg, rc=rc, pe_fg=pe_fg):
                    v.wait_ge(pe, pe_fg)
                    for ff in range(4):
                        ins = v.tensor_copy(
                            C[:, fg * 4 + ff, rc * RB:(rc + 1) * RB],
                            ps[ff][:, :])
                    ins.then_inc(ve, 1)
                on("vector", p5_ev)
                ev_m = led.inc("ve")

                for ff in range(4):
                    fc = fg * 4 + ff

                    def p5_sq(s, fc=fc, rc=rc, ev_m=ev_m, g=sqg.get(fc - 2)):
                        s.wait_ge(ve, ev_m)
                        if g:
                            s.wait_ge(pe, g)
                        s.activation(SQ[:, fc % 2, :],
                                     C[:, fc, rc * RB:(rc + 1) * RB],
                                     AF.Square).then_inc(ac, 1)
                    on("scalar", p5_sq)
                    ac_m = led.inc("ac")

                    def p5_smm(tn, fc=fc, rc=rc, ac_m=ac_m, ev_m=ev_m,
                               ve_p5start=ve_p5start):
                        if fc == 0:
                            tn.wait_ge(ve, ve_p5start)
                        tn.wait_ge(ve, ev_m)
                        tn.matmul(ps[4][:, :], sOne[:, :],
                                  C[:, fc, rc * RB:(rc + 1) * RB],
                                  start=(fc == 0), stop=(fc == NF - 1))
                        tn.wait_ge(ac, ac_m)
                        tn.matmul(ps[5][:, :], sOne[:, :], SQ[:, fc % 2, :],
                                  start=(fc == 0),
                                  stop=(fc == NF - 1)).then_inc(pe, 1)
                    on("tensor", p5_smm)
                    sqg[fc] = led.inc("pe")

            if rc == 0:
                pe_sums0 = led.val("pe")
                pending_tail.append(
                    lambda oblk=oblk, pe_sums0=pe_sums0:
                        emit_tail(0, oblk, xbig_free[0], pe_sums0))
            else:
                emit_tail(1, oblk)

            emit_barrier()

        if "Cmid" in tap_dram:
            on("sync",
               lambda q: dma_out(q, tap_dram["Cmid"][:, :, :], C[:, :, :]))

        att_stack.close()
        xq = e(nc.sbuf_tensor("xq", [P, 2, NF, RB], FP8))

        ve_hop2 = led.val("ve")

        def p6_comb(v, ve_hop2=ve_hop2):
            v.wait_ge(ve, ve_hop2)
            v.reduce_max(amc[:, 1:2], amcols[:, 64:96], AX.X).then_inc(ve, 1)
        on("vector", p6_comb)
        led.inc("ve")
        ve_am2 = led.val("ve")

        def p6_tr(tn, ve_am2=ve_am2):
            tn.wait_ge(ve, ve_am2)
            tn.transpose(ps[7][0:1, 0:P], amc[:, 1:2], sI[:, :]).then_inc(pe, 1)
        on("tensor", p6_tr)
        pe_tr2 = led.inc("pe")

        def p6_loc(v, pe_tr2=pe_tr2):
            v.wait_ge(pe, pe_tr2)
            v.reduce_max(tiny[0:1, 1:2], ps[7][0:1, 0:P], AX.X).then_inc(ve, 1)
        on("vector", p6_loc)
        ve_loc = led.inc("ve")

        def p6_ccout(q, ve_loc=ve_loc):
            q.wait_ge(ve, ve_loc)
            dma_in(q, "cc2", ccin[:, :], tiny[0:1, 1:2])
        on("sync", p6_ccout)
        t_ldcc = led.val("ld_cc2")

        def p6_cc(g, t_ldcc=t_ldcc):
            g.wait_ge(LD["cc2"], t_ldcc)
            g.collective_compute(
                "AllReduce", ALU.max, replica_groups=[list(range(8))],
                ins=[ccin.ap().opt()], outs=[ccout.ap().opt()]).then_inc(cc, 1)
        on("gpsimd", p6_cc)
        led.inc("cc")

        def p6_ccin(g):
            g.wait_ge(cc, 2)
            dma_in(g, "cc3", tiny[0:1, 2:3], ccout[:, :])
        on("gpsimd", p6_ccin)
        t_ldg = led.val("ld_cc3")

        def p6_scale(v, t_ldg=t_ldg):
            b0 = led.val("ve")
            v.wait_ge(LD["cc3"], t_ldg)
            v.tensor_single_scalar(tiny[0:1, 3:4], tiny[0:1, 2:3], INV448,
                                   ALU.mult).then_inc(ve, 1)
            v.wait_ge(ve, b0 + 1)
            v.tensor_single_scalar(tiny[0:1, 3:4], tiny[0:1, 3:4], 1e-12,
                                   ALU.max).then_inc(ve, 1)
            v.wait_ge(ve, b0 + 2)
            v.reciprocal(S[0:1, 0, 0:1], tiny[0:1, 3:4]).then_inc(ve, 1)
            v.wait_ge(ve, b0 + 3)
            v.tensor_single_scalar(S[0:1, 0, 1:2], S[0:1, 0, 0:1], 0.5,
                                   ALU.mult).then_inc(ve, 1)
            v.wait_ge(ve, b0 + 4)
            v.tensor_tensor(S[0:1, 0, 2:3], tiny[0:1, 3:4], tiny[0:1, 0:1],
                            ALU.mult).then_inc(ve, 1)
            v.wait_ge(ve, b0 + 5)
            v.tensor_single_scalar(S[0:1, 0, 2:3], S[0:1, 0, 2:3], 4.0,
                                   ALU.mult).then_inc(ve, 1)
        on("vector", p6_scale)
        for _ in range(5):
            led.inc("ve")
        ve_row = led.inc("ve")

        def p6_bc(tn, ve_row=ve_row):
            tn.wait_ge(ve, ve_row)
            tn.matmul(ps[7][0:P, 0:3], sOne[0:1, 0:P], S[0:1, 0, 0:3],
                      start=True, stop=True).then_inc(pe, 1)
        on("tensor", p6_bc)
        pe_bc2 = led.inc("pe")

        def p6_bc_ev(v, pe_bc2=pe_bc2):
            v.wait_ge(pe, pe_bc2)
            v.tensor_copy(SCAL[:, 2:5], ps[7][0:P, 0:3]).then_inc(ve, 1)
        on("vector", p6_bc_ev)
        ve_scal_x = led.inc("ve")

        if "tiny" in tap_dram:
            on("sync", lambda q: dma_out(q, tap_dram["tiny"][:, :], tiny[:, :]))
        if "scal" in tap_dram:
            on("sync", lambda q: dma_out(q, tap_dram["scal"][:, :], SCAL[:, :]))

        xq_marks = {}
        for rc in range(2):
            for fc in range(NF):
                def p7_q(v, fc=fc, rc=rc):
                    if rc == 0 and fc == 0:
                        v.wait_ge(ve, ve_scal_x)
                    v.tensor_tensor(S[:, 0, :], C[:, fc, rc * RB:(rc + 1) * RB],
                                    Smu3[:, rc, :], ALU.subtract)
                    v.tensor_tensor(S[:, 0, :], S[:, 0, :], Srho3[:, rc, :],
                                    ALU.mult)
                    emit_quant(v, S[:, 0, :], xq[:, rc, fc, :],
                               SCAL[:, 3:4]).then_inc(ve, 1)
                on("vector", p7_q)
                xq_marks[(rc, fc)] = led.inc("ve")

            if "xq" in tap_dram:
                def txq(q, rc=rc, ve_xq=xq_marks[(rc, NF - 1)]):
                    q.wait_ge(ve, ve_xq)
                    dma_out(q, tap_dram["xq"][:, :, rc * RB:(rc + 1) * RB],
                            xq[:, rc, :, :])
                on("sync", txq)

        w8_gate = {0: 0, 1: 0, 2: 0}
        fin_gate = {0: 0, 1: 0}
        for rc in range(2):

            for og in range(4):
                gidx = rc * 4 + og
                bb = 4 * (gidx % 2)
                pe_base = led.val("pe")
                for pr in range(NF // 2):
                    ve_xq = xq_marks[(rc, pr * 2 + 1)]
                    slot = (gidx * 8 + pr) % 3
                    g = w8_gate[slot]

                    def p7_ld(q, og=og, pr=pr, slot=slot, g=g):
                        if g:
                            q.wait_ge(pe, g)
                        dma_in(q, "w8%d" % slot, w8st[:, slot, :],
                               w8d[og, pr, :, :])
                    on("sync", p7_ld)
                    t_ld = led.val("ld_w8%d" % slot)

                    ve_bank = fin_gate[gidx % 2]

                    def p7_mm(tn, pr=pr, rc=rc, slot=slot, bb=bb, t_ld=t_ld,
                              ve_bank=ve_bank, ve_xq=ve_xq):
                        tn.wait_ge(LD["w8%d" % slot], t_ld)
                        tn.wait_ge(ve, ve_xq)
                        if pr == 0 and ve_bank:
                            tn.wait_ge(ve, ve_bank)
                        for i in range(2):
                            fc = pr * 2 + i
                            for oi in range(4):
                                ins = tn.matmul(
                                    ps[bb + oi][:, :],
                                    w8st[:, slot,
                                         i * 512 + oi * 128:
                                         i * 512 + (oi + 1) * 128],
                                    xq[:, rc, fc, :], start=(fc == 0),
                                    stop=(fc == NF - 1))
                            ins.then_inc(pe, 1)
                    on("tensor", p7_mm)
                    led.inc("pe")
                    w8_gate[slot] = led.inc("pe")

                pe_og = led.val("pe")

                def p7_fin(v, og=og, rc=rc, bb=bb, pe_og=pe_og):
                    v.wait_ge(pe, pe_og)
                    for oi in range(4):
                        o = og * 4 + oi
                        ctile = C[:, o, rc * RB:(rc + 1) * RB]
                        v.tensor_scalar(qb1[:, :], ps[bb + oi][:, :],
                                        SCAL[:, 4:5], None, ALU.mult)
                        v.tensor_tensor(ctile, ctile, qb1[:, :],
                                        ALU.add).then_inc(ve, 1)
                on("vector", p7_fin)
                fin_marks = [led.inc("ve") for _ in range(4)]
                ve_fin = fin_marks[-1]
                fin_gate[gidx % 2] = ve_fin

                def p7_st(q, og=og, rc=rc, fin_marks=tuple(fin_marks)):
                    for oi in range(4):
                        o = og * 4 + oi
                        q.wait_ge(ve, fin_marks[oi])
                        dma_out(q, outT[o, :, rc * RB:(rc + 1) * RB],
                                C[:, o, rc * RB:(rc + 1) * RB])
                on("scalar", p7_st)

        if "C" in tap_dram:
            on("sync", lambda q: dma_out(q, tap_dram["C"][:, :, :], C[:, :, :]))

        so_total = led.val("so")

        def fin(q, so_total=so_total):
            q.wait_ge(so, so_total)
        on("sync", fin)

    return nc


def _layout_inputs(L0_attn_x, L0_attn_g_norm, L0_attn_Wq, L0_attn_Wkv,
                   L0_attn_Wout, L0_attn_g_out, L0_ffn_weight):
    x = np.ascontiguousarray(np.asarray(L0_attn_x, dtype=np.float32))
    Wq = np.ascontiguousarray(np.asarray(L0_attn_Wq, dtype=np.float32))
    Wkv = np.ascontiguousarray(np.asarray(L0_attn_Wkv, dtype=np.float32))
    Wout = np.ascontiguousarray(np.asarray(L0_attn_Wout, dtype=np.float32))
    Wt = np.asarray(L0_ffn_weight)
    gN = np.ascontiguousarray(np.asarray(L0_attn_g_norm, dtype=np.float32))
    gO = np.ascontiguousarray(np.asarray(L0_attn_g_out, dtype=np.float32))

    Wq_t = np.ascontiguousarray(Wq.reshape(NF, P, 4, 512).transpose(2, 0, 1, 3))
    Wkv_t = np.ascontiguousarray(Wkv.reshape(NF, P, 256))
    Wo_t = np.ascontiguousarray(Wout.reshape(NF, P, 4, 512).transpose(2, 0, 1, 3))
    WtT = np.ascontiguousarray(Wt.T).reshape(NF, P, 2048)
    gN2 = np.ascontiguousarray(gN.reshape(NF, P).T)
    gO2 = np.ascontiguousarray(gO.reshape(NF, P).T)
    iden = np.eye(P, dtype=np.float32)

    in_maps = []
    for c in range(8):
        b, half = c // 2, c % 2
        order = [2 * (1 - half), 2 * (1 - half) + 1, 2 * half, 2 * half + 1]
        xb = np.empty((NB, P, NF, RB), dtype=np.float32)
        for i, bi in enumerate(order):
            blkd = x[b, bi * RB:(bi + 1) * RB, :]
            xb[i] = blkd.T.reshape(NF, P, RB).transpose(1, 0, 2)
        in_maps.append(dict(xblk=xb, Wq_t=Wq_t, Wkv_t=Wkv_t, Wo_t=Wo_t,
                            WtT=WtT, gN2=gN2, gO2=gO2, iden=iden))
    return in_maps


def kernel(L0_attn_x, L0_attn_g_norm, L0_attn_Wq, L0_attn_Wkv, L0_attn_Wout,
           L0_attn_g_out, L0_ffn_weight):
    in_maps = _layout_inputs(L0_attn_x, L0_attn_g_norm, L0_attn_Wq,
                             L0_attn_Wkv, L0_attn_Wout, L0_attn_g_out,
                             L0_ffn_weight)
    nc = build_program(taps=tuple(_TAPS))
    res = run_bass_kernel_spmd(nc, in_maps, core_ids=list(range(8)))
    out = np.empty((4, 2048, 2048), dtype=np.float32)
    for c in range(8):
        b, half = c // 2, c % 2
        o = np.asarray(res.results[c]["outT"])
        out[b, half * 1024:(half + 1) * 1024, :] = (
            o.transpose(2, 0, 1).reshape(1024, 2048))
    if _TAPS:
        kernel._last_results = res
    return out
